# version 03
# baseline (speedup 1.0000x reference)
import numpy as np
import ml_dtypes
import concourse.bass as bass
import concourse.mybir as mybir
from concourse.bass_utils import run_bass_kernel_spmd
from contextlib import ExitStack

F32 = mybir.dt.float32
BF16 = mybir.dt.bfloat16
AF = mybir.ActivationFunctionType
ALU = mybir.AluOpType
AX = mybir.AxisListType.X

B, L, D = 2, 8192, 1024
NPAIR = 8
TOK = 2048
NT = L // 128
OWN_T = TOK // 128
NG = 8
GTOK = 1024
NKT = L // 128
NSTRIP = TOK // 512
NHG = 4
NE, DFF = 8, 256

ENGS = ("tensor", "vector", "scalar", "gpsimd", "sync")


class Prog:
    def __init__(self):
        self.steps = {e: [] for e in ENGS}
        self.semnames = []
        self.counts = {}

    def sem(self, name):
        self.semnames.append(name)
        self.counts[name] = 0
        return name

    def n(self, s):
        return self.counts[s]

    def step(self, eng, fn, waits=(), incs=()):
        waits = [(s, v) for (s, v) in waits if v > 0]
        for s, k in incs:
            self.counts[s] += k
        self.steps[eng].append((fn, waits, list(incs)))

    def check_deadlock(self):
        idx = {e: 0 for e in ENGS}
        cnt = {s: 0 for s in self.semnames}
        progress = True
        while progress:
            progress = False
            for e in ENGS:
                while idx[e] < len(self.steps[e]):
                    fn, waits, incs = self.steps[e][idx[e]]
                    if all(cnt[s] >= v for s, v in waits):
                        for s, k in incs:
                            cnt[s] += k
                        idx[e] += 1
                        progress = True
                    else:
                        break
        stuck = {e: (idx[e], len(self.steps[e])) for e in ENGS if idx[e] < len(self.steps[e])}
        if stuck:
            detail = {}
            for e in stuck:
                fn, waits, incs = self.steps[e][idx[e]]
                detail[e] = [(s, v, cnt[s]) for s, v in waits if cnt[s] < v]
            raise RuntimeError(f"schedule deadlock: {stuck}; blocked: {detail}")

    def emit(self, eng_name, eng, semh):
        for fn, waits, incs in self.steps[eng_name]:
            for s, v in waits:
                eng.wait_ge(semh[s], v)
            ins = fn(eng)
            for s, k in incs:
                ins = ins.then_inc(semh[s], k)


def build_nc(debug_taps=()):
    nc = bass.Bass("TRN2", target_bir_lowering=False, debug=False)
    p = Prog()

    xbm = nc.dram_tensor("xb", [L, D], F32, kind="ExternalInput").ap()
    wkvT = nc.dram_tensor("wkvT", [D, 2048], F32, kind="ExternalInput").ap()
    wqT = nc.dram_tensor("wqT", [D, D], F32, kind="ExternalInput").ap()
    gateT = nc.dram_tensor("gateT", [D, 8], F32, kind="ExternalInput").ap()
    wg_in = nc.dram_tensor("wg", [NE * D, DFF], F32, kind="ExternalInput").ap()
    wu_in = nc.dram_tensor("wu", [NE * D, DFF], F32, kind="ExternalInput").ap()
    wd_in = nc.dram_tensor("wd", [NE * DFF, D], F32, kind="ExternalInput").ap()
    woutT = nc.dram_tensor("woutT", [D, D], F32, kind="ExternalInput").ap()
    costab = nc.dram_tensor("costab", [L, 64], F32, kind="ExternalInput").ap()
    sintab = nc.dram_tensor("sintab", [L, 64], F32, kind="ExternalInput").ap()
    sel8_in = nc.dram_tensor("sel8_in", [8, 1024], F32, kind="ExternalInput").ap()
    ident_in = nc.dram_tensor("ident_in", [128, 128], F32, kind="ExternalInput").ap()
    onesr_in = nc.dram_tensor("onesr", [1, 128], F32, kind="ExternalInput").ap()

    out = nc.dram_tensor("out", [TOK, D], F32, kind="ExternalOutput").ap()

    xn_dram = nc.dram_tensor("xn_dram", [L, D], BF16, kind="Internal").ap()
    xnlo_dram = nc.dram_tensor("xnlo_dram", [TOK, D], BF16, kind="Internal").ap()
    k_dram = nc.dram_tensor("k_dram", [L, D], BF16, kind="Internal").ap()
    q_dram = nc.dram_tensor("q_dram", [TOK, D], BF16, kind="Internal").ap()
    v_dram = nc.dram_tensor("v_dram", [L, D], BF16, kind="Internal").ap()
    o_dram = nc.dram_tensor("o_dram", [D, TOK], BF16, kind="Internal").ap()
    s_dram = nc.dram_tensor("s_dram", [TOK, D], BF16, kind="Internal").ap()

    taps = {}
    for name, shape, dt in [("q", [TOK, D], BF16), ("k", [L, D], BF16), ("v", [L, D], BF16),
                            ("o", [D, TOK], BF16), ("l", [1, NHG * NSTRIP * 2048], F32),
                            ("mask", [8, 2048], BF16), ("logit", [8, 2048], F32),
                            ("xn", [L, D], BF16), ("bias", [128, 1], F32),
                            ("s", [TOK, D], BF16), ("g", [128, 16384], BF16)]:
        if name in debug_taps:
            taps[name] = nc.dram_tensor("tap_" + name, shape, dt, kind="ExternalOutput").ap()

    es = ExitStack()
    with es:
        PS = es.enter_context(nc.psum_tensor([128, 4096], F32))

        def bank(i, w=1):
            return PS[:, i * 512:(i + w) * 512]

        A1 = es.enter_context(nc.sbuf_tensor("A1", [128, 16384], BF16, is_transpose_dest=True))
        A2 = es.enter_context(nc.sbuf_tensor("A2", [128, 16384], BF16, is_transpose_dest=True))
        A3 = es.enter_context(nc.sbuf_tensor("A3", [128, 16896], BF16, is_transpose_dest=True))
        A5 = es.enter_context(nc.sbuf_tensor("A5", [128, 20480], BF16, is_transpose_dest=True))
        A6 = es.enter_context(nc.sbuf_tensor("A6", [128, 16384], BF16))
        XNB = es.enter_context(nc.sbuf_tensor("XNB", [128, 4096], BF16))
        B1 = es.enter_context(nc.sbuf_tensor("B1", [128, 6144], F32))

        bn6 = es.enter_context(nc.sbuf_tensor("bn6", [128, 12], F32))
        mv = es.enter_context(nc.sbuf_tensor("mv", [128, 8], F32))
        rstd = es.enter_context(nc.sbuf_tensor("rstd", [128, 4], F32))
        nmur = es.enter_context(nc.sbuf_tensor("nmur", [128, 4], F32))
        stat = es.enter_context(nc.sbuf_tensor("stat", [128, 64], F32))
        bias_col = es.enter_context(nc.sbuf_tensor("bias_col", [128, 1], F32))
        eps_col = es.enter_context(nc.sbuf_tensor("eps_col", [128, 1], F32))
        ident = es.enter_context(nc.sbuf_tensor("ident", [128, 128], F32))
        onesr32 = es.enter_context(nc.sbuf_tensor("onesr32", [1, 128], F32))
        onesr16 = es.enter_context(nc.sbuf_tensor("onesr16", [1, 128], BF16))
        sel8 = es.enter_context(nc.sbuf_tensor("sel8", [8, 1024], BF16))
        gate32 = es.enter_context(nc.sbuf_tensor("gate32", [128, 64], F32))
        gatehi = es.enter_context(nc.sbuf_tensor("gatehi", [128, 64], BF16))
        gatelo = es.enter_context(nc.sbuf_tensor("gatelo", [128, 64], BF16))
        ltm = es.enter_context(nc.sbuf_tensor("ltm", [128, 128], F32))
        rmaxtm = es.enter_context(nc.sbuf_tensor("rmaxtm", [128, 16], F32))
        xnlo_b = es.enter_context(nc.sbuf_tensor("xnlo_b", [128, 1024], BF16))

        def xnf(slot, dc):
            return A1[:, slot * 8192 + dc * 1024:slot * 8192 + (dc + 1) * 1024]
        wkv = A2
        wq_sb = A3[:, 8192:16384]
        cos_sb = A3[:, 0:4096]
        sin_sb = A3[:, 4096:8192]
        k_ev = A6[:, 0:2048]
        v_ev = A6[:, 2048:4096]
        q_ev = A6[:, 4096:6144]
        ksq = A6[:, 6144:7168]
        kt_sb = A2
        vt_sb = A3
        q_sb = A5[:, 0:16384]
        p_sb = A5[:, 16384:20480]
        xin = B1[:, 0:4096]
        xn32 = B1[:, 5120:6144]
        kr32q = B1[:, 4096:5120]
        orawB = B1[:, 4096:6144]
        lrow = B1[0:1, 0:2048]
        lsb = B1[0:8, 0:2048]
        rmaxrow = B1[0:1, 2048:4096]
        lrow16 = A1[0:1, 0:2048]
        lb4 = A1[0:64, 2048:4096]
        o_tmp = A1[0:64, 4096:6144]
        wout = A3[:, 0:8192]
        o_sb = A6
        inter = A1
        xnof = A3[:, 0:16384]
        gsil = A6[:, 0:16384]
        wgu = A5[:, 0:8192]
        ebm = A5[:, 8192:12288]
        gr_sb = A5[:, 12288:14336]
        gr2_sb = A5[:, 14336:16384]
        um_sb = A5[:, 16384:18432]
        wd_sb = A2
        maskfm = XNB[0:8, 0:2048]
        sload = A5[:, 0:2048]
        x6 = B1[:, 0:2048]
        o6 = B1[:, 2048:4096]

        s1x = p.sem("s1x"); s1v1 = p.sem("s1v1"); s1a = p.sem("s1a")
        s1v2 = p.sem("s1v2"); s1st = p.sem("s1st")
        s2w = p.sem("s2w"); s2t = p.sem("s2t"); s2mm = p.sem("s2mm")
        s2r = p.sem("s2r"); s2st = p.sem("s2st"); s2stat = p.sem("s2stat")
        s2a = p.sem("s2a"); s5v = p.sem("s5v"); s5m = p.sem("s5m")
        s3k = p.sem("s3k"); s3v = p.sem("s3v"); s3sc = p.sem("s3sc")
        s3ex = p.sem("s3ex"); s3pv = p.sem("s3pv"); s3ev = p.sem("s3ev")
        s3mx = p.sem("s3mx"); s3st = p.sem("s3st")
        s4 = p.sem("s4"); s4e = p.sem("s4e"); s4st = p.sem("s4st")
        s5 = p.sem("s5"); s5w = p.sem("s5w"); s5a = p.sem("s5a")
        s6x = p.sem("s6x"); s6s = p.sem("s6s"); s6v = p.sem("s6v"); s6st = p.sem("s6st")

        def dma(dst, src):
            return lambda e: e.dma_start(dst, src)

        for d_, s_ in [(ident[:], ident_in), (onesr32[:], onesr_in), (onesr16[:], onesr_in),
                       (sel8[:], sel8_in)]:
            p.step("gpsimd", dma(d_, s_), incs=[(s2w, 16)])
        p.step("gpsimd", dma(cos_sb[:], costab.rearrange("(t p) d -> p t d", p=128)), incs=[(s2w, 16)])
        p.step("gpsimd", dma(sin_sb[:], sintab.rearrange("(t p) d -> p t d", p=128)), incs=[(s2w, 16)])
        for dc in range(8):
            p.step("gpsimd", dma(wkv[:, dc * 2048:(dc + 1) * 2048],
                                 wkvT[dc * 128:(dc + 1) * 128, :]), incs=[(s2w, 16)])
        for dc in range(8):
            p.step("gpsimd", dma(wq_sb[:, dc * 1024:(dc + 1) * 1024],
                                 wqT[dc * 128:(dc + 1) * 128, :]), incs=[(s2w, 16)])
        for dc in range(8):
            p.step("gpsimd", dma(gate32[:, dc * 8:(dc + 1) * 8],
                                 gateT[dc * 128:(dc + 1) * 128, :]), incs=[(s2w, 16)])
        W_LOADS = p.n(s2w)
        p.step("vector", lambda e: e.tensor_copy(gatehi[:], gate32[:]),
               waits=[(s2w, W_LOADS)], incs=[(s2stat, 1)])
        p.step("vector", lambda e: e.tensor_tensor(gatelo[:], gate32[:], gatehi[:], ALU.subtract),
               waits=[(s2stat, 1)], incs=[(s2stat, 1)])
        p.step("vector", lambda e: e.memset(stat[:], 0.0), incs=[(s2stat, 1)])
        p.step("vector", lambda e: e.memset(eps_col[:], 1e-5), incs=[(s1a, 1)])

        store_done = {}
        xn_done = {}
        act_done = {}
        sqrt_done = {}
        lo_done = {}
        NSKEW = 2

        def p1_A(t):
            xs = (t % 4) * 1024
            w = [(s1a, act_done[t - 4])] if t >= 4 else []
            p.step("sync", (lambda xs_, t_: lambda e: e.dma_start(
                xin[:, xs_:xs_ + 1024], xbm[t_ * 128:(t_ + 1) * 128, :]))(xs, t),
                waits=w, incs=[(s1x, 16)])

        def p1_B(t):
            xs = (t % 4) * 1024
            ms = t % 4
            wb = [(s1x, 16 * (t + 1))]
            if t:
                wb.append((s1v1, 3 * t))
            if t >= 4:
                wb.append((s1a, act_done[t - 4]))
            p.step("vector", (lambda xs_: lambda e: e.bn_stats(bn6[:, 0:6], xin[:, xs_:xs_ + 512]))(xs),
                   waits=wb, incs=[(s1v1, 1)])
            p.step("vector", (lambda xs_: lambda e: e.bn_stats(bn6[:, 6:12], xin[:, xs_ + 512:xs_ + 1024]))(xs),
                   incs=[(s1v1, 1)])
            p.step("vector", (lambda ms_: lambda e: e.bn_aggr(mv[:, ms_ * 2:ms_ * 2 + 2], bn6[:]))(ms),
                   waits=[(s1v1, 3 * t + 2)], incs=[(s1v1, 1)])

        def p1_C(t):
            ms = t % 4
            w = [(s1v1, 3 * (t + 1)), (s1a, 1)]
            if t >= 4:
                w.append((s1v2, xn_done[t - 4]))
            p.step("scalar", (lambda ms_: lambda e: e.activation(
                rstd[:, ms_:ms_ + 1], mv[:, ms_ * 2 + 1:ms_ * 2 + 2], AF.Sqrt, bias=eps_col[:]))(ms),
                waits=w, incs=[(s1a, 1)])
            sqrt_done[t] = p.n(s1a)

        def p1_D(t):
            ms = t % 4
            p.step("vector", (lambda ms_: lambda e: e.reciprocal(rstd[:, ms_:ms_ + 1], rstd[:, ms_:ms_ + 1]))(ms),
                   waits=[(s1a, sqrt_done[t])], incs=[(s1v2, 1)])
            p.step("vector", (lambda ms_: lambda e: e.scalar_tensor_tensor(
                nmur[:, ms_:ms_ + 1], mv[:, ms_ * 2:ms_ * 2 + 1], -1.0, rstd[:, ms_:ms_ + 1],
                ALU.mult, ALU.mult))(ms),
                waits=[(s1v2, p.n(s1v2))], incs=[(s1v2, 1)])
            xn_done[t] = p.n(s1v2)

        def p1_E(t):
            own = t < OWN_T
            xs = (t % 4) * 1024
            ms = t % 4
            ns = (t % 4) * 1024
            wact = [(s1v2, xn_done[t])]
            if t >= 4:
                wact.append((s1st, store_done[t - 4]))
            p.step("scalar", (lambda xs_, ns_, ms_: lambda e: e.activation(
                XNB[:, ns_:ns_ + 1024], xin[:, xs_:xs_ + 1024], AF.Identity,
                bias=nmur[:, ms_:ms_ + 1], scale=rstd[:, ms_:ms_ + 1]))(xs, ns, ms),
                waits=wact, incs=[(s1a, 1)])
            if own:
                p.step("scalar", (lambda xs_, ms_: lambda e: e.activation(
                    xn32[:, 0:1024], xin[:, xs_:xs_ + 1024], AF.Identity,
                    bias=nmur[:, ms_:ms_ + 1], scale=rstd[:, ms_:ms_ + 1]))(xs, ms),
                    waits=([(s1v2, lo_done[t - 1])] if t >= 1 else []),
                    incs=[(s1a, 1)])
            act_done[t] = p.n(s1a)

        def p1_F(t):
            if t < OWN_T:
                ns = (t % 4) * 1024
                p.step("gpsimd", (lambda ns_: lambda e: e.tensor_tensor(
                    xnlo_b[:, 0:1024], xn32[:, 0:1024], XNB[:, ns_:ns_ + 1024], ALU.subtract))(ns),
                    waits=[(s1a, act_done[t]), (s1st, store_done.get(t - 1, 0))],
                    incs=[(s1v2, 1)])
            lo_done[t] = p.n(s1v2)

        def p1_G(t):
            ns = (t % 4) * 1024
            own = t < OWN_T
            wst = [(s1a, act_done[t])]
            if own:
                wst.append((s1v2, lo_done[t]))
            p.step("sync", (lambda ns_, t_: lambda e: e.dma_start(
                xn_dram[t_ * 128:(t_ + 1) * 128, :], XNB[:, ns_:ns_ + 1024]))(ns, t),
                waits=wst, incs=[(s1st, 16)])
            if own:
                p.step("sync", (lambda t_: lambda e: e.dma_start(
                    xnlo_dram[t_ * 128:(t_ + 1) * 128, :], xnlo_b[:, 0:1024]))(t),
                    incs=[(s1st, 16)])
            store_done[t] = p.n(s1st)

        for t in range(NT + NSKEW):
            if t < NT:
                p1_A(t)
                p1_B(t)
                p1_C(t)
                p1_D(t)
            tp_ = t - NSKEW
            if 0 <= tp_ < NT:
                p1_E(tp_)
                p1_F(tp_)
                p1_G(tp_)

        xnf_loaded = {}
        xnf_free_w = {}

        def emit_xnf_load(g):
            slot = g % 2
            w = [(s1st, store_done[8 * g + 7])]
            if g >= 2:
                w.append(xnf_free_w[g - 2])
            for dc in range(8):
                p.step("sync", (lambda sl, gg, d: lambda e: e.dma_start(
                    xnf(sl, d), xn_dram[gg * GTOK:(gg + 1) * GTOK, d * 128:(d + 1) * 128],
                    transpose=True))(slot, g, dc),
                    waits=w if dc == 0 else [], incs=[(s2t, 16)])
            xnf_loaded[g] = p.n(s2t)

        emit_xnf_load(0)
        emit_xnf_load(1)

        def emit_rope_tm(src_ps, ti, dst, sem, extra_w=()):
            c3 = cos_sb[:, ti * 64:(ti + 1) * 64].rearrange(
                "p (o x) -> p o x", o=1).broadcast_to([128, 16, 64])
            s3 = sin_sb[:, ti * 64:(ti + 1) * 64].rearrange(
                "p (o x) -> p o x", o=1).broadcast_to([128, 16, 64])
            ps3 = src_ps.rearrange("p (h x) -> p h x", x=64)
            d3 = dst.rearrange("p (h x) -> p h x", x=64)
            sc3 = kr32q.rearrange("p (h x) -> p h x", x=64)
            p.step("vector", (lambda a_, b_: lambda e: e.tensor_tensor(sc3, a_, b_, ALU.mult))(ps3, c3),
                   waits=list(extra_w), incs=[(sem, 1)])
            p.step("vector", (lambda a_, b_, d_: lambda e: e.tensor_tensor(
                d_[:, :, 0:32], a_[:, :, 32:64], b_[:, :, 0:32], ALU.mult))(ps3, s3, d3),
                incs=[(sem, 1)])
            p.step("vector", (lambda a_, b_, d_: lambda e: e.tensor_tensor(
                d_[:, :, 32:64], a_[:, :, 0:32], b_[:, :, 32:64], ALU.mult))(ps3, s3, d3),
                incs=[(sem, 1)])
            p.step("vector", (lambda d_: lambda e: e.tensor_tensor(d_, d_, kr32q, ALU.add))(dst),
                   waits=[(sem, p.n(sem))], incs=[(sem, 1)])

        def emit_stats(dst, acc_col, sem):
            p.step("gpsimd", (lambda d_: lambda e: e.tensor_tensor(ksq, d_, d_, ALU.mult))(dst),
                   waits=[(sem, p.n(sem)), (s2stat, p.n(s2stat))], incs=[(s2stat, 1)])
            p.step("vector", lambda e: e.reduce_sum(
                stat[:, 48:64], ksq.rearrange("p (h x) -> p h x", x=64), AX),
                waits=[(s2stat, p.n(s2stat))], incs=[(s2stat, 1)])
            p.step("vector", (lambda c: lambda e: e.tensor_tensor(
                stat[:, c:c + 16], stat[:, c:c + 16], stat[:, 48:64], ALU.max))(acc_col),
                waits=[(s2stat, p.n(s2stat))], incs=[(s2stat, 1)])

        kst_done = {}
        vst_done = {}
        qst_done = {}
        proj_wait_extra = {}

        def emit_proj_tile(ti, qpass):
            g, tt = divmod(ti, 8)
            slot = ti % 2
            pb = bank(slot * 4, 4) if not qpass else bank(slot * 2, 2)
            w0 = [(s2t, xnf_loaded[g]), (s2w, W_LOADS)]
            key = ("q" if qpass else "kv", ti)
            w0 += proj_wait_extra.get(key, [])
            nmm = 2 if qpass else 4
            wmat = wq_sb if qpass else wkv
            wwid = 1024 if qpass else 2048
            for dc in range(8):
                p.step("tensor", (lambda sl, d, t_: lambda e: e.ldweights(
                    xnf(sl, d)[:, t_ * 128:t_ * 128 + 128]))(g % 2, dc, tt),
                    waits=w0 if dc == 0 else [])
                for sub in range(nmm):
                    p.step("tensor", (lambda sl, d, t_, sb_, b_, wm, wd_: lambda e: e.matmul(
                        b_[:, sb_ * 512:(sb_ + 1) * 512],
                        xnf(sl, d)[:, t_ * 128:t_ * 128 + 128],
                        wm[:, d * wd_ + sb_ * 512:d * wd_ + (sb_ + 1) * 512],
                        start=(d == 0), stop=(d == 7)))(g % 2, dc, tt, sub, pb, wmat, wwid),
                        incs=[(s2mm, 1)] if (dc == 7 and sub == nmm - 1) else [])
            mm_done = p.n(s2mm)
            if qpass:
                dst = q_ev[:, slot * 1024:slot * 1024 + 1024]
                ew = [(s2mm, mm_done)]
                if ti >= 2:
                    ew.append((s2st, qst_done.get(ti - 2, 0)))
                emit_rope_tm(pb[:, 0:1024], ti, dst, s2r, ew)
                emit_stats(dst, 16, s2r)
                p.step("gpsimd", (lambda d_, t_: lambda e: e.dma_start(
                    q_dram[t_ * 128:(t_ + 1) * 128, :], d_))(dst, ti),
                    waits=[(s2r, p.n(s2r)), (s2stat, p.n(s2stat))], incs=[(s2st, 16)])
                qst_done[ti] = p.n(s2st)
            else:
                kdst = k_ev[:, slot * 1024:slot * 1024 + 1024]
                ew = [(s2mm, mm_done)]
                if ti >= 2:
                    ew.append((s2st, kst_done.get(ti - 2, 0)))
                emit_rope_tm(pb[:, 0:1024], ti, kdst, s2r, ew)
                emit_stats(kdst, 0, s2r)
                p.step("gpsimd", (lambda d_, t_: lambda e: e.dma_start(
                    k_dram[t_ * 128:(t_ + 1) * 128, :], d_))(kdst, ti),
                    waits=[(s2r, p.n(s2r)), (s2stat, p.n(s2stat))], incs=[(s2st, 16)])
                kst_done[ti] = p.n(s2st)
                vdst = v_ev[:, slot * 1024:slot * 1024 + 1024]
                wv_ = [(s2mm, mm_done), (s2a, p.n(s2a))]
                if ti >= 2:
                    wv_.append((s2st, vst_done.get(ti - 2, 0)))
                p.step("scalar", (lambda b_, d_: lambda e: e.copy(d_, b_[:, 1024:2048]))(pb, vdst),
                       waits=wv_, incs=[(s2a, 1)])
                p.step("gpsimd", (lambda d_, t_: lambda e: e.dma_start(
                    v_dram[t_ * 128:(t_ + 1) * 128, :], d_))(vdst, ti),
                    waits=[(s2a, p.n(s2a))], incs=[(s2st, 16)])
                vst_done[ti] = p.n(s2st)

        rope_after = {}
        vact_after = {}
        q_rope_end = 0
        for ti in range(NT):
            if ti >= 2:
                rwait = rope_after[("kv", ti - 2)]
                if ti in (16, 17):
                    rwait = max(rwait, q_rope_end)
                proj_wait_extra[("kv", ti)] = [(s2r, rwait), (s2a, vact_after[ti - 2])]
            emit_proj_tile(ti, False)
            rope_after[("kv", ti)] = p.n(s2r)
            vact_after[ti] = p.n(s2a)
            if ti == 15:
                for qi in range(OWN_T):
                    if qi >= 2:
                        proj_wait_extra[("q", qi)] = [(s2r, rope_after[("q", qi - 2)])]
                    else:
                        proj_wait_extra[("q", qi)] = [(s2r, rope_after[("kv", 15)]),
                                                      (s2a, vact_after[15])]
                    emit_proj_tile(qi, True)
                    rope_after[("q", qi)] = p.n(s2r)
                q_rope_end = p.n(s2r)
                xnf_free_w[0] = (s2mm, p.n(s2mm))
                xnf_free_w[1] = (s2mm, p.n(s2mm))
                emit_xnf_load(2)
                emit_xnf_load(3)
            if ti >= 16 and ti % 8 == 7:
                g = ti // 8
                xnf_free_w[g] = (s2mm, p.n(s2mm))
                if g + 2 < NG:
                    emit_xnf_load(g + 2)
        LAST_ROPE = p.n(s2r)
        LAST_STAT = p.n(s2stat)
        LAST_MM = p.n(s2mm)
        LAST_VA = p.n(s2a)
        KV_ST = p.n(s2st)

        if "k" in taps:
            p.step("gpsimd", lambda e: e.dma_start(taps["k"], k_dram), waits=[(s2st, KV_ST)], incs=[(s2st, 16)])
        if "v" in taps:
            p.step("gpsimd", lambda e: e.dma_start(taps["v"], v_dram), waits=[(s2st, KV_ST)], incs=[(s2st, 16)])
        if "q" in taps:
            p.step("gpsimd", lambda e: e.dma_start(taps["q"], q_dram), waits=[(s2st, KV_ST)], incs=[(s2st, 16)])
        if "xn" in taps:
            p.step("gpsimd", lambda e: e.dma_start(taps["xn"], xn_dram), waits=[(s1st, store_done[NT - 1])], incs=[(s2st, 16)])

        XHI = [A2[:, dc * 2048:(dc + 1) * 2048] for dc in range(8)]
        XLO = [A1[:, dc * 2048:(dc + 1) * 2048] for dc in range(8)]
        for dc in range(8):
            p.step("sync", (lambda d: lambda e: e.dma_start(
                XHI[d], xn_dram[0:TOK, d * 128:(d + 1) * 128], transpose=True))(dc),
                waits=[(s2mm, LAST_MM), (s1st, store_done[NT - 1]), (s2a, LAST_VA)] if dc == 0 else [],
                incs=[(s2t, 16)])
        for dc in range(8):
            p.step("sync", (lambda d: lambda e: e.dma_start(
                XLO[d], xnlo_dram[:, d * 128:(d + 1) * 128], transpose=True))(dc),
                waits=[(s2mm, LAST_MM)] if dc == 0 else [],
                incs=[(s2t, 16)])
        GXL = p.n(s2t)
        LPS = PS[0:8, 0:2048]
        combos = [(XHI, gatehi), (XHI, gatelo), (XLO, gatehi)]
        first_mm = True
        for ci, (xsrc, gw) in enumerate(combos):
            for dc in range(8):
                w = []
                if first_mm:
                    w = [(s2t, GXL), (s2stat, 2), (s2w, W_LOADS), (s2r, LAST_ROPE),
                         (s2a, LAST_VA)]
                    first_mm = False
                p.step("tensor", (lambda gw_, d: lambda e: e.ldweights(gw_[:, d * 8:(d + 1) * 8]))(gw, dc),
                       waits=w)
                for sub in range(4):
                    st_ = (ci == 0 and dc == 0)
                    sp_ = (ci == 2 and dc == 7)
                    p.step("tensor", (lambda gw_, d, sb_, sr, st__, sp__: lambda e: e.matmul(
                        LPS[:, sb_ * 512:(sb_ + 1) * 512], gw_[:, d * 8:(d + 1) * 8],
                        sr[d][:, sb_ * 512:(sb_ + 1) * 512],
                        start=st__, stop=sp__))(gw, dc, sub, xsrc, st_, sp_),
                        incs=[(s2mm, 1)] if (ci == 2 and dc == 7 and sub == 3) else [])
        GATE_MM = p.n(s2mm)
        p.step("vector", lambda e: e.tensor_copy(lsb[:], LPS),
               waits=[(s2mm, GATE_MM), (s1v2, xn_done[NT - 1])], incs=[(s5v, 1)])
        LSB_C = p.n(s5v)
        if "logit" in taps:
            p.step("gpsimd", lambda e: e.dma_start(taps["logit"], lsb), waits=[(s5v, LSB_C)], incs=[(s2st, 16)])
        for tt in range(16):
            p.step("tensor", (lambda t_: lambda e: e.transpose(
                bank(0)[0:128, 0:8], lsb[0:8, t_ * 128:(t_ + 1) * 128], ident[0:8, 0:8]))(tt),
                waits=[(s5v, LSB_C + tt)], incs=[(s2mm, 1)])
            p.step("vector", (lambda t_: lambda e: e.tensor_copy(
                ltm[:, t_ * 8:(t_ + 1) * 8], bank(0)[0:128, 0:8]))(tt),
                waits=[(s2mm, p.n(s2mm))], incs=[(s5v, 1)])
        p.step("vector", lambda e: e.reduce_max(
            rmaxtm[:, 0:16], ltm.rearrange("p (t e) -> p t e", e=8), AX),
            waits=[(s5v, p.n(s5v))], incs=[(s5v, 1)])
        RMT = p.n(s5v)
        for tt in range(16):
            p.step("tensor", (lambda t_: lambda e: e.transpose(
                bank(1)[0:1, 0:128], rmaxtm[:, t_:t_ + 1], ident[:]))(tt),
                waits=[(s5v, RMT + tt)], incs=[(s2mm, 1)])
            p.step("vector", (lambda t_: lambda e: e.tensor_copy(
                rmaxrow[0:1, t_ * 128:(t_ + 1) * 128], bank(1)[0:1, 0:128]))(tt),
                waits=[(s2mm, p.n(s2mm))], incs=[(s5v, 1)])
        RMROW = p.n(s5v)
        for sub in range(4):
            p.step("tensor", (lambda sb_: lambda e: e.matmul(
                PS[0:8, sb_ * 512:(sb_ + 1) * 512], onesr32[0:1, 0:8],
                rmaxrow[0:1, sb_ * 512:(sb_ + 1) * 512], start=True, stop=True))(sub),
                waits=[(s5v, RMROW)] if sub == 0 else [], incs=[(s2mm, 1)])
            p.step("vector", (lambda sb_: lambda e: e.tensor_tensor(
                maskfm[0:8, sb_ * 512:(sb_ + 1) * 512], lsb[0:8, sb_ * 512:(sb_ + 1) * 512],
                PS[0:8, sb_ * 512:(sb_ + 1) * 512], ALU.is_ge))(sub),
                waits=[(s2mm, p.n(s2mm))], incs=[(s5m, 1)])
        MASK_DONE = p.n(s5m)
        if "mask" in taps:
            p.step("gpsimd", lambda e: e.dma_start(taps["mask"], maskfm[:]),
                   waits=[(s5m, MASK_DONE)], incs=[(s2st, 16)])

        p.step("tensor", lambda e: e.transpose(bank(2)[0:16, 0:128], stat[:, 0:16], ident[:]),
               waits=[(s2stat, LAST_STAT), (s5m, MASK_DONE)], incs=[(s2mm, 1)])
        p.step("tensor", lambda e: e.transpose(bank(2)[0:16, 128:256], stat[:, 16:32], ident[:]),
               incs=[(s2mm, 1)])
        TRC = p.n(s2mm)
        p.step("vector", lambda e: e.reduce_max(stat[0:16, 32:33], bank(2)[0:16, 0:128], AX),
               waits=[(s2mm, TRC), (s2stat, p.n(s2stat))], incs=[(s2stat, 1)])
        p.step("vector", lambda e: e.reduce_max(stat[0:16, 33:34], bank(2)[0:16, 128:256], AX),
               incs=[(s2stat, 1)])
        p.step("vector", lambda e: e.tensor_tensor(
            stat[0:16, 34:35], stat[0:16, 32:33], stat[0:16, 33:34], ALU.mult),
            waits=[(s2stat, p.n(s2stat))], incs=[(s2stat, 1)])
        p.step("scalar", lambda e: e.activation(stat[0:16, 35:36], stat[0:16, 34:35], AF.Sqrt),
               waits=[(s2stat, p.n(s2stat))], incs=[(s2stat, 1)])
        p.step("tensor", lambda e: e.transpose(bank(2)[0:1, 256:272], stat[0:16, 35:36], ident[0:16, 0:16]),
               waits=[(s2stat, p.n(s2stat))], incs=[(s2mm, 1)])
        p.step("vector", lambda e: e.reduce_max(stat[0:1, 36:37], bank(2)[0:1, 256:272], AX),
               waits=[(s2mm, p.n(s2mm)), (s2stat, p.n(s2stat))], incs=[(s2stat, 1)])
        p.step("vector", lambda e: e.tensor_scalar(
            stat[0:1, 37:38], stat[0:1, 36:37], -0.1275, None, ALU.mult),
            waits=[(s2stat, p.n(s2stat))], incs=[(s2stat, 1)])
        p.step("tensor", lambda e: e.matmul(bank(0)[:, 0:1], onesr32[:], stat[0:1, 37:38],
                                            start=True, stop=True),
               waits=[(s2stat, p.n(s2stat))], incs=[(s2mm, 1)])
        p.step("vector", lambda e: e.tensor_copy(bias_col[:], bank(0)[:, 0:1]),
               waits=[(s2mm, p.n(s2mm))], incs=[(s2stat, 1)])
        BIAS_DONE = p.n(s2stat)
        if "bias" in taps:
            p.step("gpsimd", lambda e: e.dma_start(taps["bias"], bias_col[:]),
                   waits=[(s2stat, BIAS_DONE)], incs=[(s2st, 16)])

        for pr in range(8):
            p.step("sync", (lambda pp: lambda e: e.dma_start(
                q_sb[:, pp * 2048:(pp + 1) * 2048],
                q_dram[0:TOK, pp * 128:(pp + 1) * 128], transpose=True))(pr),
                waits=[(s2st, KV_ST), (s2t, GXL)] if pr == 0 else [], incs=[(s2t, 16)])
        QSB_LD = p.n(s2t)
        p.step("gpsimd", lambda e: e.memset(vt_sb[:], 1.0),
               waits=[(s2r, LAST_ROPE), (s2stat, BIAS_DONE)], incs=[(s3v, 1)])
        VONES = p.n(s3v)

        blk_list = [(s, hg) for s in range(NSTRIP) for hg in range(NHG)]
        kch_ready = {}
        vch_ready = {}
        kch_free_w = {}
        load_order = [(s, hg, cc) for (s, hg) in blk_list for cc in range(4)]
        li = [0]

        def emit_kv_load(idx):
            s_, hg, cc = load_order[idx]
            cid = idx
            slot = cid % 4
            w = [(s2st, KV_ST), (s3v, VONES), (s2mm, GATE_MM), (s2t, QSB_LD)]
            if cid >= 4:
                w.append(kch_free_w[cid - 4])
            for pr in range(2):
                gpr = hg * 2 + pr
                p.step("sync", (lambda sl, gp, c_, pp: lambda e: e.dma_start(
                    kt_sb[:, sl * 4096 + pp * 2048:sl * 4096 + pp * 2048 + 2048],
                    k_dram[c_ * 2048:(c_ + 1) * 2048, gp * 128:(gp + 1) * 128],
                    transpose=True))(slot, gpr, cc, pr),
                    waits=w if pr == 0 else [], incs=[(s3k, 16)])
            kch_ready[(s_, hg, cc)] = p.n(s3k)
            for hx in range(4):
                wv_ = []
                if hx == 0:
                    wv_ = [(s3v, VONES)]
                    if cid >= 4:
                        wv_.append(kch_free_w[cid - 4])
                p.step("sync", (lambda sl, h_, c_, hx_: lambda e: e.dma_start(
                    vt_sb[:, 0:16896].rearrange("p (sl2 x) -> p sl2 x", sl2=4)[:, sl, 0:4160].rearrange("p (k h e) -> p k h e", k=16, h=4)[:, :, hx_, 0:64],
                    v_dram.rearrange("(ck p) (h e) -> p ck h e", p=128, h=16)[:, c_ * 16:(c_ + 1) * 16, h_ * 4 + hx_, :]))(slot, hg, cc, hx),
                    waits=wv_, incs=[(s3v, 16)])
            vch_ready[(s_, hg, cc)] = p.n(s3v)

        for _ in range(4):
            emit_kv_load(li[0]); li[0] += 1

        sc_done = {}
        ex_done = {}
        pv_done = {}
        ev_done = {}
        lb_done = {}
        ost_done = {}
        pending_bcast = [None]
        j = 0

        def emit_pv(pj):
            bi, rem = divmod(pj, NKT * 2)
            s_, hg = blk_list[bi]
            pkti, ppr = divmod(rem, 2)
            cc, kk = divmod(pkti, 16)
            cid = bi * 4 + cc
            slot = cid % 4
            pb = 16384 + (pj % 4) * 1024
            first = pkti == 0
            last = pkti == NKT - 1
            w = [(s3ex, ex_done[pj])]
            if first:
                w.append((s3v, vch_ready[(s_, hg, cc)]))
                if bi > 0:
                    w.append((s3ev, lb_done[bi - 1]))
            for h in range(2):
                hh = ppr * 2 + h
                vo = slot * 4224 + kk * 260 + hh * 65
                p.step("tensor", (lambda v_: lambda e: e.ldweights(vt_sb[:, v_:v_ + 128]))(vo),
                       waits=w if h == 0 else [])
                p.step("tensor", (lambda v_, hh_, pb_, f_, l_: lambda e: e.matmul(
                    PS[0:128, (4 + hh_) * 512:(4 + hh_) * 512 + 512],
                    vt_sb[:, v_:v_ + 128], A5[:, pb_ + (hh_ % 2) * 512:pb_ + (hh_ % 2) * 512 + 512],
                    start=f_, stop=l_))(vo, hh, pb, first, last),
                    incs=[(s3pv, 1)] if h == 1 else [])
            pv_done[pj] = p.n(s3pv)
            if kk == 15 and ppr == 1:
                kch_free_w[cid] = (s3pv, p.n(s3pv))
                if li[0] < len(load_order):
                    emit_kv_load(li[0]); li[0] += 1

        for bi, (s_, hg) in enumerate(blk_list):
            for kti in range(NKT):
                cc, kk = divmod(kti, 16)
                for pr in range(2):
                    so = (j % 2) * 1024
                    w = []
                    if kk == 0 and pr == 0:
                        w.append((s3k, kch_ready[(s_, hg, cc)]))
                    if j == 0:
                        w += [(s5m, MASK_DONE), (s2stat, BIAS_DONE)]
                    if j >= 2:
                        w.append((s3ex, ex_done[j - 2]))
                    slot = (bi * 4 + cc) % 4
                    ko = slot * 4096 + pr * 2048 + kk * 128
                    qo = (hg * 2 + pr) * 2048 + s_ * 512
                    p.step("tensor", (lambda k_: lambda e: e.ldweights(
                        kt_sb[0:64, k_:k_ + 128], tile_position=(0, 0)))(ko), waits=w)
                    p.step("tensor", (lambda k_: lambda e: e.ldweights(
                        kt_sb[64:128, k_:k_ + 128], tile_position=(64, 0)))(ko))
                    p.step("tensor", (lambda k_, q_, so_: lambda e: e.matmul(
                        PS[:, so_:so_ + 512], kt_sb[0:64, k_:k_ + 128], q_sb[0:64, q_:q_ + 512],
                        start=True, stop=True, tile_position=(0, 0)))(ko, qo, so))
                    p.step("tensor", (lambda k_, q_, so_: lambda e: e.matmul(
                        PS[:, so_ + 512:so_ + 1024], kt_sb[64:128, k_:k_ + 128], q_sb[64:128, q_:q_ + 512],
                        start=True, stop=True, tile_position=(64, 0)))(ko, qo, so),
                        incs=[(s3sc, 1)])
                    sc_done[j] = p.n(s3sc)
                    we = [(s3sc, sc_done[j])]
                    if j == 0:
                        we.append((s2stat, BIAS_DONE))
                    if j >= 4:
                        we.append((s3pv, pv_done[j - 4]))
                    p.step("scalar", (lambda so_, pb_: lambda e: e.activation(
                        A5[:, pb_:pb_ + 1024], PS[:, so_:so_ + 1024], AF.Exp,
                        bias=bias_col[:], scale=0.125))(so, 16384 + (j % 4) * 1024),
                        waits=we, incs=[(s3ex, 1)])
                    ex_done[j] = p.n(s3ex)
                    if j % (NKT * 2) == 1 and pending_bcast[0] is not None:
                        pending_bcast[0]()
                        pending_bcast[0] = None
                    if j >= 1 and (j - 1) % (NKT * 2) != (NKT * 2 - 1):
                        emit_pv(j - 1)
                    j += 1
            emit_pv(j - 1)
            w = [(s3pv, pv_done[j - 1])]
            if bi > 0:
                w.append((s3st, ost_done[bi - 1]))
            p.step("vector", lambda e: e.tensor_copy(orawB[0:65, :], PS[0:65, 2048:4096]),
                   waits=w, incs=[(s3ev, 1)])
            p.step("vector", lambda e: e.reciprocal(lrow, orawB[64:65, 0:2048]),
                   waits=[(s3ev, p.n(s3ev))], incs=[(s3ev, 1)])
            p.step("vector", lambda e: e.tensor_copy(lrow16[0:1, :], lrow),
                   waits=[(s3ev, p.n(s3ev))], incs=[(s3ev, 1)])
            LR = p.n(s3ev)
            if "l" in taps:
                p.step("gpsimd", (lambda bi_: lambda e: e.dma_start(
                    taps["l"][0:1, bi_ * 2048:(bi_ + 1) * 2048], lrow))(bi),
                    waits=[(s3ev, LR)], incs=[(s2st, 16)])
            def mk_bcast(LR_, bi_, hg_, ss_):
                def emit_bc():
                    for hh in range(4):
                        p.step("tensor", (lambda h_: lambda e: e.matmul(
                            PS[0:64, (4 + h_) * 512:(4 + h_) * 512 + 512],
                            onesr16[0:1, 0:64], lrow16[0:1, h_ * 512:h_ * 512 + 512],
                            start=True, stop=True))(hh),
                            waits=[(s3ev, LR_)] if hh == 0 else [],
                            incs=[(s3mx, 1)] if hh == 3 else [])
                    p.step("vector", lambda e: e.tensor_copy(lb4[0:64, :], PS[0:64, 2048:4096]),
                           waits=[(s3mx, p.n(s3mx)), (s3ev, p.n(s3ev))], incs=[(s3ev, 1)])
                    lb_done[bi_] = p.n(s3ev)
                    p.step("vector", lambda e: e.tensor_tensor(
                        o_tmp[0:64, :], orawB[0:64, :], lb4[0:64, :], ALU.mult),
                        waits=[(s3ev, lb_done[bi_])], incs=[(s3ev, 1)])
                    ev_done[bi_] = p.n(s3ev)
                    p.step("gpsimd", (lambda: lambda e: e.dma_start(
                        o_dram.rearrange("(hh q) t -> q hh t", q=64)[:, hg_ * 4:(hg_ + 1) * 4, ss_ * 512:(ss_ + 1) * 512],
                        o_tmp[0:64, :].rearrange("q (hh t) -> q hh t", t=512)))(),
                        waits=[(s3ev, ev_done[bi_])], incs=[(s3st, 16)])
                    ost_done[bi_] = p.n(s3st)
                return emit_bc
            pending_bcast[0] = mk_bcast(LR, bi, hg, s_)
            if bi == len(blk_list) - 1:
                pending_bcast[0]()
                pending_bcast[0] = None

        ATT_DONE = p.n(s3ev)
        EXP_ALL = p.n(s3ex)
        PV_ALL = p.n(s3pv)
        O_ST = p.n(s3st)
        if "o" in taps:
            p.step("gpsimd", lambda e: e.dma_start(taps["o"], o_dram), waits=[(s3st, O_ST)], incs=[(s2st, 16)])

        for dc in range(8):
            p.step("gpsimd", (lambda d: lambda e: e.dma_start(
                wout[:, d * 1024:(d + 1) * 1024], woutT[d * 128:(d + 1) * 128, :]))(dc),
                waits=[(s3ev, ATT_DONE), (s3pv, PV_ALL)] if dc == 0 else [], incs=[(s2st, 16)])
        for dc in range(8):
            p.step("sync", (lambda d: lambda e: e.dma_start(
                o_sb[:, d * 2048:(d + 1) * 2048], o_dram[d * 128:(d + 1) * 128, :]))(dc),
                waits=[(s3st, O_ST)] if dc == 0 else [], incs=[(s2st, 16)])
        WOUT_LD = p.n(s2st)

        s4_ev = {}
        for tt in range(16):
            slot = tt % 2
            sb4 = bank(slot * 2, 2)
            w0 = [(s2st, WOUT_LD), (s3ex, EXP_ALL)]
            if tt >= 2:
                w0.append((s4e, s4_ev[tt - 2]))
            for dc in range(8):
                p.step("tensor", (lambda d, t_: lambda e: e.ldweights(
                    o_sb[:, d * 2048 + t_ * 128:d * 2048 + t_ * 128 + 128]))(dc, tt),
                    waits=w0 if dc == 0 else [])
                for half in range(2):
                    p.step("tensor", (lambda d, h, b_, t_: lambda e: e.matmul(
                        b_[:, h * 512:(h + 1) * 512],
                        o_sb[:, d * 2048 + t_ * 128:d * 2048 + t_ * 128 + 128],
                        wout[:, d * 1024 + h * 512:d * 1024 + (h + 1) * 512],
                        start=(d == 0), stop=(d == 7)))(dc, half, sb4, tt),
                        incs=[(s4, 1)] if dc == 7 and half == 1 else [])
            wev = [(s4, p.n(s4))]
            if tt >= 2:
                wev.append((s4st, 16 * (tt - 1)))
            p.step("vector", (lambda sl, b_: lambda e: e.tensor_copy(
                p_sb[:, sl * 1024:sl * 1024 + 1024], b_))(slot, sb4),
                waits=wev, incs=[(s4e, 1)])
            s4_ev[tt] = p.n(s4e)
            p.step("gpsimd", (lambda sl, t_: lambda e: e.dma_start(
                s_dram[t_ * 128:(t_ + 1) * 128, :], p_sb[:, sl * 1024:sl * 1024 + 1024]))(slot, tt),
                waits=[(s4e, s4_ev[tt])], incs=[(s4st, 16), (s2st, 16)])
        S_ST = p.n(s4st)
        P4_EV = p.n(s4e)
        if "s" in taps:
            p.step("gpsimd", lambda e: e.dma_start(taps["s"], s_dram), waits=[(s4st, S_ST)], incs=[(s2st, 16)])

        for dc in range(8):
            p.step("sync", (lambda d: lambda e: e.dma_start(
                xnof[:, d * 2048:(d + 1) * 2048],
                xn_dram[0:TOK, d * 128:(d + 1) * 128], transpose=True))(dc),
                waits=[(s4e, s4_ev[15])] if dc == 0 else [], incs=[(s2t, 16)])
        XNOF_LD = p.n(s2t)

        for rc in range(16):
            p.step("gpsimd", (lambda r_: lambda e: e.dma_start(
                wd_sb[:, r_ * 1024:(r_ + 1) * 1024],
                wd_in[r_ * 128:(r_ + 1) * 128, :]))(rc),
                waits=[(s3pv, PV_ALL), (s3ev, ATT_DONE)] if rc == 0 else [], incs=[(s5w, 16)])
        WD_LD = p.n(s5w)

        wgu_ld = {}
        wgu_free = {}

        def emit_wgu_load(key):
            half, e_ = divmod(key, NE)
            slot = key % 2
            w = []
            if key == 0:
                w = [(s5w, WD_LD)]
            if key >= 2:
                w.append(wgu_free[key - 2])
            for op, win in ((0, wg_in), (1, wu_in)):
                for ch in range(8):
                    p.step("gpsimd", (lambda sl, o_, c_, wi, ee: lambda e: e.dma_start(
                        wgu[:, sl * 4096 + o_ * 2048 + c_ * 256:sl * 4096 + o_ * 2048 + (c_ + 1) * 256],
                        wi[ee * 1024 + c_ * 128:ee * 1024 + (c_ + 1) * 128, :]))(slot, op, ch, win, e_),
                        waits=(w if (op == 0 and ch == 0) else []), incs=[(s5w, 16)])
            wgu_ld[key] = p.n(s5w)

        emit_wgu_load(0)
        emit_wgu_load(1)

        gu_psum_free = {}
        ebm_done = {}
        inter_done = {}
        down_ev = {}
        oui = 0
        for half in range(2):
            hcol = half * 1024
            for e_ in range(NE):
                key = half * NE + e_
                eslot = key % 2
                gslot = oui % 2
                gb = bank(gslot * 4, 4)
                w = [(s5m, MASK_DONE), (s5w, wgu_ld[key]), (s4st, S_ST), (s2t, XNOF_LD)]
                if key == 0:
                    w += [(s3ev, ATT_DONE), (s4e, P4_EV)]
                if half == 1 and e_ == 0:
                    w.append((s5a, down_ev[(0, 7)]))
                if oui >= 2:
                    w.append(gu_psum_free[oui - 2])
                for sub in range(2):
                    p.step("tensor", (lambda sb_, b_, ee, hc: lambda e: e.matmul(
                        b_[:, sb_ * 512:(sb_ + 1) * 512], sel8[:, ee * 128:ee * 128 + 128],
                        maskfm[0:8, hc + sb_ * 512:hc + (sb_ + 1) * 512],
                        start=True, stop=True))(sub, gb, e_, hcol),
                        waits=w if sub == 0 else [], incs=[(s2mm, 1)] if sub == 1 else [])
                mb_mm = p.n(s2mm)
                p.step("vector", (lambda sl, b_: lambda e: e.tensor_copy(
                    ebm[:, sl * 2048:sl * 2048 + 1024], b_[:, 0:1024]))(eslot, gb),
                    waits=[(s2mm, mb_mm)], incs=[(s5v, 1)])
                ebm_done[key] = p.n(s5v)
                gu_mm = {}
                for op in range(2):
                    slot = oui % 2
                    ob = bank(slot * 4, 4)
                    w0 = []
                    if op == 0:
                        w0.append((s5v, ebm_done[key]))
                    else:
                        if oui >= 2:
                            w0.append(gu_psum_free[oui - 2])
                        if half == 1 and e_ == 0:
                            w0.append((s5a, down_ev[(0, 7)]))
                    for Mt in range(2):
                        for dc in range(8):
                            p.step("tensor", (lambda sl_, o_, c_, kk_: lambda e: e.ldweights(
                                wgu[:, (kk_ % 2) * 4096 + o_ * 2048 + c_ * 256 + sl_ * 128:
                                       (kk_ % 2) * 4096 + o_ * 2048 + c_ * 256 + sl_ * 128 + 128]))(Mt, op, dc, key),
                                waits=w0 if (Mt == 0 and dc == 0) else [])
                            for sub in range(2):
                                p.step("tensor", (lambda o_, c_, m_, sb_, b_, kk_, hc: lambda e: e.matmul(
                                    b_[:, m_ * 1024 + sb_ * 512:m_ * 1024 + (sb_ + 1) * 512],
                                    wgu[:, (kk_ % 2) * 4096 + o_ * 2048 + c_ * 256 + m_ * 128:
                                           (kk_ % 2) * 4096 + o_ * 2048 + c_ * 256 + m_ * 128 + 128],
                                    xnof[:, c_ * 2048 + hc + sb_ * 512:c_ * 2048 + hc + (sb_ + 1) * 512],
                                    start=(c_ == 0), stop=(c_ == 7)))(op, dc, Mt, sub, ob, key, hcol),
                                    incs=[(s5, 1)] if (Mt == 1 and dc == 7 and sub == 1) else [])
                    gu_mm[op] = p.n(s5)
                    if op == 0:
                        wr = [(s5, gu_mm[0]), (s5a, p.n(s5a))]
                        if key >= 1:
                            wr.append((s5v, inter_done[key - 1]))
                        p.step("scalar", (lambda b_: lambda e: e.activation(
                            gr_sb[:, 0:2048], b_[:, 0:2048], AF.Relu))(ob),
                            waits=wr, incs=[(s5a, 1)])
                        gu_psum_free[oui] = (s5a, p.n(s5a))
                    else:
                        for Mt in range(2):
                            p.step("vector", (lambda b_, sl, m_: lambda e: e.tensor_tensor(
                                um_sb[:, m_ * 1024:m_ * 1024 + 1024],
                                b_[:, m_ * 1024:m_ * 1024 + 1024],
                                ebm[:, sl * 2048:sl * 2048 + 1024], ALU.mult))(ob, eslot, Mt),
                                waits=([(s5, gu_mm[1]), (s5v, p.n(s5v))] if Mt == 0 else []),
                                incs=[(s5v, 1)])
                        gu_psum_free[oui] = (s5v, p.n(s5v))
                    oui += 1
                p.step("vector", lambda e: e.tensor_tensor(
                    gr2_sb[:, 0:2048], gr_sb[:, 0:2048], gr_sb[:, 0:2048], ALU.mult),
                    waits=[(s5a, p.n(s5a)), (s5v, p.n(s5v))], incs=[(s5v, 1)])
                p.step("vector", (lambda ee: lambda e: e.tensor_tensor(
                    inter[:, ee * 2048:ee * 2048 + 2048], gr2_sb[:, 0:2048], um_sb[:, 0:2048], ALU.mult))(e_),
                    waits=[(s5v, p.n(s5v))], incs=[(s5v, 1)])
                inter_done[key] = p.n(s5v)
                wgu_free[key] = (s5, gu_mm[1])
                nk = key + 2
                if nk < 2 * NE:
                    emit_wgu_load(nk)
            for tt in range(8):
                slot = tt % 2
                db = bank(slot * 2, 2)
                w0 = [(s5v, inter_done[half * NE + NE - 1]), (s5w, WD_LD)]
                if oui >= 2:
                    w0.append(gu_psum_free[oui - 2])
                    w0.append(gu_psum_free[oui - 1])
                if tt >= 2:
                    w0.append((s5a, down_ev[(half, tt - 2)]))
                for rc in range(16):
                    p.step("tensor", (lambda r_, t_: lambda e: e.ldweights(
                        inter[:, r_ * 1024 + t_ * 128:r_ * 1024 + t_ * 128 + 128]))(rc, tt),
                        waits=w0 if rc == 0 else [])
                    for hf in range(2):
                        p.step("tensor", (lambda r_, t_, h_, b_: lambda e: e.matmul(
                            b_[:, h_ * 512:(h_ + 1) * 512],
                            inter[:, r_ * 1024 + t_ * 128:r_ * 1024 + t_ * 128 + 128],
                            wd_sb[:, r_ * 1024 + h_ * 512:r_ * 1024 + (h_ + 1) * 512],
                            start=(r_ == 0), stop=(r_ == 15)))(rc, tt, hf, db),
                            incs=[(s5, 1)] if rc == 15 and hf == 1 else [])
                p.step("scalar", (lambda b_, t_, hf_: lambda e: e.activation(
                    gsil[:, (hf_ * 8 + t_) * 1024:(hf_ * 8 + t_) * 1024 + 1024],
                    b_[:, 0:1024], AF.Silu))(db, tt, half),
                    waits=[(s5, p.n(s5)), (s5a, p.n(s5a))] +
                          ([(s4e, P4_EV), (s2st, WOUT_LD)] if (half == 0 and tt == 0) else []),
                    incs=[(s5a, 1)])
                down_ev[(half, tt)] = p.n(s5a)
        FFN_DONE = p.n(s5a)
        if "g" in taps:
            p.step("gpsimd", lambda e: e.dma_start(taps["g"], gsil), waits=[(s5a, FFN_DONE)], incs=[(s2st, 16)])

        out_done = {}
        for tt in range(16):
            slot = tt % 2
            wx = [(s6v, out_done[tt - 2])] if tt >= 2 else [(s4st, S_ST)]
            p.step("sync", (lambda sl, t_: lambda e: e.dma_start(
                x6[:, sl * 1024:sl * 1024 + 1024], xbm[t_ * 128:(t_ + 1) * 128, :]))(slot, tt),
                waits=wx, incs=[(s6x, 16)])
            ws = [(s4st, 16 * (tt + 1)), (s5a, FFN_DONE)]
            if tt >= 2:
                ws.append((s6v, out_done[tt - 2]))
            p.step("sync", (lambda sl, t_: lambda e: e.dma_start(
                sload[:, sl * 1024:sl * 1024 + 1024], s_dram[t_ * 128:(t_ + 1) * 128, :]))(slot, tt),
                waits=ws, incs=[(s6s, 16)])
            p.step("vector", (lambda sl, gc: lambda e: e.tensor_tensor(
                o6[:, sl * 1024:sl * 1024 + 1024], sload[:, sl * 1024:sl * 1024 + 1024],
                gsil[:, gc:gc + 1024], ALU.mult))(slot, tt * 1024),
                waits=[(s6s, 16 * (tt + 1)), (s5a, FFN_DONE)] +
                      ([(s6st, 16 * (tt - 1))] if tt >= 2 else []),
                incs=[(s6v, 1)])
            p.step("vector", (lambda sl: lambda e: e.tensor_tensor(
                o6[:, sl * 1024:sl * 1024 + 1024], o6[:, sl * 1024:sl * 1024 + 1024],
                x6[:, sl * 1024:sl * 1024 + 1024], ALU.add))(slot),
                waits=[(s6x, 16 * (tt + 1)), (s6v, p.n(s6v))], incs=[(s6v, 1)])
            out_done[tt] = p.n(s6v)
            p.step("gpsimd", (lambda sl, t_: lambda e: e.dma_start(
                out[t_ * 128:(t_ + 1) * 128, :], o6[:, sl * 1024:sl * 1024 + 1024]))(slot, tt),
                waits=[(s6v, out_done[tt])], incs=[(s6st, 16)])
        p.step("gpsimd", lambda e: e.nop(), waits=[(s6st, 16 * 16)])

        p.check_deadlock()

        semh = {}
        for name in p.semnames:
            semh[name] = es.enter_context(nc.semaphore(name))
        block = es.enter_context(nc.Block())

        @block.tensor
        def _(t):
            p.emit("tensor", t, semh)

        @block.vector
        def _(v):
            p.emit("vector", v, semh)

        @block.scalar
        def _(s):
            p.emit("scalar", s, semh)

        @block.gpsimd
        def _(g):
            p.emit("gpsimd", g, semh)

        @block.sync
        def _(sy):
            p.emit("sync", sy, semh)

    return nc


_ROPE_BASE = 10000.0


def _host_tables(off):
    half = 32
    inv = 1.0 / _ROPE_BASE ** (np.arange(half, dtype=np.float64) / half)
    pos = np.arange(L, dtype=np.float64)
    pos = np.concatenate([pos[off:], pos[:off]])
    ang = pos[:, None] * inv[None, :]
    c = np.cos(ang).astype(np.float32)
    s = np.sin(ang).astype(np.float32)
    cost = np.concatenate([c, c], axis=1)
    sint = np.concatenate([-s, s], axis=1)
    return np.ascontiguousarray(cost), np.ascontiguousarray(sint)


def _host_consts():
    sel8 = np.zeros((8, 1024), np.float32)
    for e in range(8):
        sel8[e, e * 128:(e + 1) * 128] = 1
    ident = np.eye(128, dtype=np.float32)
    onesr = np.ones((1, 128), np.float32)
    return sel8, ident, onesr


_NC_CACHE = {}


def kernel(L0_attn_x, L0_attn_w_qkv, L0_attn_w_out, L0_ffn_gate_w,
           L0_ffn_wg, L0_ffn_wu, L0_ffn_wd, _debug_taps=(), _trace=False):
    x = np.asarray(L0_attn_x, dtype=np.float32)
    w_qkv = np.asarray(L0_attn_w_qkv, dtype=np.float32)
    w_out = np.asarray(L0_attn_w_out, dtype=np.float32)
    gate_w = np.asarray(L0_ffn_gate_w, dtype=np.float32)
    wg = np.asarray(L0_ffn_wg, dtype=np.float32)
    wu = np.asarray(L0_ffn_wu, dtype=np.float32)
    wd = np.asarray(L0_ffn_wd, dtype=np.float32)

    wkvT = np.ascontiguousarray(w_qkv[D:3 * D].T)
    wqT = np.ascontiguousarray(w_qkv[0:D].T)
    gateT = np.ascontiguousarray(gate_w.T)
    wg_f = np.ascontiguousarray(wg.reshape(NE * D, DFF))
    wu_f = np.ascontiguousarray(wu.reshape(NE * D, DFF))
    wd_f = np.ascontiguousarray(wd.reshape(NE * DFF, D))
    woutT_ = np.ascontiguousarray(w_out.T)
    sel8, ident, onesr = _host_consts()

    key = tuple(sorted(_debug_taps))
    if key not in _NC_CACHE:
        _NC_CACHE[key] = build_nc(_debug_taps)
    nc = _NC_CACHE[key]

    in_maps = []
    for c in range(8):
        b = c // 4
        off = 2048 * (c % 4)
        xb_ = np.ascontiguousarray(np.concatenate([x[b, off:], x[b, :off]], axis=0))
        cost, sint = _host_tables(off)
        in_maps.append({
            "xb": xb_, "wkvT": wkvT, "wqT": wqT, "gateT": gateT,
            "wg": wg_f, "wu": wu_f, "wd": wd_f, "woutT": woutT_,
            "costab": cost, "sintab": sint, "sel8_in": sel8,
            "ident_in": ident, "onesr": onesr,
        })
    res = run_bass_kernel_spmd(nc, in_maps, core_ids=list(range(8)), trace=_trace)
    outp = np.empty((B, L, D), np.float32)
    for c in range(8):
        b = c // 4
        off = 2048 * (c % 4)
        outp[b, off:off + 2048] = res.results[c]["out"]
    if _debug_taps:
        return outp, res
    return outp


# version 04
# speedup vs baseline: 1.2347x; 1.2347x over previous
import numpy as np
import ml_dtypes
import concourse.bass as bass
import concourse.mybir as mybir
from concourse.bass_utils import run_bass_kernel_spmd
from contextlib import ExitStack

F32 = mybir.dt.float32
BF16 = mybir.dt.bfloat16
AF = mybir.ActivationFunctionType
ALU = mybir.AluOpType
AX = mybir.AxisListType.X

B, L, D = 2, 8192, 1024
NPAIR = 8
TOK = 2048
NT = L // 128
OWN_T = TOK // 128
NG = 8
GTOK = 1024
NKT = L // 128
NSTRIP = TOK // 512
NHG = 4
NE, DFF = 8, 256

ENGS = ("tensor", "vector", "scalar", "gpsimd", "sync")


class Prog:
    def __init__(self):
        self.steps = {e: [] for e in ENGS}
        self.semnames = []
        self.counts = {}

    def sem(self, name):
        self.semnames.append(name)
        self.counts[name] = 0
        return name

    def n(self, s):
        return self.counts[s]

    def step(self, eng, fn, waits=(), incs=()):
        waits = [(s, v) for (s, v) in waits if v > 0]
        for s, k in incs:
            self.counts[s] += k
        self.steps[eng].append((fn, waits, list(incs)))

    def check_deadlock(self):
        idx = {e: 0 for e in ENGS}
        cnt = {s: 0 for s in self.semnames}
        progress = True
        while progress:
            progress = False
            for e in ENGS:
                while idx[e] < len(self.steps[e]):
                    fn, waits, incs = self.steps[e][idx[e]]
                    if all(cnt[s] >= v for s, v in waits):
                        for s, k in incs:
                            cnt[s] += k
                        idx[e] += 1
                        progress = True
                    else:
                        break
        stuck = {e: (idx[e], len(self.steps[e])) for e in ENGS if idx[e] < len(self.steps[e])}
        if stuck:
            detail = {}
            for e in stuck:
                fn, waits, incs = self.steps[e][idx[e]]
                detail[e] = [(s, v, cnt[s]) for s, v in waits if cnt[s] < v]
            raise RuntimeError(f"schedule deadlock: {stuck}; blocked: {detail}")

    def emit(self, eng_name, eng, semh):
        for fn, waits, incs in self.steps[eng_name]:
            for s, v in waits:
                eng.wait_ge(semh[s], v)
            ins = fn(eng)
            for s, k in incs:
                ins = ins.then_inc(semh[s], k)


def build_nc(debug_taps=()):
    nc = bass.Bass("TRN2", target_bir_lowering=False, debug=False)
    p = Prog()

    xbm = nc.dram_tensor("xb", [L, D], F32, kind="ExternalInput").ap()
    wkvT = nc.dram_tensor("wkvT", [D, 2048], F32, kind="ExternalInput").ap()
    wqT = nc.dram_tensor("wqT", [D, D], F32, kind="ExternalInput").ap()
    gateT = nc.dram_tensor("gateT", [D, 8], F32, kind="ExternalInput").ap()
    wg_in = nc.dram_tensor("wg", [NE * D, DFF], F32, kind="ExternalInput").ap()
    wu_in = nc.dram_tensor("wu", [NE * D, DFF], F32, kind="ExternalInput").ap()
    wd_in = nc.dram_tensor("wd", [NE * DFF, D], F32, kind="ExternalInput").ap()
    woutT = nc.dram_tensor("woutT", [D, D], F32, kind="ExternalInput").ap()
    costab = nc.dram_tensor("costab", [L, 64], F32, kind="ExternalInput").ap()
    sintab = nc.dram_tensor("sintab", [L, 64], F32, kind="ExternalInput").ap()
    sel8_in = nc.dram_tensor("sel8_in", [8, 1024], F32, kind="ExternalInput").ap()
    ident_in = nc.dram_tensor("ident_in", [128, 128], F32, kind="ExternalInput").ap()
    onesr_in = nc.dram_tensor("onesr", [1, 128], F32, kind="ExternalInput").ap()

    out = nc.dram_tensor("out", [TOK, D], F32, kind="ExternalOutput").ap()

    xn_dram = nc.dram_tensor("xn_dram", [L, D], BF16, kind="Internal").ap()
    xnlo_dram = nc.dram_tensor("xnlo_dram", [TOK, D], BF16, kind="Internal").ap()
    k_dram = nc.dram_tensor("k_dram", [L, D], BF16, kind="Internal").ap()
    q_dram = nc.dram_tensor("q_dram", [TOK, D], BF16, kind="Internal").ap()
    v_dram = nc.dram_tensor("v_dram", [L, D], BF16, kind="Internal").ap()
    o_dram = nc.dram_tensor("o_dram", [D, TOK], BF16, kind="Internal").ap()
    s_dram = nc.dram_tensor("s_dram", [TOK, D], BF16, kind="Internal").ap()

    taps = {}
    for name, shape, dt in [("q", [TOK, D], BF16), ("k", [L, D], BF16), ("v", [L, D], BF16),
                            ("o", [D, TOK], BF16), ("l", [1, NHG * NSTRIP * 2048], F32),
                            ("mask", [8, 2048], BF16), ("logit", [8, 2048], F32),
                            ("xn", [L, D], BF16), ("bias", [128, 1], F32),
                            ("s", [TOK, D], BF16), ("g", [128, 16384], BF16)]:
        if name in debug_taps:
            taps[name] = nc.dram_tensor("tap_" + name, shape, dt, kind="ExternalOutput").ap()

    es = ExitStack()
    with es:
        PS = es.enter_context(nc.psum_tensor([128, 4096], F32))

        def bank(i, w=1):
            return PS[:, i * 512:(i + w) * 512]

        A1 = es.enter_context(nc.sbuf_tensor("A1", [128, 16384], BF16, is_transpose_dest=True))
        A2 = es.enter_context(nc.sbuf_tensor("A2", [128, 16384], BF16, is_transpose_dest=True))
        A3 = es.enter_context(nc.sbuf_tensor("A3", [128, 16896], BF16, is_transpose_dest=True))
        A5 = es.enter_context(nc.sbuf_tensor("A5", [128, 18432], BF16, is_transpose_dest=True))
        A6 = es.enter_context(nc.sbuf_tensor("A6", [128, 16384], BF16))
        XNB = es.enter_context(nc.sbuf_tensor("XNB", [128, 4096], BF16))
        B1 = es.enter_context(nc.sbuf_tensor("B1", [128, 6144], F32))

        bn6 = es.enter_context(nc.sbuf_tensor("bn6", [128, 12], F32))
        mv = es.enter_context(nc.sbuf_tensor("mv", [128, 8], F32))
        rstd = es.enter_context(nc.sbuf_tensor("rstd", [128, 4], F32))
        nmur = es.enter_context(nc.sbuf_tensor("nmur", [128, 4], F32))
        stat = es.enter_context(nc.sbuf_tensor("stat", [128, 64], F32))
        bias_col = es.enter_context(nc.sbuf_tensor("bias_col", [128, 1], F32))
        eps_col = es.enter_context(nc.sbuf_tensor("eps_col", [128, 1], F32))
        ident = es.enter_context(nc.sbuf_tensor("ident", [128, 128], F32))
        onesr32 = es.enter_context(nc.sbuf_tensor("onesr32", [1, 128], F32))
        onesr16 = es.enter_context(nc.sbuf_tensor("onesr16", [1, 128], BF16))
        sel8 = es.enter_context(nc.sbuf_tensor("sel8", [8, 1024], BF16))
        gate32 = es.enter_context(nc.sbuf_tensor("gate32", [128, 64], F32))
        gatehi = es.enter_context(nc.sbuf_tensor("gatehi", [128, 64], BF16))
        gatelo = es.enter_context(nc.sbuf_tensor("gatelo", [128, 64], BF16))
        ltm = es.enter_context(nc.sbuf_tensor("ltm", [128, 128], F32))
        rmaxtm = es.enter_context(nc.sbuf_tensor("rmaxtm", [128, 16], F32))
        xnlo_b = es.enter_context(nc.sbuf_tensor("xnlo_b", [128, 1024], BF16))

        def xnf(slot, dc):
            return A1[:, slot * 8192 + dc * 1024:slot * 8192 + (dc + 1) * 1024]
        wkv = A2
        wq_sb = A3[:, 8192:16384]
        cos_sb = A3[:, 0:4096]
        sin_sb = A3[:, 4096:8192]
        k_ev = A6[:, 0:2048]
        v_ev = A6[:, 2048:4096]
        q_ev = A6[:, 4096:6144]
        ksq = A6[:, 6144:7168]
        kt_sb = A2
        vt_sb = A3
        q_sb = A5[:, 0:16384]
        p_sb = A6[:, 0:8192]
        xin = B1[:, 0:4096]
        xn32 = B1[:, 5120:6144]
        kr32q = B1[:, 4096:5120]
        orawB = B1[:, 4096:6144]
        lrow = B1[0:1, 0:2048]
        lsb = B1[0:8, 0:2048]
        rmaxrow = B1[0:1, 2048:4096]
        lrow16 = A1[0:1, 0:2048]
        lb4 = A1[0:64, 2048:4096]
        o_tmp = A1[0:64, 4096:6144]
        wout = A3[:, 0:8192]
        o_sb = A6
        inter = A1
        xnof = A3[:, 0:16384]
        gsil = A6[:, 0:16384]
        wgu = A5[:, 0:8192]
        ebm = A5[:, 8192:12288]
        gr_sb = A5[:, 12288:14336]
        gr2_sb = A5[:, 14336:16384]
        um_sb = A5[:, 16384:18432]
        wd_sb = A2
        maskfm = XNB[0:8, 0:2048]
        sload = A5[:, 0:2048]
        x6 = B1[:, 0:2048]
        o6 = B1[:, 2048:4096]

        s1x = p.sem("s1x"); s1v1 = p.sem("s1v1"); s1a = p.sem("s1a")
        s1v2 = p.sem("s1v2"); s1st = p.sem("s1st"); s1f = p.sem("s1f")
        s2w = p.sem("s2w"); s2t = p.sem("s2t"); s2mm = p.sem("s2mm")
        s2r = p.sem("s2r"); s2st = p.sem("s2st"); s2stat = p.sem("s2stat")
        s2a = p.sem("s2a"); s5v = p.sem("s5v"); s5m = p.sem("s5m")
        s3k = p.sem("s3k"); s3v = p.sem("s3v"); s3sc = p.sem("s3sc")
        s3ex = p.sem("s3ex"); s3pv = p.sem("s3pv"); s3ev = p.sem("s3ev")
        s3mx = p.sem("s3mx"); s3st = p.sem("s3st")
        s4 = p.sem("s4"); s4e = p.sem("s4e"); s4st = p.sem("s4st")
        s5 = p.sem("s5"); s5w = p.sem("s5w"); s5a = p.sem("s5a")
        s6x = p.sem("s6x"); s6s = p.sem("s6s"); s6v = p.sem("s6v"); s6st = p.sem("s6st")

        def dma(dst, src):
            return lambda e: e.dma_start(dst, src)

        for d_, s_ in [(ident[:], ident_in), (onesr32[:], onesr_in), (onesr16[:], onesr_in),
                       (sel8[:], sel8_in)]:
            p.step("gpsimd", dma(d_, s_), incs=[(s2w, 16)])
        p.step("gpsimd", dma(cos_sb[:], costab.rearrange("(t p) d -> p t d", p=128)), incs=[(s2w, 16)])
        p.step("gpsimd", dma(sin_sb[:], sintab.rearrange("(t p) d -> p t d", p=128)), incs=[(s2w, 16)])
        for dc in range(8):
            p.step("gpsimd", dma(wkv[:, dc * 2048:(dc + 1) * 2048],
                                 wkvT[dc * 128:(dc + 1) * 128, :]), incs=[(s2w, 16)])
        for dc in range(8):
            p.step("gpsimd", dma(wq_sb[:, dc * 1024:(dc + 1) * 1024],
                                 wqT[dc * 128:(dc + 1) * 128, :]), incs=[(s2w, 16)])
        for dc in range(8):
            p.step("gpsimd", dma(gate32[:, dc * 8:(dc + 1) * 8],
                                 gateT[dc * 128:(dc + 1) * 128, :]), incs=[(s2w, 16)])
        W_LOADS = p.n(s2w)
        p.step("vector", lambda e: e.tensor_copy(gatehi[:], gate32[:]),
               waits=[(s2w, W_LOADS)], incs=[(s2stat, 1)])
        p.step("vector", lambda e: e.tensor_tensor(gatelo[:], gate32[:], gatehi[:], ALU.subtract),
               waits=[(s2stat, 1)], incs=[(s2stat, 1)])
        p.step("vector", lambda e: e.memset(stat[:], 0.0), incs=[(s2stat, 1)])
        p.step("vector", lambda e: e.memset(eps_col[:], 1e-5), incs=[(s1a, 1)])

        xnf_loaded = {}
        xnf_free_w = {}

        def emit_xnf_load(g):
            slot = g % 2
            w = [(s1st, store_done[8 * g + 7])]
            if g >= 2:
                w.append(xnf_free_w[g - 2])
            for dc in range(8):
                p.step("sync", (lambda sl, gg, d: lambda e: e.dma_start(
                    xnf(sl, d), xn_dram[gg * GTOK:(gg + 1) * GTOK, d * 128:(d + 1) * 128],
                    transpose=True))(slot, g, dc),
                    waits=w if dc == 0 else [], incs=[(s2t, 16)])
            xnf_loaded[g] = p.n(s2t)

        store_done = {}
        xn_done = {}
        act_done = {}
        sqrt_done = {}
        lo_done = {}
        NSKEW = 2

        def p1_A(t):
            xs = (t % 4) * 1024
            w = [(s1a, act_done[t - 4])] if t >= 4 else []
            p.step("sync", (lambda xs_, t_: lambda e: e.dma_start(
                xin[:, xs_:xs_ + 1024], xbm[t_ * 128:(t_ + 1) * 128, :]))(xs, t),
                waits=w, incs=[(s1x, 16)])

        def p1_B(t):
            xs = (t % 4) * 1024
            ms = t % 4
            wb = [(s1x, 16 * (t + 1))]
            if t:
                wb.append((s1v1, 3 * t))
            if t >= 4:
                wb.append((s1a, act_done[t - 4]))
            p.step("vector", (lambda xs_: lambda e: e.bn_stats(bn6[:, 0:6], xin[:, xs_:xs_ + 512]))(xs),
                   waits=wb, incs=[(s1v1, 1)])
            p.step("vector", (lambda xs_: lambda e: e.bn_stats(bn6[:, 6:12], xin[:, xs_ + 512:xs_ + 1024]))(xs),
                   incs=[(s1v1, 1)])
            p.step("vector", (lambda ms_: lambda e: e.bn_aggr(mv[:, ms_ * 2:ms_ * 2 + 2], bn6[:]))(ms),
                   waits=[(s1v1, 3 * t + 2)], incs=[(s1v1, 1)])

        def p1_C(t):
            ms = t % 4
            w = [(s1v1, 3 * (t + 1)), (s1a, 1)]
            if t >= 4:
                w.append((s1v2, xn_done[t - 4]))
            p.step("scalar", (lambda ms_: lambda e: e.activation(
                rstd[:, ms_:ms_ + 1], mv[:, ms_ * 2 + 1:ms_ * 2 + 2], AF.Sqrt, bias=eps_col[:]))(ms),
                waits=w, incs=[(s1a, 1)])
            sqrt_done[t] = p.n(s1a)

        def p1_D(t):
            ms = t % 4
            p.step("vector", (lambda ms_: lambda e: e.reciprocal(rstd[:, ms_:ms_ + 1], rstd[:, ms_:ms_ + 1]))(ms),
                   waits=[(s1a, sqrt_done[t])], incs=[(s1v2, 1)])
            p.step("vector", (lambda ms_: lambda e: e.scalar_tensor_tensor(
                nmur[:, ms_:ms_ + 1], mv[:, ms_ * 2:ms_ * 2 + 1], -1.0, rstd[:, ms_:ms_ + 1],
                ALU.mult, ALU.mult))(ms),
                waits=[(s1v2, p.n(s1v2))], incs=[(s1v2, 1)])
            xn_done[t] = p.n(s1v2)

        def p1_E(t):
            own = t < OWN_T
            xs = (t % 4) * 1024
            ms = t % 4
            ns = (t % 4) * 1024
            wact = [(s1v2, xn_done[t])]
            if t >= 4:
                wact.append((s1st, store_done[t - 4]))
            p.step("scalar", (lambda xs_, ns_, ms_: lambda e: e.activation(
                XNB[:, ns_:ns_ + 1024], xin[:, xs_:xs_ + 1024], AF.Identity,
                bias=nmur[:, ms_:ms_ + 1], scale=rstd[:, ms_:ms_ + 1]))(xs, ns, ms),
                waits=wact, incs=[(s1a, 1)])
            if own:
                p.step("scalar", (lambda xs_, ms_: lambda e: e.activation(
                    xn32[:, 0:1024], xin[:, xs_:xs_ + 1024], AF.Identity,
                    bias=nmur[:, ms_:ms_ + 1], scale=rstd[:, ms_:ms_ + 1]))(xs, ms),
                    waits=([(s1f, lo_done[t - 1])] if t >= 1 else []),
                    incs=[(s1a, 1)])
            act_done[t] = p.n(s1a)

        def p1_F(t):
            if t < OWN_T:
                ns = (t % 4) * 1024
                p.step("gpsimd", (lambda ns_: lambda e: e.tensor_tensor(
                    xnlo_b[:, 0:1024], xn32[:, 0:1024], XNB[:, ns_:ns_ + 1024], ALU.subtract))(ns),
                    waits=[(s1a, act_done[t]), (s1st, store_done.get(t - 1, 0))],
                    incs=[(s1f, 1)])
            lo_done[t] = p.n(s1f)

        def p1_G(t):
            ns = (t % 4) * 1024
            own = t < OWN_T
            wst = [(s1a, act_done[t])]
            if own:
                wst.append((s1f, lo_done[t]))
            p.step("gpsimd", (lambda ns_, t_: lambda e: e.dma_start(
                xn_dram[t_ * 128:(t_ + 1) * 128, :], XNB[:, ns_:ns_ + 1024]))(ns, t),
                waits=wst, incs=[(s1st, 16)])
            if own:
                p.step("gpsimd", (lambda t_: lambda e: e.dma_start(
                    xnlo_dram[t_ * 128:(t_ + 1) * 128, :], xnlo_b[:, 0:1024]))(t),
                    incs=[(s1st, 16)])
            store_done[t] = p.n(s1st)

        for t in range(NT + NSKEW):
            if t < NT:
                p1_A(t)
                p1_B(t)
                p1_C(t)
                p1_D(t)
            tp_ = t - NSKEW
            if 0 <= tp_ < NT:
                p1_E(tp_)
                p1_F(tp_)
                p1_G(tp_)
                if tp_ == 7:
                    emit_xnf_load(0)
                if tp_ == 15:
                    emit_xnf_load(1)


        def emit_rope_tm(src_ps, ti, dst, sem, extra_w=()):
            c3 = cos_sb[:, ti * 64:(ti + 1) * 64].rearrange(
                "p (o x) -> p o x", o=1).broadcast_to([128, 16, 64])
            s3 = sin_sb[:, ti * 64:(ti + 1) * 64].rearrange(
                "p (o x) -> p o x", o=1).broadcast_to([128, 16, 64])
            ps3 = src_ps.rearrange("p (h x) -> p h x", x=64)
            d3 = dst.rearrange("p (h x) -> p h x", x=64)
            sc3 = kr32q.rearrange("p (h x) -> p h x", x=64)
            p.step("vector", (lambda a_, b_: lambda e: e.tensor_tensor(sc3, a_, b_, ALU.mult))(ps3, c3),
                   waits=list(extra_w), incs=[(sem, 1)])
            p.step("vector", (lambda a_, b_, d_: lambda e: e.tensor_tensor(
                d_[:, :, 0:32], a_[:, :, 32:64], b_[:, :, 0:32], ALU.mult))(ps3, s3, d3),
                incs=[(sem, 1)])
            p.step("vector", (lambda a_, b_, d_: lambda e: e.tensor_tensor(
                d_[:, :, 32:64], a_[:, :, 0:32], b_[:, :, 32:64], ALU.mult))(ps3, s3, d3),
                incs=[(sem, 1)])
            p.step("vector", (lambda d_: lambda e: e.tensor_tensor(d_, d_, kr32q, ALU.add))(dst),
                   waits=[(sem, p.n(sem))], incs=[(sem, 1)])

        def emit_stats(dst, acc_col, sem):
            p.step("gpsimd", (lambda d_: lambda e: e.tensor_tensor(ksq, d_, d_, ALU.mult))(dst),
                   waits=[(sem, p.n(sem)), (s2stat, p.n(s2stat))], incs=[(s2stat, 1)])
            p.step("vector", lambda e: e.reduce_sum(
                stat[:, 48:64], ksq.rearrange("p (h x) -> p h x", x=64), AX),
                waits=[(s2stat, p.n(s2stat))], incs=[(s2stat, 1)])
            p.step("vector", (lambda c: lambda e: e.tensor_tensor(
                stat[:, c:c + 16], stat[:, c:c + 16], stat[:, 48:64], ALU.max))(acc_col),
                waits=[(s2stat, p.n(s2stat))], incs=[(s2stat, 1)])

        kst_done = {}
        vst_done = {}
        qst_done = {}
        proj_wait_extra = {}

        def emit_proj_tile(ti, qpass):
            g, tt = divmod(ti, 8)
            slot = ti % 2
            pb = bank(slot * 4, 4) if not qpass else bank(slot * 2, 2)
            w0 = [(s2t, xnf_loaded[g]), (s2w, W_LOADS)]
            key = ("q" if qpass else "kv", ti)
            w0 += proj_wait_extra.get(key, [])
            nmm = 2 if qpass else 4
            wmat = wq_sb if qpass else wkv
            wwid = 1024 if qpass else 2048
            for dc in range(8):
                p.step("tensor", (lambda sl, d, t_: lambda e: e.ldweights(
                    xnf(sl, d)[:, t_ * 128:t_ * 128 + 128]))(g % 2, dc, tt),
                    waits=w0 if dc == 0 else [])
                for sub in range(nmm):
                    p.step("tensor", (lambda sl, d, t_, sb_, b_, wm, wd_: lambda e: e.matmul(
                        b_[:, sb_ * 512:(sb_ + 1) * 512],
                        xnf(sl, d)[:, t_ * 128:t_ * 128 + 128],
                        wm[:, d * wd_ + sb_ * 512:d * wd_ + (sb_ + 1) * 512],
                        start=(d == 0), stop=(d == 7)))(g % 2, dc, tt, sub, pb, wmat, wwid),
                        incs=[(s2mm, 1)] if (dc == 7 and sub == nmm - 1) else [])
            mm_done = p.n(s2mm)
            if qpass:
                dst = q_ev[:, slot * 1024:slot * 1024 + 1024]
                ew = [(s2mm, mm_done)]
                if ti >= 2:
                    ew.append((s2st, qst_done.get(ti - 2, 0)))
                emit_rope_tm(pb[:, 0:1024], ti, dst, s2r, ew)
                emit_stats(dst, 16, s2r)
                p.step("gpsimd", (lambda d_, t_: lambda e: e.dma_start(
                    q_dram[t_ * 128:(t_ + 1) * 128, :], d_))(dst, ti),
                    waits=[(s2r, p.n(s2r)), (s2stat, p.n(s2stat))], incs=[(s2st, 16)])
                qst_done[ti] = p.n(s2st)
            else:
                kdst = k_ev[:, slot * 1024:slot * 1024 + 1024]
                ew = [(s2mm, mm_done)]
                if ti >= 2:
                    ew.append((s2st, kst_done.get(ti - 2, 0)))
                emit_rope_tm(pb[:, 0:1024], ti, kdst, s2r, ew)
                emit_stats(kdst, 0, s2r)
                p.step("gpsimd", (lambda d_, t_: lambda e: e.dma_start(
                    k_dram[t_ * 128:(t_ + 1) * 128, :], d_))(kdst, ti),
                    waits=[(s2r, p.n(s2r)), (s2stat, p.n(s2stat))], incs=[(s2st, 16)])
                kst_done[ti] = p.n(s2st)
                vdst = v_ev[:, slot * 1024:slot * 1024 + 1024]
                wv_ = [(s2mm, mm_done), (s2a, p.n(s2a))]
                if ti >= 2:
                    wv_.append((s2st, vst_done.get(ti - 2, 0)))
                p.step("scalar", (lambda b_, d_: lambda e: e.copy(d_, b_[:, 1024:2048]))(pb, vdst),
                       waits=wv_, incs=[(s2a, 1)])
                p.step("gpsimd", (lambda d_, t_: lambda e: e.dma_start(
                    v_dram[t_ * 128:(t_ + 1) * 128, :], d_))(vdst, ti),
                    waits=[(s2a, p.n(s2a))], incs=[(s2st, 16)])
                vst_done[ti] = p.n(s2st)

        rope_after = {}
        vact_after = {}
        q_rope_end = 0
        for ti in range(NT):
            if ti >= 2:
                rwait = rope_after[("kv", ti - 2)]
                if ti in (16, 17):
                    rwait = max(rwait, q_rope_end)
                proj_wait_extra[("kv", ti)] = [(s2r, rwait), (s2a, vact_after[ti - 2])]
            emit_proj_tile(ti, False)
            rope_after[("kv", ti)] = p.n(s2r)
            vact_after[ti] = p.n(s2a)
            if ti == 15:
                for qi in range(OWN_T):
                    if qi >= 2:
                        proj_wait_extra[("q", qi)] = [(s2r, rope_after[("q", qi - 2)])]
                    else:
                        proj_wait_extra[("q", qi)] = [(s2r, rope_after[("kv", 15)]),
                                                      (s2a, vact_after[15])]
                    emit_proj_tile(qi, True)
                    rope_after[("q", qi)] = p.n(s2r)
                q_rope_end = p.n(s2r)
                xnf_free_w[0] = (s2mm, p.n(s2mm))
                xnf_free_w[1] = (s2mm, p.n(s2mm))
                emit_xnf_load(2)
                emit_xnf_load(3)
            if ti >= 16 and ti % 8 == 7:
                g = ti // 8
                xnf_free_w[g] = (s2mm, p.n(s2mm))
                if g + 2 < NG:
                    emit_xnf_load(g + 2)
        LAST_ROPE = p.n(s2r)
        LAST_STAT = p.n(s2stat)
        LAST_MM = p.n(s2mm)
        LAST_VA = p.n(s2a)
        KV_ST = p.n(s2st)

        if "k" in taps:
            p.step("gpsimd", lambda e: e.dma_start(taps["k"], k_dram), waits=[(s2st, KV_ST)], incs=[(s2st, 16)])
        if "v" in taps:
            p.step("gpsimd", lambda e: e.dma_start(taps["v"], v_dram), waits=[(s2st, KV_ST)], incs=[(s2st, 16)])
        if "q" in taps:
            p.step("gpsimd", lambda e: e.dma_start(taps["q"], q_dram), waits=[(s2st, KV_ST)], incs=[(s2st, 16)])
        if "xn" in taps:
            p.step("gpsimd", lambda e: e.dma_start(taps["xn"], xn_dram), waits=[(s1st, store_done[NT - 1])], incs=[(s2st, 16)])

        XHI = [A2[:, dc * 2048:(dc + 1) * 2048] for dc in range(8)]
        XLO = [A1[:, dc * 2048:(dc + 1) * 2048] for dc in range(8)]
        for dc in range(8):
            p.step("sync", (lambda d: lambda e: e.dma_start(
                XHI[d], xn_dram[0:TOK, d * 128:(d + 1) * 128], transpose=True))(dc),
                waits=[(s2mm, LAST_MM), (s1st, store_done[NT - 1]), (s2a, LAST_VA)] if dc == 0 else [],
                incs=[(s2t, 16)])
        for dc in range(8):
            p.step("sync", (lambda d: lambda e: e.dma_start(
                XLO[d], xnlo_dram[:, d * 128:(d + 1) * 128], transpose=True))(dc),
                waits=[(s2mm, LAST_MM)] if dc == 0 else [],
                incs=[(s2t, 16)])
        GXL = p.n(s2t)
        LPS = PS[0:8, 0:2048]
        combos = [(XHI, gatehi), (XHI, gatelo), (XLO, gatehi)]
        first_mm = True
        for ci, (xsrc, gw) in enumerate(combos):
            for dc in range(8):
                w = []
                if first_mm:
                    w = [(s2t, GXL), (s2stat, 2), (s2w, W_LOADS), (s2r, LAST_ROPE),
                         (s2a, LAST_VA)]
                    first_mm = False
                p.step("tensor", (lambda gw_, d: lambda e: e.ldweights(gw_[:, d * 8:(d + 1) * 8]))(gw, dc),
                       waits=w)
                for sub in range(4):
                    st_ = (ci == 0 and dc == 0)
                    sp_ = (ci == 2 and dc == 7)
                    p.step("tensor", (lambda gw_, d, sb_, sr, st__, sp__: lambda e: e.matmul(
                        LPS[:, sb_ * 512:(sb_ + 1) * 512], gw_[:, d * 8:(d + 1) * 8],
                        sr[d][:, sb_ * 512:(sb_ + 1) * 512],
                        start=st__, stop=sp__))(gw, dc, sub, xsrc, st_, sp_),
                        incs=[(s2mm, 1)] if (ci == 2 and dc == 7 and sub == 3) else [])
        GATE_MM = p.n(s2mm)
        p.step("vector", lambda e: e.tensor_copy(lsb[:], LPS),
               waits=[(s2mm, GATE_MM), (s1v2, xn_done[NT - 1])], incs=[(s5v, 1)])
        LSB_C = p.n(s5v)
        if "logit" in taps:
            p.step("gpsimd", lambda e: e.dma_start(taps["logit"], lsb), waits=[(s5v, LSB_C)], incs=[(s2st, 16)])
        for tt in range(16):
            p.step("tensor", (lambda t_: lambda e: e.transpose(
                bank(0)[0:128, 0:8], lsb[0:8, t_ * 128:(t_ + 1) * 128], ident[0:8, 0:8]))(tt),
                waits=[(s5v, LSB_C + tt)], incs=[(s2mm, 1)])
            p.step("vector", (lambda t_: lambda e: e.tensor_copy(
                ltm[:, t_ * 8:(t_ + 1) * 8], bank(0)[0:128, 0:8]))(tt),
                waits=[(s2mm, p.n(s2mm))], incs=[(s5v, 1)])
        p.step("vector", lambda e: e.reduce_max(
            rmaxtm[:, 0:16], ltm.rearrange("p (t e) -> p t e", e=8), AX),
            waits=[(s5v, p.n(s5v))], incs=[(s5v, 1)])
        RMT = p.n(s5v)
        for tt in range(16):
            p.step("tensor", (lambda t_: lambda e: e.transpose(
                bank(1)[0:1, 0:128], rmaxtm[:, t_:t_ + 1], ident[:]))(tt),
                waits=[(s5v, RMT + tt)], incs=[(s2mm, 1)])
            p.step("vector", (lambda t_: lambda e: e.tensor_copy(
                rmaxrow[0:1, t_ * 128:(t_ + 1) * 128], bank(1)[0:1, 0:128]))(tt),
                waits=[(s2mm, p.n(s2mm))], incs=[(s5v, 1)])
        RMROW = p.n(s5v)
        for sub in range(4):
            p.step("tensor", (lambda sb_: lambda e: e.matmul(
                PS[0:8, sb_ * 512:(sb_ + 1) * 512], onesr32[0:1, 0:8],
                rmaxrow[0:1, sb_ * 512:(sb_ + 1) * 512], start=True, stop=True))(sub),
                waits=[(s5v, RMROW)] if sub == 0 else [], incs=[(s2mm, 1)])
            p.step("vector", (lambda sb_: lambda e: e.tensor_tensor(
                maskfm[0:8, sb_ * 512:(sb_ + 1) * 512], lsb[0:8, sb_ * 512:(sb_ + 1) * 512],
                PS[0:8, sb_ * 512:(sb_ + 1) * 512], ALU.is_ge))(sub),
                waits=[(s2mm, p.n(s2mm))], incs=[(s5m, 1)])
        MASK_DONE = p.n(s5m)
        if "mask" in taps:
            p.step("gpsimd", lambda e: e.dma_start(taps["mask"], maskfm[:]),
                   waits=[(s5m, MASK_DONE)], incs=[(s2st, 16)])

        p.step("tensor", lambda e: e.transpose(bank(2)[0:16, 0:128], stat[:, 0:16], ident[:]),
               waits=[(s2stat, LAST_STAT), (s5m, MASK_DONE)], incs=[(s2mm, 1)])
        p.step("tensor", lambda e: e.transpose(bank(2)[0:16, 128:256], stat[:, 16:32], ident[:]),
               incs=[(s2mm, 1)])
        TRC = p.n(s2mm)
        p.step("vector", lambda e: e.reduce_max(stat[0:16, 32:33], bank(2)[0:16, 0:128], AX),
               waits=[(s2mm, TRC), (s2stat, p.n(s2stat))], incs=[(s2stat, 1)])
        p.step("vector", lambda e: e.reduce_max(stat[0:16, 33:34], bank(2)[0:16, 128:256], AX),
               incs=[(s2stat, 1)])
        p.step("vector", lambda e: e.tensor_tensor(
            stat[0:16, 34:35], stat[0:16, 32:33], stat[0:16, 33:34], ALU.mult),
            waits=[(s2stat, p.n(s2stat))], incs=[(s2stat, 1)])
        p.step("scalar", lambda e: e.activation(stat[0:16, 35:36], stat[0:16, 34:35], AF.Sqrt),
               waits=[(s2stat, p.n(s2stat))], incs=[(s2stat, 1)])
        p.step("tensor", lambda e: e.transpose(bank(2)[0:1, 256:272], stat[0:16, 35:36], ident[0:16, 0:16]),
               waits=[(s2stat, p.n(s2stat))], incs=[(s2mm, 1)])
        p.step("vector", lambda e: e.reduce_max(stat[0:1, 36:37], bank(2)[0:1, 256:272], AX),
               waits=[(s2mm, p.n(s2mm)), (s2stat, p.n(s2stat))], incs=[(s2stat, 1)])
        p.step("vector", lambda e: e.tensor_scalar(
            stat[0:1, 37:38], stat[0:1, 36:37], -0.1275, None, ALU.mult),
            waits=[(s2stat, p.n(s2stat))], incs=[(s2stat, 1)])
        p.step("tensor", lambda e: e.matmul(bank(0)[:, 0:1], onesr32[:], stat[0:1, 37:38],
                                            start=True, stop=True),
               waits=[(s2stat, p.n(s2stat))], incs=[(s2mm, 1)])
        p.step("vector", lambda e: e.tensor_copy(bias_col[:], bank(0)[:, 0:1]),
               waits=[(s2mm, p.n(s2mm))], incs=[(s2stat, 1)])
        BIAS_DONE = p.n(s2stat)
        if "bias" in taps:
            p.step("gpsimd", lambda e: e.dma_start(taps["bias"], bias_col[:]),
                   waits=[(s2stat, BIAS_DONE)], incs=[(s2st, 16)])

        for pr in range(8):
            p.step("sync", (lambda pp: lambda e: e.dma_start(
                q_sb[:, pp * 2048:(pp + 1) * 2048],
                q_dram[0:TOK, pp * 128:(pp + 1) * 128], transpose=True))(pr),
                waits=[(s2st, KV_ST), (s2t, GXL)] if pr == 0 else [], incs=[(s2t, 16)])
        QSB_LD = p.n(s2t)
        p.step("gpsimd", lambda e: e.memset(vt_sb[:], 1.0),
               waits=[(s2r, LAST_ROPE), (s2stat, BIAS_DONE)], incs=[(s3v, 1)])
        VONES = p.n(s3v)

        blk_list = [(s, hg) for s in range(NSTRIP) for hg in range(NHG)]
        kch_ready = {}
        vch_ready = {}
        kch_free_w = {}
        load_order = [(s, hg, cc) for (s, hg) in blk_list for cc in range(4)]
        li = [0]

        def emit_kv_load(idx):
            s_, hg, cc = load_order[idx]
            cid = idx
            slot = cid % 4
            w = [(s2st, KV_ST), (s3v, VONES), (s2mm, GATE_MM), (s2t, QSB_LD)]
            if cid >= 4:
                w.append(kch_free_w[cid - 4])
            for pr in range(2):
                gpr = hg * 2 + pr
                p.step("sync", (lambda sl, gp, c_, pp: lambda e: e.dma_start(
                    kt_sb[:, sl * 4096 + pp * 2048:sl * 4096 + pp * 2048 + 2048],
                    k_dram[c_ * 2048:(c_ + 1) * 2048, gp * 128:(gp + 1) * 128],
                    transpose=True))(slot, gpr, cc, pr),
                    waits=w if pr == 0 else [], incs=[(s3k, 16)])
            kch_ready[(s_, hg, cc)] = p.n(s3k)
            for hx in range(4):
                wv_ = []
                if hx == 0:
                    wv_ = [(s3v, VONES)]
                    if cid >= 4:
                        wv_.append(kch_free_w[cid - 4])
                p.step("sync", (lambda sl, h_, c_, hx_: lambda e: e.dma_start(
                    vt_sb[:, 0:16896].rearrange("p (sl2 x) -> p sl2 x", sl2=4)[:, sl, 0:4160].rearrange("p (k h e) -> p k h e", k=16, h=4)[:, :, hx_, 0:64],
                    v_dram.rearrange("(ck p) (h e) -> p ck h e", p=128, h=16)[:, c_ * 16:(c_ + 1) * 16, h_ * 4 + hx_, :]))(slot, hg, cc, hx),
                    waits=wv_, incs=[(s3v, 16)])
            vch_ready[(s_, hg, cc)] = p.n(s3v)

        for _ in range(4):
            emit_kv_load(li[0]); li[0] += 1

        sc_done = {}
        ex_done = {}
        pv_done = {}
        ev_done = {}
        lb_done = {}
        ost_done = {}
        pending_bcast = [None]
        j = 0

        def emit_pv(pj):
            bi, rem = divmod(pj, NKT * 2)
            s_, hg = blk_list[bi]
            pkti, ppr = divmod(rem, 2)
            cc, kk = divmod(pkti, 16)
            cid = bi * 4 + cc
            slot = cid % 4
            pb = (pj % 8) * 1024
            first = pkti == 0
            last = pkti == NKT - 1
            w = [(s3ex, ex_done[pj])]
            if pkti == 0 and bi > 0:
                w.append((s3ev, lb_done[bi - 1][ppr]))
            if first:
                w.append((s3v, vch_ready[(s_, hg, cc)]))
            for h in range(2):
                hh = ppr * 2 + h
                vo = slot * 4224 + kk * 260 + hh * 65
                p.step("tensor", (lambda v_: lambda e: e.ldweights(vt_sb[:, v_:v_ + 128]))(vo),
                       waits=w if h == 0 else [])
                p.step("tensor", (lambda v_, hh_, pb_, f_, l_: lambda e: e.matmul(
                    PS[0:128, (4 + hh_) * 512:(4 + hh_) * 512 + 512],
                    vt_sb[:, v_:v_ + 128], p_sb[:, pb_ + (hh_ % 2) * 512:pb_ + (hh_ % 2) * 512 + 512],
                    start=f_, stop=l_))(vo, hh, pb, first, last),
                    incs=[(s3pv, 1)] if h == 1 else [])
            pv_done[pj] = p.n(s3pv)
            if kk == 15 and ppr == 1:
                kch_free_w[cid] = (s3pv, p.n(s3pv))
                if li[0] < len(load_order):
                    emit_kv_load(li[0]); li[0] += 1

        for bi, (s_, hg) in enumerate(blk_list):
            for kti in range(NKT):
                cc, kk = divmod(kti, 16)
                for pr in range(2):
                    so = (j % 2) * 1024
                    w = []
                    if kk == 0 and pr == 0:
                        w.append((s3k, kch_ready[(s_, hg, cc)]))
                    if j == 0:
                        w += [(s5m, MASK_DONE), (s2stat, BIAS_DONE)]
                    if j >= 2:
                        w.append((s3ex, ex_done[j - 2]))
                    slot = (bi * 4 + cc) % 4
                    ko = slot * 4096 + pr * 2048 + kk * 128
                    qo = (hg * 2 + pr) * 2048 + s_ * 512
                    p.step("tensor", (lambda k_: lambda e: e.ldweights(
                        kt_sb[0:64, k_:k_ + 128], tile_position=(0, 0)))(ko), waits=w)
                    p.step("tensor", (lambda k_: lambda e: e.ldweights(
                        kt_sb[64:128, k_:k_ + 128], tile_position=(64, 0)))(ko))
                    p.step("tensor", (lambda k_, q_, so_: lambda e: e.matmul(
                        PS[:, so_:so_ + 512], kt_sb[0:64, k_:k_ + 128], q_sb[0:64, q_:q_ + 512],
                        start=True, stop=True, tile_position=(0, 0)))(ko, qo, so))
                    p.step("tensor", (lambda k_, q_, so_: lambda e: e.matmul(
                        PS[:, so_ + 512:so_ + 1024], kt_sb[64:128, k_:k_ + 128], q_sb[64:128, q_:q_ + 512],
                        start=True, stop=True, tile_position=(64, 0)))(ko, qo, so),
                        incs=[(s3sc, 1)])
                    sc_done[j] = p.n(s3sc)
                    we = [(s3sc, sc_done[j])]
                    if j == 0:
                        we.append((s2stat, BIAS_DONE))
                    if j >= 8:
                        we.append((s3pv, pv_done[j - 8]))
                    p.step("scalar", (lambda so_, pb_: lambda e: e.activation(
                        p_sb[:, pb_:pb_ + 1024], PS[:, so_:so_ + 1024], AF.Exp,
                        bias=bias_col[:], scale=0.125))(so, (j % 8) * 1024),
                        waits=we, incs=[(s3ex, 1)])
                    ex_done[j] = p.n(s3ex)
                    if j % (NKT * 2) == 1 and pending_bcast[0] is not None:
                        pending_bcast[0]()
                        pending_bcast[0] = None
                    if j >= 1 and (j - 1) % (NKT * 2) != (NKT * 2 - 1):
                        emit_pv(j - 1)
                    j += 1
            emit_pv(j - 1)
            w = [(s3pv, pv_done[j - 1])]
            if bi > 0:
                w.append((s3st, ost_done[bi - 1]))
            p.step("vector", lambda e: e.tensor_copy(orawB[0:65, :], PS[0:65, 2048:4096]),
                   waits=w, incs=[(s3ev, 1)])
            LR = p.n(s3ev)
            def mk_bcast(CP_, bi_, hg_, ss_):
                def emit_bc():
                    lb_pair = []
                    for hh in range(4):
                        p.step("vector", (lambda h_: lambda e: e.reciprocal(
                            lrow[0:1, h_ * 512:h_ * 512 + 512],
                            orawB[64:65, h_ * 512:h_ * 512 + 512]))(hh),
                            waits=[(s3ev, CP_)] if hh == 0 else [], incs=[(s3ev, 1)])
                        p.step("vector", (lambda h_: lambda e: e.tensor_copy(
                            lrow16[0:1, h_ * 512:h_ * 512 + 512],
                            lrow[0:1, h_ * 512:h_ * 512 + 512]))(hh),
                            waits=[(s3ev, p.n(s3ev))], incs=[(s3ev, 1)])
                        p.step("tensor", (lambda h_: lambda e: e.matmul(
                            PS[0:64, (4 + h_) * 512:(4 + h_) * 512 + 512],
                            onesr16[0:1, 0:64], lrow16[0:1, h_ * 512:h_ * 512 + 512],
                            start=True, stop=True))(hh),
                            waits=[(s3ev, p.n(s3ev))], incs=[(s3mx, 1)])
                        p.step("vector", (lambda h_: lambda e: e.tensor_copy(
                            lb4[0:64, h_ * 512:h_ * 512 + 512],
                            PS[0:64, (4 + h_) * 512:(4 + h_) * 512 + 512]))(hh),
                            waits=[(s3mx, p.n(s3mx))], incs=[(s3ev, 1)])
                        if hh in (1, 3):
                            lb_pair.append(p.n(s3ev))
                    lb_done[bi_] = tuple(lb_pair)
                    p.step("vector", lambda e: e.tensor_tensor(
                        o_tmp[0:64, :], orawB[0:64, :], lb4[0:64, :], ALU.mult),
                        waits=[(s3ev, lb_pair[1])], incs=[(s3ev, 1)])
                    ev_done[bi_] = p.n(s3ev)
                    p.step("gpsimd", (lambda: lambda e: e.dma_start(
                        o_dram.rearrange("(hh q) t -> q hh t", q=64)[:, hg_ * 4:(hg_ + 1) * 4, ss_ * 512:(ss_ + 1) * 512],
                        o_tmp[0:64, :].rearrange("q (hh t) -> q hh t", t=512)))(),
                        waits=[(s3ev, ev_done[bi_])], incs=[(s3st, 16)])
                    ost_done[bi_] = p.n(s3st)
                return emit_bc
            pending_bcast[0] = mk_bcast(LR, bi, hg, s_)
            if bi == len(blk_list) - 1:
                pending_bcast[0]()
                pending_bcast[0] = None

        ATT_DONE = p.n(s3ev)
        EXP_ALL = p.n(s3ex)
        PV_ALL = p.n(s3pv)
        O_ST = p.n(s3st)
        if "o" in taps:
            p.step("gpsimd", lambda e: e.dma_start(taps["o"], o_dram), waits=[(s3st, O_ST)], incs=[(s2st, 16)])

        for dc in range(8):
            p.step("gpsimd", (lambda d: lambda e: e.dma_start(
                wout[:, d * 1024:(d + 1) * 1024], woutT[d * 128:(d + 1) * 128, :]))(dc),
                waits=[(s3ev, ATT_DONE), (s3pv, PV_ALL)] if dc == 0 else [], incs=[(s2st, 16)])
        for dc in range(8):
            p.step("sync", (lambda d: lambda e: e.dma_start(
                o_sb[:, d * 2048:(d + 1) * 2048], o_dram[d * 128:(d + 1) * 128, :]))(dc),
                waits=[(s3st, O_ST)] if dc == 0 else [], incs=[(s2st, 16)])
        WOUT_LD = p.n(s2st)

        s4_ev = {}
        for tt in range(16):
            slot = tt % 2
            sb4 = bank(slot * 2, 2)
            w0 = [(s2st, WOUT_LD), (s3ex, EXP_ALL)]
            if tt >= 2:
                w0.append((s4e, s4_ev[tt - 2]))
            for dc in range(8):
                p.step("tensor", (lambda d, t_: lambda e: e.ldweights(
                    o_sb[:, d * 2048 + t_ * 128:d * 2048 + t_ * 128 + 128]))(dc, tt),
                    waits=w0 if dc == 0 else [])
                for half in range(2):
                    p.step("tensor", (lambda d, h, b_, t_: lambda e: e.matmul(
                        b_[:, h * 512:(h + 1) * 512],
                        o_sb[:, d * 2048 + t_ * 128:d * 2048 + t_ * 128 + 128],
                        wout[:, d * 1024 + h * 512:d * 1024 + (h + 1) * 512],
                        start=(d == 0), stop=(d == 7)))(dc, half, sb4, tt),
                        incs=[(s4, 1)] if dc == 7 and half == 1 else [])
            wev = [(s4, p.n(s4))]
            if tt >= 2:
                wev.append((s4st, 16 * (tt - 1)))
            p.step("vector", (lambda sl, b_: lambda e: e.tensor_copy(
                A5[:, 16384 + sl * 1024:16384 + sl * 1024 + 1024], b_))(slot, sb4),
                waits=wev, incs=[(s4e, 1)])
            s4_ev[tt] = p.n(s4e)
            p.step("gpsimd", (lambda sl, t_: lambda e: e.dma_start(
                s_dram[t_ * 128:(t_ + 1) * 128, :], A5[:, 16384 + sl * 1024:16384 + sl * 1024 + 1024]))(slot, tt),
                waits=[(s4e, s4_ev[tt])], incs=[(s4st, 16), (s2st, 16)])
        S_ST = p.n(s4st)
        P4_EV = p.n(s4e)
        if "s" in taps:
            p.step("gpsimd", lambda e: e.dma_start(taps["s"], s_dram), waits=[(s4st, S_ST)], incs=[(s2st, 16)])

        for dc in range(8):
            p.step("sync", (lambda d: lambda e: e.dma_start(
                xnof[:, d * 2048:(d + 1) * 2048],
                xn_dram[0:TOK, d * 128:(d + 1) * 128], transpose=True))(dc),
                waits=[(s4e, s4_ev[15])] if dc == 0 else [], incs=[(s2t, 16)])
        XNOF_LD = p.n(s2t)

        for rc in range(16):
            p.step("gpsimd", (lambda r_: lambda e: e.dma_start(
                wd_sb[:, r_ * 1024:(r_ + 1) * 1024],
                wd_in[r_ * 128:(r_ + 1) * 128, :]))(rc),
                waits=[(s3pv, PV_ALL), (s3ev, ATT_DONE)] if rc == 0 else [], incs=[(s5w, 16)])
        WD_LD = p.n(s5w)

        wgu_ld = {}
        wgu_free = {}

        def emit_wgu_load(key):
            half, e_ = divmod(key, NE)
            slot = key % 2
            w = []
            if key == 0:
                w = [(s5w, WD_LD)]
            if key >= 2:
                w.append(wgu_free[key - 2])
            for op, win in ((0, wg_in), (1, wu_in)):
                for ch in range(8):
                    p.step("gpsimd", (lambda sl, o_, c_, wi, ee: lambda e: e.dma_start(
                        wgu[:, sl * 4096 + o_ * 2048 + c_ * 256:sl * 4096 + o_ * 2048 + (c_ + 1) * 256],
                        wi[ee * 1024 + c_ * 128:ee * 1024 + (c_ + 1) * 128, :]))(slot, op, ch, win, e_),
                        waits=(w if (op == 0 and ch == 0) else []), incs=[(s5w, 16)])
            wgu_ld[key] = p.n(s5w)

        emit_wgu_load(0)
        emit_wgu_load(1)

        gu_psum_free = {}
        ebm_done = {}
        inter_done = {}
        down_ev = {}
        oui = 0
        for half in range(2):
            hcol = half * 1024
            for e_ in range(NE):
                key = half * NE + e_
                eslot = key % 2
                gslot = oui % 2
                gb = bank(gslot * 4, 4)
                w = [(s5m, MASK_DONE), (s5w, wgu_ld[key]), (s4st, S_ST), (s2t, XNOF_LD)]
                if key == 0:
                    w += [(s3ev, ATT_DONE), (s4e, P4_EV)]
                if half == 1 and e_ == 0:
                    w.append((s5a, down_ev[(0, 7)]))
                if oui >= 2:
                    w.append(gu_psum_free[oui - 2])
                for sub in range(2):
                    p.step("tensor", (lambda sb_, b_, ee, hc: lambda e: e.matmul(
                        b_[:, sb_ * 512:(sb_ + 1) * 512], sel8[:, ee * 128:ee * 128 + 128],
                        maskfm[0:8, hc + sb_ * 512:hc + (sb_ + 1) * 512],
                        start=True, stop=True))(sub, gb, e_, hcol),
                        waits=w if sub == 0 else [], incs=[(s2mm, 1)] if sub == 1 else [])
                mb_mm = p.n(s2mm)
                p.step("vector", (lambda sl, b_: lambda e: e.tensor_copy(
                    ebm[:, sl * 2048:sl * 2048 + 1024], b_[:, 0:1024]))(eslot, gb),
                    waits=[(s2mm, mb_mm)], incs=[(s5v, 1)])
                ebm_done[key] = p.n(s5v)
                gu_mm = {}
                for op in range(2):
                    slot = oui % 2
                    ob = bank(slot * 4, 4)
                    w0 = []
                    if op == 0:
                        w0.append((s5v, ebm_done[key]))
                    else:
                        if oui >= 2:
                            w0.append(gu_psum_free[oui - 2])
                        if half == 1 and e_ == 0:
                            w0.append((s5a, down_ev[(0, 7)]))
                    for Mt in range(2):
                        for dc in range(8):
                            p.step("tensor", (lambda sl_, o_, c_, kk_: lambda e: e.ldweights(
                                wgu[:, (kk_ % 2) * 4096 + o_ * 2048 + c_ * 256 + sl_ * 128:
                                       (kk_ % 2) * 4096 + o_ * 2048 + c_ * 256 + sl_ * 128 + 128]))(Mt, op, dc, key),
                                waits=w0 if (Mt == 0 and dc == 0) else [])
                            for sub in range(2):
                                p.step("tensor", (lambda o_, c_, m_, sb_, b_, kk_, hc: lambda e: e.matmul(
                                    b_[:, m_ * 1024 + sb_ * 512:m_ * 1024 + (sb_ + 1) * 512],
                                    wgu[:, (kk_ % 2) * 4096 + o_ * 2048 + c_ * 256 + m_ * 128:
                                           (kk_ % 2) * 4096 + o_ * 2048 + c_ * 256 + m_ * 128 + 128],
                                    xnof[:, c_ * 2048 + hc + sb_ * 512:c_ * 2048 + hc + (sb_ + 1) * 512],
                                    start=(c_ == 0), stop=(c_ == 7)))(op, dc, Mt, sub, ob, key, hcol),
                                    incs=[(s5, 1)] if (Mt == 1 and dc == 7 and sub == 1) else [])
                    gu_mm[op] = p.n(s5)
                    if op == 0:
                        wr = [(s5, gu_mm[0]), (s5a, p.n(s5a))]
                        if key >= 1:
                            wr.append((s5v, inter_done[key - 1]))
                        p.step("scalar", (lambda b_: lambda e: e.activation(
                            gr_sb[:, 0:2048], b_[:, 0:2048], AF.Relu))(ob),
                            waits=wr, incs=[(s5a, 1)])
                        gu_psum_free[oui] = (s5a, p.n(s5a))
                    else:
                        for Mt in range(2):
                            p.step("vector", (lambda b_, sl, m_: lambda e: e.tensor_tensor(
                                um_sb[:, m_ * 1024:m_ * 1024 + 1024],
                                b_[:, m_ * 1024:m_ * 1024 + 1024],
                                ebm[:, sl * 2048:sl * 2048 + 1024], ALU.mult))(ob, eslot, Mt),
                                waits=([(s5, gu_mm[1]), (s5v, p.n(s5v))] if Mt == 0 else []),
                                incs=[(s5v, 1)])
                        gu_psum_free[oui] = (s5v, p.n(s5v))
                    oui += 1
                p.step("vector", lambda e: e.tensor_tensor(
                    gr2_sb[:, 0:2048], gr_sb[:, 0:2048], gr_sb[:, 0:2048], ALU.mult),
                    waits=[(s5a, p.n(s5a)), (s5v, p.n(s5v))], incs=[(s5v, 1)])
                p.step("vector", (lambda ee: lambda e: e.tensor_tensor(
                    inter[:, ee * 2048:ee * 2048 + 2048], gr2_sb[:, 0:2048], um_sb[:, 0:2048], ALU.mult))(e_),
                    waits=[(s5v, p.n(s5v))], incs=[(s5v, 1)])
                inter_done[key] = p.n(s5v)
                wgu_free[key] = (s5, gu_mm[1])
                nk = key + 2
                if nk < 2 * NE:
                    emit_wgu_load(nk)
            for tt in range(8):
                slot = tt % 2
                db = bank(slot * 2, 2)
                w0 = [(s5v, inter_done[half * NE + NE - 1]), (s5w, WD_LD)]
                if oui >= 2:
                    w0.append(gu_psum_free[oui - 2])
                    w0.append(gu_psum_free[oui - 1])
                if tt >= 2:
                    w0.append((s5a, down_ev[(half, tt - 2)]))
                for rc in range(16):
                    p.step("tensor", (lambda r_, t_: lambda e: e.ldweights(
                        inter[:, r_ * 1024 + t_ * 128:r_ * 1024 + t_ * 128 + 128]))(rc, tt),
                        waits=w0 if rc == 0 else [])
                    for hf in range(2):
                        p.step("tensor", (lambda r_, t_, h_, b_: lambda e: e.matmul(
                            b_[:, h_ * 512:(h_ + 1) * 512],
                            inter[:, r_ * 1024 + t_ * 128:r_ * 1024 + t_ * 128 + 128],
                            wd_sb[:, r_ * 1024 + h_ * 512:r_ * 1024 + (h_ + 1) * 512],
                            start=(r_ == 0), stop=(r_ == 15)))(rc, tt, hf, db),
                            incs=[(s5, 1)] if rc == 15 and hf == 1 else [])
                p.step("scalar", (lambda b_, t_, hf_: lambda e: e.activation(
                    gsil[:, (hf_ * 8 + t_) * 1024:(hf_ * 8 + t_) * 1024 + 1024],
                    b_[:, 0:1024], AF.Silu))(db, tt, half),
                    waits=[(s5, p.n(s5)), (s5a, p.n(s5a))] +
                          ([(s4e, P4_EV), (s2st, WOUT_LD)] if (half == 0 and tt == 0) else []),
                    incs=[(s5a, 1)])
                down_ev[(half, tt)] = p.n(s5a)
        FFN_DONE = p.n(s5a)
        if "g" in taps:
            p.step("gpsimd", lambda e: e.dma_start(taps["g"], gsil), waits=[(s5a, FFN_DONE)], incs=[(s2st, 16)])

        out_done = {}
        for tt in range(16):
            slot = tt % 2
            wx = [(s6v, out_done[tt - 2])] if tt >= 2 else [(s4st, S_ST)]
            p.step("sync", (lambda sl, t_: lambda e: e.dma_start(
                x6[:, sl * 1024:sl * 1024 + 1024], xbm[t_ * 128:(t_ + 1) * 128, :]))(slot, tt),
                waits=wx, incs=[(s6x, 16)])
            ws = [(s4st, 16 * (tt + 1)), (s5a, FFN_DONE)]
            if tt >= 2:
                ws.append((s6v, out_done[tt - 2]))
            p.step("sync", (lambda sl, t_: lambda e: e.dma_start(
                sload[:, sl * 1024:sl * 1024 + 1024], s_dram[t_ * 128:(t_ + 1) * 128, :]))(slot, tt),
                waits=ws, incs=[(s6s, 16)])
            p.step("vector", (lambda sl, gc: lambda e: e.tensor_tensor(
                o6[:, sl * 1024:sl * 1024 + 1024], sload[:, sl * 1024:sl * 1024 + 1024],
                gsil[:, gc:gc + 1024], ALU.mult))(slot, tt * 1024),
                waits=[(s6s, 16 * (tt + 1)), (s5a, FFN_DONE)] +
                      ([(s6st, 16 * (tt - 1))] if tt >= 2 else []),
                incs=[(s6v, 1)])
            p.step("vector", (lambda sl: lambda e: e.tensor_tensor(
                o6[:, sl * 1024:sl * 1024 + 1024], o6[:, sl * 1024:sl * 1024 + 1024],
                x6[:, sl * 1024:sl * 1024 + 1024], ALU.add))(slot),
                waits=[(s6x, 16 * (tt + 1)), (s6v, p.n(s6v))], incs=[(s6v, 1)])
            out_done[tt] = p.n(s6v)
            p.step("gpsimd", (lambda sl, t_: lambda e: e.dma_start(
                out[t_ * 128:(t_ + 1) * 128, :], o6[:, sl * 1024:sl * 1024 + 1024]))(slot, tt),
                waits=[(s6v, out_done[tt])], incs=[(s6st, 16)])
        p.step("gpsimd", lambda e: e.nop(), waits=[(s6st, 16 * 16)])

        p.check_deadlock()

        semh = {}
        for name in p.semnames:
            semh[name] = es.enter_context(nc.semaphore(name))
        block = es.enter_context(nc.Block())

        @block.tensor
        def _(t):
            p.emit("tensor", t, semh)

        @block.vector
        def _(v):
            p.emit("vector", v, semh)

        @block.scalar
        def _(s):
            p.emit("scalar", s, semh)

        @block.gpsimd
        def _(g):
            p.emit("gpsimd", g, semh)

        @block.sync
        def _(sy):
            p.emit("sync", sy, semh)

    return nc


_ROPE_BASE = 10000.0


def _host_tables(off):
    half = 32
    inv = 1.0 / _ROPE_BASE ** (np.arange(half, dtype=np.float64) / half)
    pos = np.arange(L, dtype=np.float64)
    pos = np.concatenate([pos[off:], pos[:off]])
    ang = pos[:, None] * inv[None, :]
    c = np.cos(ang).astype(np.float32)
    s = np.sin(ang).astype(np.float32)
    cost = np.concatenate([c, c], axis=1)
    sint = np.concatenate([-s, s], axis=1)
    return np.ascontiguousarray(cost), np.ascontiguousarray(sint)


def _host_consts():
    sel8 = np.zeros((8, 1024), np.float32)
    for e in range(8):
        sel8[e, e * 128:(e + 1) * 128] = 1
    ident = np.eye(128, dtype=np.float32)
    onesr = np.ones((1, 128), np.float32)
    return sel8, ident, onesr


_NC_CACHE = {}


def kernel(L0_attn_x, L0_attn_w_qkv, L0_attn_w_out, L0_ffn_gate_w,
           L0_ffn_wg, L0_ffn_wu, L0_ffn_wd, _debug_taps=(), _trace=False):
    x = np.asarray(L0_attn_x, dtype=np.float32)
    w_qkv = np.asarray(L0_attn_w_qkv, dtype=np.float32)
    w_out = np.asarray(L0_attn_w_out, dtype=np.float32)
    gate_w = np.asarray(L0_ffn_gate_w, dtype=np.float32)
    wg = np.asarray(L0_ffn_wg, dtype=np.float32)
    wu = np.asarray(L0_ffn_wu, dtype=np.float32)
    wd = np.asarray(L0_ffn_wd, dtype=np.float32)

    wkvT = np.ascontiguousarray(w_qkv[D:3 * D].T)
    wqT = np.ascontiguousarray(w_qkv[0:D].T)
    gateT = np.ascontiguousarray(gate_w.T)
    wg_f = np.ascontiguousarray(wg.reshape(NE * D, DFF))
    wu_f = np.ascontiguousarray(wu.reshape(NE * D, DFF))
    wd_f = np.ascontiguousarray(wd.reshape(NE * DFF, D))
    woutT_ = np.ascontiguousarray(w_out.T)
    sel8, ident, onesr = _host_consts()

    key = tuple(sorted(_debug_taps))
    if key not in _NC_CACHE:
        _NC_CACHE[key] = build_nc(_debug_taps)
    nc = _NC_CACHE[key]

    in_maps = []
    for c in range(8):
        b = c // 4
        off = 2048 * (c % 4)
        xb_ = np.ascontiguousarray(np.concatenate([x[b, off:], x[b, :off]], axis=0))
        cost, sint = _host_tables(off)
        in_maps.append({
            "xb": xb_, "wkvT": wkvT, "wqT": wqT, "gateT": gateT,
            "wg": wg_f, "wu": wu_f, "wd": wd_f, "woutT": woutT_,
            "costab": cost, "sintab": sint, "sel8_in": sel8,
            "ident_in": ident, "onesr": onesr,
        })
    res = run_bass_kernel_spmd(nc, in_maps, core_ids=list(range(8)), trace=_trace)
    outp = np.empty((B, L, D), np.float32)
    for c in range(8):
        b = c // 4
        off = 2048 * (c % 4)
        outp[b, off:off + 2048] = res.results[c]["out"]
    if _debug_taps:
        return outp, res
    return outp


# version 05
# speedup vs baseline: 1.2404x; 1.0046x over previous
import numpy as np
import ml_dtypes
import concourse.bass as bass
import concourse.mybir as mybir
from concourse.bass_utils import run_bass_kernel_spmd
from contextlib import ExitStack

F32 = mybir.dt.float32
BF16 = mybir.dt.bfloat16
AF = mybir.ActivationFunctionType
ALU = mybir.AluOpType
AX = mybir.AxisListType.X

B, L, D = 2, 8192, 1024
NPAIR = 8
TOK = 2048
NT = L // 128
OWN_T = TOK // 128
NG = 8
GTOK = 1024
NKT = L // 128
NSTRIP = TOK // 512
NHG = 4
NE, DFF = 8, 256

ENGS = ("tensor", "vector", "scalar", "gpsimd", "sync")


class Prog:
    def __init__(self):
        self.steps = {e: [] for e in ENGS}
        self.semnames = []
        self.counts = {}

    def sem(self, name):
        self.semnames.append(name)
        self.counts[name] = 0
        return name

    def n(self, s):
        return self.counts[s]

    def step(self, eng, fn, waits=(), incs=()):
        waits = [(s, v) for (s, v) in waits if v > 0]
        for s, k in incs:
            self.counts[s] += k
        self.steps[eng].append((fn, waits, list(incs)))

    def check_deadlock(self):
        idx = {e: 0 for e in ENGS}
        cnt = {s: 0 for s in self.semnames}
        progress = True
        while progress:
            progress = False
            for e in ENGS:
                while idx[e] < len(self.steps[e]):
                    fn, waits, incs = self.steps[e][idx[e]]
                    if all(cnt[s] >= v for s, v in waits):
                        for s, k in incs:
                            cnt[s] += k
                        idx[e] += 1
                        progress = True
                    else:
                        break
        stuck = {e: (idx[e], len(self.steps[e])) for e in ENGS if idx[e] < len(self.steps[e])}
        if stuck:
            detail = {}
            for e in stuck:
                fn, waits, incs = self.steps[e][idx[e]]
                detail[e] = [(s, v, cnt[s]) for s, v in waits if cnt[s] < v]
            raise RuntimeError(f"schedule deadlock: {stuck}; blocked: {detail}")

    def emit(self, eng_name, eng, semh):
        for fn, waits, incs in self.steps[eng_name]:
            for s, v in waits:
                eng.wait_ge(semh[s], v)
            ins = fn(eng)
            for s, k in incs:
                ins = ins.then_inc(semh[s], k)


def build_nc(debug_taps=()):
    nc = bass.Bass("TRN2", target_bir_lowering=False, debug=False)
    p = Prog()

    xbm = nc.dram_tensor("xb", [L, D], F32, kind="ExternalInput").ap()
    wkvT = nc.dram_tensor("wkvT", [D, 2048], F32, kind="ExternalInput").ap()
    wqT = nc.dram_tensor("wqT", [D, D], F32, kind="ExternalInput").ap()
    gateT = nc.dram_tensor("gateT", [D, 8], F32, kind="ExternalInput").ap()
    wg_in = nc.dram_tensor("wg", [NE * D, DFF], F32, kind="ExternalInput").ap()
    wu_in = nc.dram_tensor("wu", [NE * D, DFF], F32, kind="ExternalInput").ap()
    wd_in = nc.dram_tensor("wd", [NE * DFF, D], F32, kind="ExternalInput").ap()
    woutT = nc.dram_tensor("woutT", [D, D], F32, kind="ExternalInput").ap()
    costab = nc.dram_tensor("costab", [L, 64], F32, kind="ExternalInput").ap()
    sintab = nc.dram_tensor("sintab", [L, 64], F32, kind="ExternalInput").ap()
    sel8_in = nc.dram_tensor("sel8_in", [8, 1024], F32, kind="ExternalInput").ap()
    ident_in = nc.dram_tensor("ident_in", [128, 128], F32, kind="ExternalInput").ap()
    onesr_in = nc.dram_tensor("onesr", [1, 128], F32, kind="ExternalInput").ap()

    out = nc.dram_tensor("out", [TOK, D], F32, kind="ExternalOutput").ap()

    xn_dram = nc.dram_tensor("xn_dram", [L, D], BF16, kind="Internal").ap()
    xnlo_dram = nc.dram_tensor("xnlo_dram", [TOK, D], BF16, kind="Internal").ap()
    k_dram = nc.dram_tensor("k_dram", [L, D], BF16, kind="Internal").ap()
    q_dram = nc.dram_tensor("q_dram", [TOK, D], BF16, kind="Internal").ap()
    v_dram = nc.dram_tensor("v_dram", [L, D], BF16, kind="Internal").ap()
    o_dram = nc.dram_tensor("o_dram", [D, TOK], BF16, kind="Internal").ap()
    s_dram = nc.dram_tensor("s_dram", [TOK, D], BF16, kind="Internal").ap()

    taps = {}
    for name, shape, dt in [("q", [TOK, D], BF16), ("k", [L, D], BF16), ("v", [L, D], BF16),
                            ("o", [D, TOK], BF16), ("l", [1, NHG * NSTRIP * 2048], F32),
                            ("mask", [8, 2048], BF16), ("logit", [8, 2048], F32),
                            ("xn", [L, D], BF16), ("bias", [128, 1], F32),
                            ("s", [TOK, D], BF16), ("g", [128, 16384], BF16)]:
        if name in debug_taps:
            taps[name] = nc.dram_tensor("tap_" + name, shape, dt, kind="ExternalOutput").ap()

    es = ExitStack()
    with es:
        PS = es.enter_context(nc.psum_tensor([128, 4096], F32))

        def bank(i, w=1):
            return PS[:, i * 512:(i + w) * 512]

        A1 = es.enter_context(nc.sbuf_tensor("A1", [128, 16384], BF16, is_transpose_dest=True))
        A2 = es.enter_context(nc.sbuf_tensor("A2", [128, 16384], BF16, is_transpose_dest=True))
        A3 = es.enter_context(nc.sbuf_tensor("A3", [128, 16896], BF16, is_transpose_dest=True))
        A5 = es.enter_context(nc.sbuf_tensor("A5", [128, 18432], BF16, is_transpose_dest=True))
        A6 = es.enter_context(nc.sbuf_tensor("A6", [128, 16384], BF16))
        XNB = es.enter_context(nc.sbuf_tensor("XNB", [128, 4096], BF16))
        B1 = es.enter_context(nc.sbuf_tensor("B1", [128, 6144], F32))

        bn6 = es.enter_context(nc.sbuf_tensor("bn6", [128, 12], F32))
        mv = es.enter_context(nc.sbuf_tensor("mv", [128, 16], F32))
        rstd = es.enter_context(nc.sbuf_tensor("rstd", [128, 8], F32))
        nmur = es.enter_context(nc.sbuf_tensor("nmur", [128, 8], F32))
        stat = es.enter_context(nc.sbuf_tensor("stat", [128, 64], F32))
        bias_col = es.enter_context(nc.sbuf_tensor("bias_col", [128, 1], F32))
        eps_col = es.enter_context(nc.sbuf_tensor("eps_col", [128, 1], F32))
        ident = es.enter_context(nc.sbuf_tensor("ident", [128, 128], F32))
        onesr32 = es.enter_context(nc.sbuf_tensor("onesr32", [1, 128], F32))
        onesr16 = es.enter_context(nc.sbuf_tensor("onesr16", [1, 128], BF16))
        sel8 = es.enter_context(nc.sbuf_tensor("sel8", [8, 1024], BF16))
        gate32 = es.enter_context(nc.sbuf_tensor("gate32", [128, 64], F32))
        gatehi = es.enter_context(nc.sbuf_tensor("gatehi", [128, 64], BF16))
        gatelo = es.enter_context(nc.sbuf_tensor("gatelo", [128, 64], BF16))
        ltm = es.enter_context(nc.sbuf_tensor("ltm", [128, 128], F32))
        rmaxtm = es.enter_context(nc.sbuf_tensor("rmaxtm", [128, 16], F32))
        xnlo_b = es.enter_context(nc.sbuf_tensor("xnlo_b", [128, 1024], BF16))

        def xnf(slot, dc):
            return A1[:, slot * 8192 + dc * 1024:slot * 8192 + (dc + 1) * 1024]
        wkv = A2
        wq_sb = A3[:, 8192:16384]
        cos_sb = A3[:, 0:4096]
        sin_sb = A3[:, 4096:8192]
        k_ev = A6[:, 0:2048]
        v_ev = A6[:, 2048:4096]
        q_ev = A6[:, 4096:6144]
        ksq = A6[:, 6144:7168]
        kt_sb = A2
        vt_sb = A3
        q_sb = A5[:, 0:16384]
        p_sb = A6[:, 0:8192]
        xin = B1[:, 0:4096]
        xn32 = B1[:, 5120:6144]
        kr32q = B1[:, 4096:5120]
        orawB = B1[:, 4096:6144]
        lrow = B1[0:1, 0:2048]
        lsb = B1[0:8, 0:2048]
        rmaxrow = B1[0:1, 2048:4096]
        lrow16 = A1[0:1, 0:2048]
        lb4 = A1[0:64, 2048:4096]
        o_tmp = A1[0:64, 4096:6144]
        wout = A3[:, 0:8192]
        o_sb = A6
        inter = A1
        xnof = A3[:, 0:16384]
        gsil = A6[:, 0:16384]
        wgu = A5[:, 0:8192]
        ebm = A5[:, 8192:12288]
        gr_sb = A5[:, 12288:14336]
        gr2_sb = A5[:, 14336:16384]
        um_sb = A5[:, 16384:18432]
        wd_sb = A2
        maskfm = XNB[0:8, 0:2048]
        sload = A5[:, 0:2048]
        x6 = B1[:, 0:2048]
        o6 = B1[:, 2048:4096]

        s1x = p.sem("s1x"); s1v1 = p.sem("s1v1"); s1a = p.sem("s1a")
        s1v2 = p.sem("s1v2"); s1st = p.sem("s1st"); s1f = p.sem("s1f")
        s2w = p.sem("s2w"); s2t = p.sem("s2t"); s2mm = p.sem("s2mm")
        s2r = p.sem("s2r"); s2st = p.sem("s2st"); s2stat = p.sem("s2stat")
        s2a = p.sem("s2a"); s5v = p.sem("s5v"); s5m = p.sem("s5m")
        s3k = p.sem("s3k"); s3v = p.sem("s3v"); s3sc = p.sem("s3sc")
        s3ex = p.sem("s3ex"); s3pv = p.sem("s3pv"); s3ev = p.sem("s3ev")
        s3mx = p.sem("s3mx"); s3st = p.sem("s3st")
        s4 = p.sem("s4"); s4e = p.sem("s4e"); s4st = p.sem("s4st")
        s5 = p.sem("s5"); s5w = p.sem("s5w"); s5a = p.sem("s5a")
        s6x = p.sem("s6x"); s6s = p.sem("s6s"); s6v = p.sem("s6v"); s6st = p.sem("s6st")

        def dma(dst, src):
            return lambda e: e.dma_start(dst, src)

        for d_, s_ in [(ident[:], ident_in), (onesr32[:], onesr_in), (onesr16[:], onesr_in),
                       (sel8[:], sel8_in)]:
            p.step("gpsimd", dma(d_, s_), incs=[(s2w, 16)])
        p.step("gpsimd", dma(cos_sb[:], costab.rearrange("(t p) d -> p t d", p=128)), incs=[(s2w, 16)])
        p.step("gpsimd", dma(sin_sb[:], sintab.rearrange("(t p) d -> p t d", p=128)), incs=[(s2w, 16)])
        for dc in range(8):
            p.step("gpsimd", dma(wkv[:, dc * 2048:(dc + 1) * 2048],
                                 wkvT[dc * 128:(dc + 1) * 128, :]), incs=[(s2w, 16)])
        for dc in range(8):
            p.step("gpsimd", dma(wq_sb[:, dc * 1024:(dc + 1) * 1024],
                                 wqT[dc * 128:(dc + 1) * 128, :]), incs=[(s2w, 16)])
        for dc in range(8):
            p.step("gpsimd", dma(gate32[:, dc * 8:(dc + 1) * 8],
                                 gateT[dc * 128:(dc + 1) * 128, :]), incs=[(s2w, 16)])
        W_LOADS = p.n(s2w)
        p.step("vector", lambda e: e.tensor_copy(gatehi[:], gate32[:]),
               waits=[(s2w, W_LOADS)], incs=[(s2stat, 1)])
        p.step("vector", lambda e: e.tensor_tensor(gatelo[:], gate32[:], gatehi[:], ALU.subtract),
               waits=[(s2stat, 1)], incs=[(s2stat, 1)])
        p.step("vector", lambda e: e.memset(stat[:], 0.0), incs=[(s2stat, 1)])
        p.step("vector", lambda e: e.memset(eps_col[:], 1e-5), incs=[(s1a, 1)])

        xnf_loaded = {}
        xnf_free_w = {}

        def emit_xnf_load(g):
            slot = g % 2
            w = [(s1st, store_done[8 * g + 7])]
            if g >= 2:
                w.append(xnf_free_w[g - 2])
            for dc in range(8):
                p.step("sync", (lambda sl, gg, d: lambda e: e.dma_start(
                    xnf(sl, d), xn_dram[gg * GTOK:(gg + 1) * GTOK, d * 128:(d + 1) * 128],
                    transpose=True))(slot, g, dc),
                    waits=w if dc == 0 else [], incs=[(s2t, 16)])
            xnf_loaded[g] = p.n(s2t)

        store_done = {}
        xn_done = {}
        act_done = {}
        sqrt_done = {}
        lo_done = {}
        NSKEW = 2

        def p1_A(t):
            xs = (t % 4) * 1024
            w = [(s1a, act_done[t - 4])] if t >= 4 else []
            p.step("sync", (lambda xs_, t_: lambda e: e.dma_start(
                xin[:, xs_:xs_ + 1024], xbm[t_ * 128:(t_ + 1) * 128, :]))(xs, t),
                waits=w, incs=[(s1x, 16)])

        def p1_B(t):
            xs = (t % 4) * 1024
            ms = t % 8
            wb = [(s1x, 16 * (t + 1))]
            if t:
                wb.append((s1v1, 3 * t))
            if t >= 8:
                wb.append((s1a, act_done[t - 8]))
            p.step("vector", (lambda xs_: lambda e: e.bn_stats(bn6[:, 0:6], xin[:, xs_:xs_ + 512]))(xs),
                   waits=wb, incs=[(s1v1, 1)])
            p.step("vector", (lambda xs_: lambda e: e.bn_stats(bn6[:, 6:12], xin[:, xs_ + 512:xs_ + 1024]))(xs),
                   incs=[(s1v1, 1)])
            p.step("vector", (lambda ms_: lambda e: e.bn_aggr(mv[:, ms_ * 2:ms_ * 2 + 2], bn6[:]))(ms),
                   waits=[(s1v1, 3 * t + 2)], incs=[(s1v1, 1)])

        def p1_C(t):
            ms = t % 8
            w = [(s1v1, 3 * (t + 1)), (s1a, 1)]
            if t >= 8:
                w.append((s1v2, xn_done[t - 8]))
            p.step("scalar", (lambda ms_: lambda e: e.activation(
                rstd[:, ms_:ms_ + 1], mv[:, ms_ * 2 + 1:ms_ * 2 + 2], AF.Sqrt, bias=eps_col[:]))(ms),
                waits=w, incs=[(s1a, 1)])
            sqrt_done[t] = p.n(s1a)

        def p1_D(t):
            ms = t % 8
            p.step("vector", (lambda ms_: lambda e: e.reciprocal(rstd[:, ms_:ms_ + 1], rstd[:, ms_:ms_ + 1]))(ms),
                   waits=[(s1a, sqrt_done[t])], incs=[(s1v2, 1)])
            p.step("vector", (lambda ms_: lambda e: e.scalar_tensor_tensor(
                nmur[:, ms_:ms_ + 1], mv[:, ms_ * 2:ms_ * 2 + 1], -1.0, rstd[:, ms_:ms_ + 1],
                ALU.mult, ALU.mult))(ms),
                waits=[(s1v2, p.n(s1v2))], incs=[(s1v2, 1)])
            xn_done[t] = p.n(s1v2)

        def p1_E(t):
            own = t < OWN_T
            xs = (t % 4) * 1024
            ms = t % 8
            ns = (t % 4) * 1024
            wact = [(s1v2, xn_done[t])]
            if t >= 4:
                wact.append((s1st, store_done[t - 4]))
            p.step("scalar", (lambda xs_, ns_, ms_: lambda e: e.activation(
                XNB[:, ns_:ns_ + 1024], xin[:, xs_:xs_ + 1024], AF.Identity,
                bias=nmur[:, ms_:ms_ + 1], scale=rstd[:, ms_:ms_ + 1]))(xs, ns, ms),
                waits=wact, incs=[(s1a, 1)])
            if own:
                p.step("scalar", (lambda xs_, ms_: lambda e: e.activation(
                    xn32[:, 0:1024], xin[:, xs_:xs_ + 1024], AF.Identity,
                    bias=nmur[:, ms_:ms_ + 1], scale=rstd[:, ms_:ms_ + 1]))(xs, ms),
                    waits=([(s1f, lo_done[t - 1])] if t >= 1 else []),
                    incs=[(s1a, 1)])
            act_done[t] = p.n(s1a)

        def p1_F(t):
            if t < OWN_T:
                ns = (t % 4) * 1024
                p.step("gpsimd", (lambda ns_: lambda e: e.tensor_tensor(
                    xnlo_b[:, 0:1024], xn32[:, 0:1024], XNB[:, ns_:ns_ + 1024], ALU.subtract))(ns),
                    waits=[(s1a, act_done[t]), (s1st, store_done.get(t - 1, 0))],
                    incs=[(s1f, 1)])
            lo_done[t] = p.n(s1f)

        def p1_G(t):
            ns = (t % 4) * 1024
            own = t < OWN_T
            wst = [(s1a, act_done[t])]
            if own:
                wst.append((s1f, lo_done[t]))
            p.step("gpsimd", (lambda ns_, t_: lambda e: e.dma_start(
                xn_dram[t_ * 128:(t_ + 1) * 128, :], XNB[:, ns_:ns_ + 1024]))(ns, t),
                waits=wst, incs=[(s1st, 16)])
            if own:
                p.step("gpsimd", (lambda t_: lambda e: e.dma_start(
                    xnlo_dram[t_ * 128:(t_ + 1) * 128, :], xnlo_b[:, 0:1024]))(t),
                    incs=[(s1st, 16)])
            store_done[t] = p.n(s1st)

        for t in range(NT + NSKEW):
            if t < NT:
                p1_A(t)
                p1_B(t)
                p1_C(t)
                p1_D(t)
            tp_ = t - NSKEW
            if 0 <= tp_ < NT:
                p1_E(tp_)
                p1_F(tp_)
                p1_G(tp_)
                if tp_ == 7:
                    emit_xnf_load(0)
                if tp_ == 15:
                    emit_xnf_load(1)


        def emit_rope_tm(src_ps, ti, dst, sem, extra_w=()):
            c3 = cos_sb[:, ti * 64:(ti + 1) * 64].rearrange(
                "p (o x) -> p o x", o=1).broadcast_to([128, 16, 64])
            s3 = sin_sb[:, ti * 64:(ti + 1) * 64].rearrange(
                "p (o x) -> p o x", o=1).broadcast_to([128, 16, 64])
            ps3 = src_ps.rearrange("p (h x) -> p h x", x=64)
            d3 = dst.rearrange("p (h x) -> p h x", x=64)
            sc3 = kr32q.rearrange("p (h x) -> p h x", x=64)
            p.step("vector", (lambda a_, b_: lambda e: e.tensor_tensor(sc3, a_, b_, ALU.mult))(ps3, c3),
                   waits=list(extra_w), incs=[(sem, 1)])
            p.step("vector", (lambda a_, b_, d_: lambda e: e.tensor_tensor(
                d_[:, :, 0:32], a_[:, :, 32:64], b_[:, :, 0:32], ALU.mult))(ps3, s3, d3),
                incs=[(sem, 1)])
            p.step("vector", (lambda a_, b_, d_: lambda e: e.tensor_tensor(
                d_[:, :, 32:64], a_[:, :, 0:32], b_[:, :, 32:64], ALU.mult))(ps3, s3, d3),
                incs=[(sem, 1)])
            p.step("vector", (lambda d_: lambda e: e.tensor_tensor(d_, d_, kr32q, ALU.add))(dst),
                   waits=[(sem, p.n(sem))], incs=[(sem, 1)])

        def emit_stats(dst, acc_col, sem):
            p.step("gpsimd", (lambda d_: lambda e: e.tensor_tensor(ksq, d_, d_, ALU.mult))(dst),
                   waits=[(sem, p.n(sem)), (s2stat, p.n(s2stat))], incs=[(s2stat, 1)])
            p.step("vector", lambda e: e.reduce_sum(
                stat[:, 48:64], ksq.rearrange("p (h x) -> p h x", x=64), AX),
                waits=[(s2stat, p.n(s2stat))], incs=[(s2stat, 1)])
            p.step("vector", (lambda c: lambda e: e.tensor_tensor(
                stat[:, c:c + 16], stat[:, c:c + 16], stat[:, 48:64], ALU.max))(acc_col),
                waits=[(s2stat, p.n(s2stat))], incs=[(s2stat, 1)])

        kst_done = {}
        vst_done = {}
        qst_done = {}
        proj_wait_extra = {}

        def emit_proj_tile(ti, qpass):
            g, tt = divmod(ti, 8)
            slot = ti % 2
            pb = bank(slot * 4, 4) if not qpass else bank(slot * 2, 2)
            w0 = [(s2t, xnf_loaded[g]), (s2w, W_LOADS)]
            key = ("q" if qpass else "kv", ti)
            w0 += proj_wait_extra.get(key, [])
            nmm = 2 if qpass else 4
            wmat = wq_sb if qpass else wkv
            wwid = 1024 if qpass else 2048
            for dc in range(8):
                p.step("tensor", (lambda sl, d, t_: lambda e: e.ldweights(
                    xnf(sl, d)[:, t_ * 128:t_ * 128 + 128]))(g % 2, dc, tt),
                    waits=w0 if dc == 0 else [])
                for sub in range(nmm):
                    p.step("tensor", (lambda sl, d, t_, sb_, b_, wm, wd_: lambda e: e.matmul(
                        b_[:, sb_ * 512:(sb_ + 1) * 512],
                        xnf(sl, d)[:, t_ * 128:t_ * 128 + 128],
                        wm[:, d * wd_ + sb_ * 512:d * wd_ + (sb_ + 1) * 512],
                        start=(d == 0), stop=(d == 7)))(g % 2, dc, tt, sub, pb, wmat, wwid),
                        incs=[(s2mm, 1)] if (dc == 7 and sub == nmm - 1) else [])
            mm_done = p.n(s2mm)
            if qpass:
                dst = q_ev[:, slot * 1024:slot * 1024 + 1024]
                ew = [(s2mm, mm_done)]
                if ti >= 2:
                    ew.append((s2st, qst_done.get(ti - 2, 0)))
                emit_rope_tm(pb[:, 0:1024], ti, dst, s2r, ew)
                emit_stats(dst, 16, s2r)
                p.step("gpsimd", (lambda d_, t_: lambda e: e.dma_start(
                    q_dram[t_ * 128:(t_ + 1) * 128, :], d_))(dst, ti),
                    waits=[(s2r, p.n(s2r)), (s2stat, p.n(s2stat))], incs=[(s2st, 16)])
                qst_done[ti] = p.n(s2st)
            else:
                kdst = k_ev[:, slot * 1024:slot * 1024 + 1024]
                ew = [(s2mm, mm_done)]
                if ti >= 2:
                    ew.append((s2st, kst_done.get(ti - 2, 0)))
                emit_rope_tm(pb[:, 0:1024], ti, kdst, s2r, ew)
                emit_stats(kdst, 0, s2r)
                p.step("gpsimd", (lambda d_, t_: lambda e: e.dma_start(
                    k_dram[t_ * 128:(t_ + 1) * 128, :], d_))(kdst, ti),
                    waits=[(s2r, p.n(s2r)), (s2stat, p.n(s2stat))], incs=[(s2st, 16)])
                kst_done[ti] = p.n(s2st)
                vdst = v_ev[:, slot * 1024:slot * 1024 + 1024]
                wv_ = [(s2mm, mm_done), (s2a, p.n(s2a))]
                if ti >= 2:
                    wv_.append((s2st, vst_done.get(ti - 2, 0)))
                p.step("scalar", (lambda b_, d_: lambda e: e.copy(d_, b_[:, 1024:2048]))(pb, vdst),
                       waits=wv_, incs=[(s2a, 1)])
                p.step("gpsimd", (lambda d_, t_: lambda e: e.dma_start(
                    v_dram[t_ * 128:(t_ + 1) * 128, :], d_))(vdst, ti),
                    waits=[(s2a, p.n(s2a))], incs=[(s2st, 16)])
                vst_done[ti] = p.n(s2st)

        rope_after = {}
        vact_after = {}
        q_rope_end = 0
        for ti in range(NT):
            if ti >= 2:
                rwait = rope_after[("kv", ti - 2)]
                if ti in (16, 17):
                    rwait = max(rwait, q_rope_end)
                proj_wait_extra[("kv", ti)] = [(s2r, rwait), (s2a, vact_after[ti - 2])]
            emit_proj_tile(ti, False)
            rope_after[("kv", ti)] = p.n(s2r)
            vact_after[ti] = p.n(s2a)
            if ti == 15:
                for qi in range(OWN_T):
                    if qi >= 2:
                        proj_wait_extra[("q", qi)] = [(s2r, rope_after[("q", qi - 2)])]
                    else:
                        proj_wait_extra[("q", qi)] = [(s2r, rope_after[("kv", 15)]),
                                                      (s2a, vact_after[15])]
                    emit_proj_tile(qi, True)
                    rope_after[("q", qi)] = p.n(s2r)
                q_rope_end = p.n(s2r)
                xnf_free_w[0] = (s2mm, p.n(s2mm))
                xnf_free_w[1] = (s2mm, p.n(s2mm))
                emit_xnf_load(2)
                emit_xnf_load(3)
            if ti >= 16 and ti % 8 == 7:
                g = ti // 8
                xnf_free_w[g] = (s2mm, p.n(s2mm))
                if g + 2 < NG:
                    emit_xnf_load(g + 2)
        LAST_ROPE = p.n(s2r)
        LAST_STAT = p.n(s2stat)
        LAST_MM = p.n(s2mm)
        LAST_VA = p.n(s2a)
        KV_ST = p.n(s2st)

        if "k" in taps:
            p.step("gpsimd", lambda e: e.dma_start(taps["k"], k_dram), waits=[(s2st, KV_ST)], incs=[(s2st, 16)])
        if "v" in taps:
            p.step("gpsimd", lambda e: e.dma_start(taps["v"], v_dram), waits=[(s2st, KV_ST)], incs=[(s2st, 16)])
        if "q" in taps:
            p.step("gpsimd", lambda e: e.dma_start(taps["q"], q_dram), waits=[(s2st, KV_ST)], incs=[(s2st, 16)])
        if "xn" in taps:
            p.step("gpsimd", lambda e: e.dma_start(taps["xn"], xn_dram), waits=[(s1st, store_done[NT - 1])], incs=[(s2st, 16)])

        XHI = [A2[:, dc * 2048:(dc + 1) * 2048] for dc in range(8)]
        XLO = [A1[:, dc * 2048:(dc + 1) * 2048] for dc in range(8)]
        for dc in range(8):
            p.step("sync", (lambda d: lambda e: e.dma_start(
                XHI[d], xn_dram[0:TOK, d * 128:(d + 1) * 128], transpose=True))(dc),
                waits=[(s2mm, LAST_MM), (s1st, store_done[NT - 1]), (s2a, LAST_VA)] if dc == 0 else [],
                incs=[(s2t, 16)])
        for dc in range(8):
            p.step("sync", (lambda d: lambda e: e.dma_start(
                XLO[d], xnlo_dram[:, d * 128:(d + 1) * 128], transpose=True))(dc),
                waits=[(s2mm, LAST_MM)] if dc == 0 else [],
                incs=[(s2t, 16)])
        GXL = p.n(s2t)
        LPS = PS[0:8, 0:2048]
        combos = [(XHI, gatehi), (XHI, gatelo), (XLO, gatehi)]
        first_mm = True
        for ci, (xsrc, gw) in enumerate(combos):
            for dc in range(8):
                w = []
                if first_mm:
                    w = [(s2t, GXL), (s2stat, 2), (s2w, W_LOADS), (s2r, LAST_ROPE),
                         (s2a, LAST_VA)]
                    first_mm = False
                p.step("tensor", (lambda gw_, d: lambda e: e.ldweights(gw_[:, d * 8:(d + 1) * 8]))(gw, dc),
                       waits=w)
                for sub in range(4):
                    st_ = (ci == 0 and dc == 0)
                    sp_ = (ci == 2 and dc == 7)
                    p.step("tensor", (lambda gw_, d, sb_, sr, st__, sp__: lambda e: e.matmul(
                        LPS[:, sb_ * 512:(sb_ + 1) * 512], gw_[:, d * 8:(d + 1) * 8],
                        sr[d][:, sb_ * 512:(sb_ + 1) * 512],
                        start=st__, stop=sp__))(gw, dc, sub, xsrc, st_, sp_),
                        incs=[(s2mm, 1)] if (ci == 2 and dc == 7 and sub == 3) else [])
        GATE_MM = p.n(s2mm)
        p.step("vector", lambda e: e.tensor_copy(lsb[:], LPS),
               waits=[(s2mm, GATE_MM), (s1v2, xn_done[NT - 1])], incs=[(s5v, 1)])
        LSB_C = p.n(s5v)
        if "logit" in taps:
            p.step("gpsimd", lambda e: e.dma_start(taps["logit"], lsb), waits=[(s5v, LSB_C)], incs=[(s2st, 16)])
        for tt in range(16):
            p.step("tensor", (lambda t_: lambda e: e.transpose(
                bank(0)[0:128, 0:8], lsb[0:8, t_ * 128:(t_ + 1) * 128], ident[0:8, 0:8]))(tt),
                waits=[(s5v, LSB_C + tt)], incs=[(s2mm, 1)])
            p.step("vector", (lambda t_: lambda e: e.tensor_copy(
                ltm[:, t_ * 8:(t_ + 1) * 8], bank(0)[0:128, 0:8]))(tt),
                waits=[(s2mm, p.n(s2mm))], incs=[(s5v, 1)])
        p.step("vector", lambda e: e.reduce_max(
            rmaxtm[:, 0:16], ltm.rearrange("p (t e) -> p t e", e=8), AX),
            waits=[(s5v, p.n(s5v))], incs=[(s5v, 1)])
        RMT = p.n(s5v)
        for tt in range(16):
            p.step("tensor", (lambda t_: lambda e: e.transpose(
                bank(1)[0:1, 0:128], rmaxtm[:, t_:t_ + 1], ident[:]))(tt),
                waits=[(s5v, RMT + tt)], incs=[(s2mm, 1)])
            p.step("vector", (lambda t_: lambda e: e.tensor_copy(
                rmaxrow[0:1, t_ * 128:(t_ + 1) * 128], bank(1)[0:1, 0:128]))(tt),
                waits=[(s2mm, p.n(s2mm))], incs=[(s5v, 1)])
        RMROW = p.n(s5v)
        for sub in range(4):
            p.step("tensor", (lambda sb_: lambda e: e.matmul(
                PS[0:8, sb_ * 512:(sb_ + 1) * 512], onesr32[0:1, 0:8],
                rmaxrow[0:1, sb_ * 512:(sb_ + 1) * 512], start=True, stop=True))(sub),
                waits=[(s5v, RMROW)] if sub == 0 else [], incs=[(s2mm, 1)])
            p.step("vector", (lambda sb_: lambda e: e.tensor_tensor(
                maskfm[0:8, sb_ * 512:(sb_ + 1) * 512], lsb[0:8, sb_ * 512:(sb_ + 1) * 512],
                PS[0:8, sb_ * 512:(sb_ + 1) * 512], ALU.is_ge))(sub),
                waits=[(s2mm, p.n(s2mm))], incs=[(s5m, 1)])
        MASK_DONE = p.n(s5m)
        if "mask" in taps:
            p.step("gpsimd", lambda e: e.dma_start(taps["mask"], maskfm[:]),
                   waits=[(s5m, MASK_DONE)], incs=[(s2st, 16)])

        p.step("tensor", lambda e: e.transpose(bank(2)[0:16, 0:128], stat[:, 0:16], ident[:]),
               waits=[(s2stat, LAST_STAT), (s5m, MASK_DONE)], incs=[(s2mm, 1)])
        p.step("tensor", lambda e: e.transpose(bank(2)[0:16, 128:256], stat[:, 16:32], ident[:]),
               incs=[(s2mm, 1)])
        TRC = p.n(s2mm)
        p.step("vector", lambda e: e.reduce_max(stat[0:16, 32:33], bank(2)[0:16, 0:128], AX),
               waits=[(s2mm, TRC), (s2stat, p.n(s2stat))], incs=[(s2stat, 1)])
        p.step("vector", lambda e: e.reduce_max(stat[0:16, 33:34], bank(2)[0:16, 128:256], AX),
               incs=[(s2stat, 1)])
        p.step("vector", lambda e: e.tensor_tensor(
            stat[0:16, 34:35], stat[0:16, 32:33], stat[0:16, 33:34], ALU.mult),
            waits=[(s2stat, p.n(s2stat))], incs=[(s2stat, 1)])
        p.step("scalar", lambda e: e.activation(stat[0:16, 35:36], stat[0:16, 34:35], AF.Sqrt),
               waits=[(s2stat, p.n(s2stat))], incs=[(s2stat, 1)])
        p.step("tensor", lambda e: e.transpose(bank(2)[0:1, 256:272], stat[0:16, 35:36], ident[0:16, 0:16]),
               waits=[(s2stat, p.n(s2stat))], incs=[(s2mm, 1)])
        p.step("vector", lambda e: e.reduce_max(stat[0:1, 36:37], bank(2)[0:1, 256:272], AX),
               waits=[(s2mm, p.n(s2mm)), (s2stat, p.n(s2stat))], incs=[(s2stat, 1)])
        p.step("vector", lambda e: e.tensor_scalar(
            stat[0:1, 37:38], stat[0:1, 36:37], -0.1275, None, ALU.mult),
            waits=[(s2stat, p.n(s2stat))], incs=[(s2stat, 1)])
        p.step("tensor", lambda e: e.matmul(bank(0)[:, 0:1], onesr32[:], stat[0:1, 37:38],
                                            start=True, stop=True),
               waits=[(s2stat, p.n(s2stat))], incs=[(s2mm, 1)])
        p.step("vector", lambda e: e.tensor_copy(bias_col[:], bank(0)[:, 0:1]),
               waits=[(s2mm, p.n(s2mm))], incs=[(s2stat, 1)])
        BIAS_DONE = p.n(s2stat)
        if "bias" in taps:
            p.step("gpsimd", lambda e: e.dma_start(taps["bias"], bias_col[:]),
                   waits=[(s2stat, BIAS_DONE)], incs=[(s2st, 16)])

        for pr in range(8):
            p.step("sync", (lambda pp: lambda e: e.dma_start(
                q_sb[:, pp * 2048:(pp + 1) * 2048],
                q_dram[0:TOK, pp * 128:(pp + 1) * 128], transpose=True))(pr),
                waits=[(s2st, KV_ST), (s2t, GXL)] if pr == 0 else [], incs=[(s2t, 16)])
        QSB_LD = p.n(s2t)
        p.step("gpsimd", lambda e: e.memset(vt_sb[:], 1.0),
               waits=[(s2r, LAST_ROPE), (s2stat, BIAS_DONE)], incs=[(s3v, 1)])
        VONES = p.n(s3v)

        blk_list = [(s, hg) for s in range(NSTRIP) for hg in range(NHG)]
        kch_ready = {}
        vch_ready = {}
        kch_free_w = {}
        load_order = [(s, hg, cc) for (s, hg) in blk_list for cc in range(4)]
        li = [0]

        def emit_kv_load(idx):
            s_, hg, cc = load_order[idx]
            cid = idx
            slot = cid % 4
            w = [(s2st, KV_ST), (s3v, VONES), (s2mm, GATE_MM), (s2t, QSB_LD)]
            if cid >= 4:
                w.append(kch_free_w[cid - 4])
            for pr in range(2):
                gpr = hg * 2 + pr
                p.step("sync", (lambda sl, gp, c_, pp: lambda e: e.dma_start(
                    kt_sb[:, sl * 4096 + pp * 2048:sl * 4096 + pp * 2048 + 2048],
                    k_dram[c_ * 2048:(c_ + 1) * 2048, gp * 128:(gp + 1) * 128],
                    transpose=True))(slot, gpr, cc, pr),
                    waits=w if pr == 0 else [], incs=[(s3k, 16)])
            kch_ready[(s_, hg, cc)] = p.n(s3k)
            for hx in range(4):
                wv_ = []
                if hx == 0:
                    wv_ = [(s3v, VONES)]
                    if cid >= 4:
                        wv_.append(kch_free_w[cid - 4])
                p.step("sync", (lambda sl, h_, c_, hx_: lambda e: e.dma_start(
                    vt_sb[:, 0:16896].rearrange("p (sl2 x) -> p sl2 x", sl2=4)[:, sl, 0:4160].rearrange("p (k h e) -> p k h e", k=16, h=4)[:, :, hx_, 0:64],
                    v_dram.rearrange("(ck p) (h e) -> p ck h e", p=128, h=16)[:, c_ * 16:(c_ + 1) * 16, h_ * 4 + hx_, :]))(slot, hg, cc, hx),
                    waits=wv_, incs=[(s3v, 16)])
            vch_ready[(s_, hg, cc)] = p.n(s3v)

        for _ in range(4):
            emit_kv_load(li[0]); li[0] += 1

        sc_done = {}
        ex_done = {}
        pv_done = {}
        ev_done = {}
        lb_done = {}
        ost_done = {}
        pending_bcast = [None]
        pv_next = [0]
        j = 0

        def emit_pv(pj):
            bi, rem = divmod(pj, NKT * 2)
            s_, hg = blk_list[bi]
            pkti, ppr = divmod(rem, 2)
            cc, kk = divmod(pkti, 16)
            cid = bi * 4 + cc
            slot = cid % 4
            pb = (pj % 8) * 1024
            first = pkti == 0
            last = pkti == NKT - 1
            w = [(s3ex, ex_done[pj])]
            if pkti == 0 and bi > 0:
                w.append((s3ev, lb_done[bi - 1][ppr]))
            if first:
                w.append((s3v, vch_ready[(s_, hg, cc)]))
            for h in range(2):
                hh = ppr * 2 + h
                vo = slot * 4224 + kk * 260 + hh * 65
                p.step("tensor", (lambda v_: lambda e: e.ldweights(vt_sb[:, v_:v_ + 128]))(vo),
                       waits=w if h == 0 else [])
                p.step("tensor", (lambda v_, hh_, pb_, f_, l_: lambda e: e.matmul(
                    PS[0:128, (4 + hh_) * 512:(4 + hh_) * 512 + 512],
                    vt_sb[:, v_:v_ + 128], p_sb[:, pb_ + (hh_ % 2) * 512:pb_ + (hh_ % 2) * 512 + 512],
                    start=f_, stop=l_))(vo, hh, pb, first, last),
                    incs=[(s3pv, 1)] if h == 1 else [])
            pv_done[pj] = p.n(s3pv)
            if kk == 15 and ppr == 1:
                kch_free_w[cid] = (s3pv, p.n(s3pv))
                if li[0] < len(load_order):
                    emit_kv_load(li[0]); li[0] += 1

        for bi, (s_, hg) in enumerate(blk_list):
            for kti in range(NKT):
                cc, kk = divmod(kti, 16)
                for pr in range(2):
                    so = (j % 2) * 1024
                    w = []
                    if kk == 0 and pr == 0:
                        w.append((s3k, kch_ready[(s_, hg, cc)]))
                    if j == 0:
                        w += [(s5m, MASK_DONE), (s2stat, BIAS_DONE)]
                    if j >= 2:
                        w.append((s3ex, ex_done[j - 2]))
                    slot = (bi * 4 + cc) % 4
                    ko = slot * 4096 + pr * 2048 + kk * 128
                    qo = (hg * 2 + pr) * 2048 + s_ * 512
                    p.step("tensor", (lambda k_: lambda e: e.ldweights(
                        kt_sb[0:64, k_:k_ + 128], tile_position=(0, 0)))(ko), waits=w)
                    p.step("tensor", (lambda k_: lambda e: e.ldweights(
                        kt_sb[64:128, k_:k_ + 128], tile_position=(64, 0)))(ko))
                    p.step("tensor", (lambda k_, q_, so_: lambda e: e.matmul(
                        PS[:, so_:so_ + 512], kt_sb[0:64, k_:k_ + 128], q_sb[0:64, q_:q_ + 512],
                        start=True, stop=True, tile_position=(0, 0)))(ko, qo, so))
                    p.step("tensor", (lambda k_, q_, so_: lambda e: e.matmul(
                        PS[:, so_ + 512:so_ + 1024], kt_sb[64:128, k_:k_ + 128], q_sb[64:128, q_:q_ + 512],
                        start=True, stop=True, tile_position=(64, 0)))(ko, qo, so),
                        incs=[(s3sc, 1)])
                    sc_done[j] = p.n(s3sc)
                    we = [(s3sc, sc_done[j])]
                    if j == 0:
                        we.append((s2stat, BIAS_DONE))
                    if j >= 8:
                        we.append((s3pv, pv_done[j - 8]))
                    p.step("scalar", (lambda so_, pb_: lambda e: e.activation(
                        p_sb[:, pb_:pb_ + 1024], PS[:, so_:so_ + 1024], AF.Exp,
                        bias=bias_col[:], scale=0.125))(so, (j % 8) * 1024),
                        waits=we, incs=[(s3ex, 1)])
                    ex_done[j] = p.n(s3ex)
                    local = j % (NKT * 2)
                    if local == 7 and pending_bcast[0] is not None:
                        pending_bcast[0]()
                        pending_bcast[0] = None
                    if local >= 7 or j < NKT * 2:
                        while pv_next[0] < j:
                            emit_pv(pv_next[0])
                            pv_next[0] += 1
                    j += 1
            while pv_next[0] < j:
                emit_pv(pv_next[0])
                pv_next[0] += 1
            w = [(s3pv, pv_done[j - 1])]
            if bi > 0:
                w.append((s3st, ost_done[bi - 1]))
            p.step("vector", lambda e: e.tensor_copy(orawB[0:65, :], PS[0:65, 2048:4096]),
                   waits=w, incs=[(s3ev, 1)])
            LR = p.n(s3ev)
            def mk_bcast(CP_, bi_, hg_, ss_):
                def emit_bc():
                    lb_pair = []
                    for hh in range(4):
                        p.step("vector", (lambda h_: lambda e: e.reciprocal(
                            lrow[0:1, h_ * 512:h_ * 512 + 512],
                            orawB[64:65, h_ * 512:h_ * 512 + 512]))(hh),
                            waits=[(s3ev, CP_)] if hh == 0 else [], incs=[(s3ev, 1)])
                        p.step("vector", (lambda h_: lambda e: e.tensor_copy(
                            lrow16[0:1, h_ * 512:h_ * 512 + 512],
                            lrow[0:1, h_ * 512:h_ * 512 + 512]))(hh),
                            waits=[(s3ev, p.n(s3ev))], incs=[(s3ev, 1)])
                        p.step("tensor", (lambda h_: lambda e: e.matmul(
                            PS[0:64, (4 + h_) * 512:(4 + h_) * 512 + 512],
                            onesr16[0:1, 0:64], lrow16[0:1, h_ * 512:h_ * 512 + 512],
                            start=True, stop=True))(hh),
                            waits=[(s3ev, p.n(s3ev))], incs=[(s3mx, 1)])
                        p.step("vector", (lambda h_: lambda e: e.tensor_copy(
                            lb4[0:64, h_ * 512:h_ * 512 + 512],
                            PS[0:64, (4 + h_) * 512:(4 + h_) * 512 + 512]))(hh),
                            waits=[(s3mx, p.n(s3mx))], incs=[(s3ev, 1)])
                        if hh in (1, 3):
                            lb_pair.append(p.n(s3ev))
                    lb_done[bi_] = tuple(lb_pair)
                    p.step("vector", lambda e: e.tensor_tensor(
                        o_tmp[0:64, :], orawB[0:64, :], lb4[0:64, :], ALU.mult),
                        waits=[(s3ev, lb_pair[1])], incs=[(s3ev, 1)])
                    ev_done[bi_] = p.n(s3ev)
                    p.step("gpsimd", (lambda: lambda e: e.dma_start(
                        o_dram.rearrange("(hh q) t -> q hh t", q=64)[:, hg_ * 4:(hg_ + 1) * 4, ss_ * 512:(ss_ + 1) * 512],
                        o_tmp[0:64, :].rearrange("q (hh t) -> q hh t", t=512)))(),
                        waits=[(s3ev, ev_done[bi_])], incs=[(s3st, 16)])
                    ost_done[bi_] = p.n(s3st)
                return emit_bc
            pending_bcast[0] = mk_bcast(LR, bi, hg, s_)
            if bi == len(blk_list) - 1:
                pending_bcast[0]()
                pending_bcast[0] = None

        ATT_DONE = p.n(s3ev)
        EXP_ALL = p.n(s3ex)
        PV_ALL = p.n(s3pv)
        O_ST = p.n(s3st)
        if "o" in taps:
            p.step("gpsimd", lambda e: e.dma_start(taps["o"], o_dram), waits=[(s3st, O_ST)], incs=[(s2st, 16)])

        for dc in range(8):
            p.step("gpsimd", (lambda d: lambda e: e.dma_start(
                wout[:, d * 1024:(d + 1) * 1024], woutT[d * 128:(d + 1) * 128, :]))(dc),
                waits=[(s3ev, ATT_DONE), (s3pv, PV_ALL)] if dc == 0 else [], incs=[(s2st, 16)])
        for dc in range(8):
            p.step("sync", (lambda d: lambda e: e.dma_start(
                o_sb[:, d * 2048:(d + 1) * 2048], o_dram[d * 128:(d + 1) * 128, :]))(dc),
                waits=[(s3st, O_ST)] if dc == 0 else [], incs=[(s2st, 16)])
        WOUT_LD = p.n(s2st)

        s4_ev = {}
        for tt in range(16):
            slot = tt % 2
            sb4 = bank(slot * 2, 2)
            w0 = [(s2st, WOUT_LD), (s3ex, EXP_ALL)]
            if tt >= 2:
                w0.append((s4e, s4_ev[tt - 2]))
            for dc in range(8):
                p.step("tensor", (lambda d, t_: lambda e: e.ldweights(
                    o_sb[:, d * 2048 + t_ * 128:d * 2048 + t_ * 128 + 128]))(dc, tt),
                    waits=w0 if dc == 0 else [])
                for half in range(2):
                    p.step("tensor", (lambda d, h, b_, t_: lambda e: e.matmul(
                        b_[:, h * 512:(h + 1) * 512],
                        o_sb[:, d * 2048 + t_ * 128:d * 2048 + t_ * 128 + 128],
                        wout[:, d * 1024 + h * 512:d * 1024 + (h + 1) * 512],
                        start=(d == 0), stop=(d == 7)))(dc, half, sb4, tt),
                        incs=[(s4, 1)] if dc == 7 and half == 1 else [])
            wev = [(s4, p.n(s4))]
            if tt >= 2:
                wev.append((s4st, 16 * (tt - 1)))
            p.step("vector", (lambda sl, b_: lambda e: e.tensor_copy(
                A5[:, 16384 + sl * 1024:16384 + sl * 1024 + 1024], b_))(slot, sb4),
                waits=wev, incs=[(s4e, 1)])
            s4_ev[tt] = p.n(s4e)
            p.step("gpsimd", (lambda sl, t_: lambda e: e.dma_start(
                s_dram[t_ * 128:(t_ + 1) * 128, :], A5[:, 16384 + sl * 1024:16384 + sl * 1024 + 1024]))(slot, tt),
                waits=[(s4e, s4_ev[tt])], incs=[(s4st, 16), (s2st, 16)])
        S_ST = p.n(s4st)
        P4_EV = p.n(s4e)
        if "s" in taps:
            p.step("gpsimd", lambda e: e.dma_start(taps["s"], s_dram), waits=[(s4st, S_ST)], incs=[(s2st, 16)])

        for dc in range(8):
            p.step("sync", (lambda d: lambda e: e.dma_start(
                xnof[:, d * 2048:(d + 1) * 2048],
                xn_dram[0:TOK, d * 128:(d + 1) * 128], transpose=True))(dc),
                waits=[(s4e, s4_ev[15])] if dc == 0 else [], incs=[(s2t, 16)])
        XNOF_LD = p.n(s2t)

        for rc in range(16):
            p.step("gpsimd", (lambda r_: lambda e: e.dma_start(
                wd_sb[:, r_ * 1024:(r_ + 1) * 1024],
                wd_in[r_ * 128:(r_ + 1) * 128, :]))(rc),
                waits=[(s3pv, PV_ALL), (s3ev, ATT_DONE)] if rc == 0 else [], incs=[(s5w, 16)])
        WD_LD = p.n(s5w)

        wgu_ld = {}
        wgu_free = {}

        def emit_wgu_load(key):
            half, e_ = divmod(key, NE)
            slot = key % 2
            w = []
            if key == 0:
                w = [(s5w, WD_LD)]
            if key >= 2:
                w.append(wgu_free[key - 2])
            for op, win in ((0, wg_in), (1, wu_in)):
                for ch in range(8):
                    p.step("gpsimd", (lambda sl, o_, c_, wi, ee: lambda e: e.dma_start(
                        wgu[:, sl * 4096 + o_ * 2048 + c_ * 256:sl * 4096 + o_ * 2048 + (c_ + 1) * 256],
                        wi[ee * 1024 + c_ * 128:ee * 1024 + (c_ + 1) * 128, :]))(slot, op, ch, win, e_),
                        waits=(w if (op == 0 and ch == 0) else []), incs=[(s5w, 16)])
            wgu_ld[key] = p.n(s5w)

        emit_wgu_load(0)
        emit_wgu_load(1)

        gu_psum_free = {}
        ebm_done = {}
        inter_done = {}
        down_ev = {}
        oui = 0
        for half in range(2):
            hcol = half * 1024
            for e_ in range(NE):
                key = half * NE + e_
                eslot = key % 2
                gslot = oui % 2
                gb = bank(gslot * 4, 4)
                w = [(s5m, MASK_DONE), (s5w, wgu_ld[key]), (s4st, S_ST), (s2t, XNOF_LD)]
                if key == 0:
                    w += [(s3ev, ATT_DONE), (s4e, P4_EV)]
                if half == 1 and e_ == 0:
                    w.append((s5a, down_ev[(0, 7)]))
                if oui >= 2:
                    w.append(gu_psum_free[oui - 2])
                for sub in range(2):
                    p.step("tensor", (lambda sb_, b_, ee, hc: lambda e: e.matmul(
                        b_[:, sb_ * 512:(sb_ + 1) * 512], sel8[:, ee * 128:ee * 128 + 128],
                        maskfm[0:8, hc + sb_ * 512:hc + (sb_ + 1) * 512],
                        start=True, stop=True))(sub, gb, e_, hcol),
                        waits=w if sub == 0 else [], incs=[(s2mm, 1)] if sub == 1 else [])
                mb_mm = p.n(s2mm)
                p.step("vector", (lambda sl, b_: lambda e: e.tensor_copy(
                    ebm[:, sl * 2048:sl * 2048 + 1024], b_[:, 0:1024]))(eslot, gb),
                    waits=[(s2mm, mb_mm)], incs=[(s5v, 1)])
                ebm_done[key] = p.n(s5v)
                gu_mm = {}
                for op in range(2):
                    slot = oui % 2
                    ob = bank(slot * 4, 4)
                    w0 = []
                    if op == 0:
                        w0.append((s5v, ebm_done[key]))
                    else:
                        if oui >= 2:
                            w0.append(gu_psum_free[oui - 2])
                        if half == 1 and e_ == 0:
                            w0.append((s5a, down_ev[(0, 7)]))
                    for Mt in range(2):
                        for dc in range(8):
                            p.step("tensor", (lambda sl_, o_, c_, kk_: lambda e: e.ldweights(
                                wgu[:, (kk_ % 2) * 4096 + o_ * 2048 + c_ * 256 + sl_ * 128:
                                       (kk_ % 2) * 4096 + o_ * 2048 + c_ * 256 + sl_ * 128 + 128]))(Mt, op, dc, key),
                                waits=w0 if (Mt == 0 and dc == 0) else [])
                            for sub in range(2):
                                p.step("tensor", (lambda o_, c_, m_, sb_, b_, kk_, hc: lambda e: e.matmul(
                                    b_[:, m_ * 1024 + sb_ * 512:m_ * 1024 + (sb_ + 1) * 512],
                                    wgu[:, (kk_ % 2) * 4096 + o_ * 2048 + c_ * 256 + m_ * 128:
                                           (kk_ % 2) * 4096 + o_ * 2048 + c_ * 256 + m_ * 128 + 128],
                                    xnof[:, c_ * 2048 + hc + sb_ * 512:c_ * 2048 + hc + (sb_ + 1) * 512],
                                    start=(c_ == 0), stop=(c_ == 7)))(op, dc, Mt, sub, ob, key, hcol),
                                    incs=[(s5, 1)] if (Mt == 1 and dc == 7 and sub == 1) else [])
                    gu_mm[op] = p.n(s5)
                    if op == 0:
                        wr = [(s5, gu_mm[0]), (s5a, p.n(s5a))]
                        if key >= 1:
                            wr.append((s5v, inter_done[key - 1]))
                        p.step("scalar", (lambda b_: lambda e: e.activation(
                            gr_sb[:, 0:2048], b_[:, 0:2048], AF.Relu))(ob),
                            waits=wr, incs=[(s5a, 1)])
                        gu_psum_free[oui] = (s5a, p.n(s5a))
                    else:
                        for Mt in range(2):
                            p.step("vector", (lambda b_, sl, m_: lambda e: e.tensor_tensor(
                                um_sb[:, m_ * 1024:m_ * 1024 + 1024],
                                b_[:, m_ * 1024:m_ * 1024 + 1024],
                                ebm[:, sl * 2048:sl * 2048 + 1024], ALU.mult))(ob, eslot, Mt),
                                waits=([(s5, gu_mm[1]), (s5v, p.n(s5v))] if Mt == 0 else []),
                                incs=[(s5v, 1)])
                        gu_psum_free[oui] = (s5v, p.n(s5v))
                    oui += 1
                p.step("vector", lambda e: e.tensor_tensor(
                    gr2_sb[:, 0:2048], gr_sb[:, 0:2048], gr_sb[:, 0:2048], ALU.mult),
                    waits=[(s5a, p.n(s5a)), (s5v, p.n(s5v))], incs=[(s5v, 1)])
                p.step("vector", (lambda ee: lambda e: e.tensor_tensor(
                    inter[:, ee * 2048:ee * 2048 + 2048], gr2_sb[:, 0:2048], um_sb[:, 0:2048], ALU.mult))(e_),
                    waits=[(s5v, p.n(s5v))], incs=[(s5v, 1)])
                inter_done[key] = p.n(s5v)
                wgu_free[key] = (s5, gu_mm[1])
                nk = key + 2
                if nk < 2 * NE:
                    emit_wgu_load(nk)
            for tt in range(8):
                slot = tt % 2
                db = bank(slot * 2, 2)
                w0 = [(s5v, inter_done[half * NE + NE - 1]), (s5w, WD_LD)]
                if oui >= 2:
                    w0.append(gu_psum_free[oui - 2])
                    w0.append(gu_psum_free[oui - 1])
                if tt >= 2:
                    w0.append((s5a, down_ev[(half, tt - 2)]))
                for rc in range(16):
                    p.step("tensor", (lambda r_, t_: lambda e: e.ldweights(
                        inter[:, r_ * 1024 + t_ * 128:r_ * 1024 + t_ * 128 + 128]))(rc, tt),
                        waits=w0 if rc == 0 else [])
                    for hf in range(2):
                        p.step("tensor", (lambda r_, t_, h_, b_: lambda e: e.matmul(
                            b_[:, h_ * 512:(h_ + 1) * 512],
                            inter[:, r_ * 1024 + t_ * 128:r_ * 1024 + t_ * 128 + 128],
                            wd_sb[:, r_ * 1024 + h_ * 512:r_ * 1024 + (h_ + 1) * 512],
                            start=(r_ == 0), stop=(r_ == 15)))(rc, tt, hf, db),
                            incs=[(s5, 1)] if rc == 15 and hf == 1 else [])
                p.step("scalar", (lambda b_, t_, hf_: lambda e: e.activation(
                    gsil[:, (hf_ * 8 + t_) * 1024:(hf_ * 8 + t_) * 1024 + 1024],
                    b_[:, 0:1024], AF.Silu))(db, tt, half),
                    waits=[(s5, p.n(s5)), (s5a, p.n(s5a))] +
                          ([(s4e, P4_EV), (s2st, WOUT_LD)] if (half == 0 and tt == 0) else []),
                    incs=[(s5a, 1)])
                down_ev[(half, tt)] = p.n(s5a)
        FFN_DONE = p.n(s5a)
        if "g" in taps:
            p.step("gpsimd", lambda e: e.dma_start(taps["g"], gsil), waits=[(s5a, FFN_DONE)], incs=[(s2st, 16)])

        out_done = {}
        for tt in range(16):
            slot = tt % 2
            wx = [(s6v, out_done[tt - 2])] if tt >= 2 else [(s4st, S_ST)]
            p.step("sync", (lambda sl, t_: lambda e: e.dma_start(
                x6[:, sl * 1024:sl * 1024 + 1024], xbm[t_ * 128:(t_ + 1) * 128, :]))(slot, tt),
                waits=wx, incs=[(s6x, 16)])
            ws = [(s4st, 16 * (tt + 1)), (s5a, FFN_DONE)]
            if tt >= 2:
                ws.append((s6v, out_done[tt - 2]))
            p.step("sync", (lambda sl, t_: lambda e: e.dma_start(
                sload[:, sl * 1024:sl * 1024 + 1024], s_dram[t_ * 128:(t_ + 1) * 128, :]))(slot, tt),
                waits=ws, incs=[(s6s, 16)])
            p.step("vector", (lambda sl, gc: lambda e: e.tensor_tensor(
                o6[:, sl * 1024:sl * 1024 + 1024], sload[:, sl * 1024:sl * 1024 + 1024],
                gsil[:, gc:gc + 1024], ALU.mult))(slot, tt * 1024),
                waits=[(s6s, 16 * (tt + 1)), (s5a, FFN_DONE)] +
                      ([(s6st, 16 * (tt - 1))] if tt >= 2 else []),
                incs=[(s6v, 1)])
            p.step("vector", (lambda sl: lambda e: e.tensor_tensor(
                o6[:, sl * 1024:sl * 1024 + 1024], o6[:, sl * 1024:sl * 1024 + 1024],
                x6[:, sl * 1024:sl * 1024 + 1024], ALU.add))(slot),
                waits=[(s6x, 16 * (tt + 1)), (s6v, p.n(s6v))], incs=[(s6v, 1)])
            out_done[tt] = p.n(s6v)
            p.step("gpsimd", (lambda sl, t_: lambda e: e.dma_start(
                out[t_ * 128:(t_ + 1) * 128, :], o6[:, sl * 1024:sl * 1024 + 1024]))(slot, tt),
                waits=[(s6v, out_done[tt])], incs=[(s6st, 16)])
        p.step("gpsimd", lambda e: e.nop(), waits=[(s6st, 16 * 16)])

        p.check_deadlock()

        semh = {}
        for name in p.semnames:
            semh[name] = es.enter_context(nc.semaphore(name))
        block = es.enter_context(nc.Block())

        @block.tensor
        def _(t):
            p.emit("tensor", t, semh)

        @block.vector
        def _(v):
            p.emit("vector", v, semh)

        @block.scalar
        def _(s):
            p.emit("scalar", s, semh)

        @block.gpsimd
        def _(g):
            p.emit("gpsimd", g, semh)

        @block.sync
        def _(sy):
            p.emit("sync", sy, semh)

    return nc


_ROPE_BASE = 10000.0


def _host_tables(off):
    half = 32
    inv = 1.0 / _ROPE_BASE ** (np.arange(half, dtype=np.float64) / half)
    pos = np.arange(L, dtype=np.float64)
    pos = np.concatenate([pos[off:], pos[:off]])
    ang = pos[:, None] * inv[None, :]
    c = np.cos(ang).astype(np.float32)
    s = np.sin(ang).astype(np.float32)
    cost = np.concatenate([c, c], axis=1)
    sint = np.concatenate([-s, s], axis=1)
    return np.ascontiguousarray(cost), np.ascontiguousarray(sint)


def _host_consts():
    sel8 = np.zeros((8, 1024), np.float32)
    for e in range(8):
        sel8[e, e * 128:(e + 1) * 128] = 1
    ident = np.eye(128, dtype=np.float32)
    onesr = np.ones((1, 128), np.float32)
    return sel8, ident, onesr


_NC_CACHE = {}


def kernel(L0_attn_x, L0_attn_w_qkv, L0_attn_w_out, L0_ffn_gate_w,
           L0_ffn_wg, L0_ffn_wu, L0_ffn_wd, _debug_taps=(), _trace=False):
    x = np.asarray(L0_attn_x, dtype=np.float32)
    w_qkv = np.asarray(L0_attn_w_qkv, dtype=np.float32)
    w_out = np.asarray(L0_attn_w_out, dtype=np.float32)
    gate_w = np.asarray(L0_ffn_gate_w, dtype=np.float32)
    wg = np.asarray(L0_ffn_wg, dtype=np.float32)
    wu = np.asarray(L0_ffn_wu, dtype=np.float32)
    wd = np.asarray(L0_ffn_wd, dtype=np.float32)

    wkvT = np.ascontiguousarray(w_qkv[D:3 * D].T)
    wqT = np.ascontiguousarray(w_qkv[0:D].T)
    gateT = np.ascontiguousarray(gate_w.T)
    wg_f = np.ascontiguousarray(wg.reshape(NE * D, DFF))
    wu_f = np.ascontiguousarray(wu.reshape(NE * D, DFF))
    wd_f = np.ascontiguousarray(wd.reshape(NE * DFF, D))
    woutT_ = np.ascontiguousarray(w_out.T)
    sel8, ident, onesr = _host_consts()

    key = tuple(sorted(_debug_taps))
    if key not in _NC_CACHE:
        _NC_CACHE[key] = build_nc(_debug_taps)
    nc = _NC_CACHE[key]

    in_maps = []
    for c in range(8):
        b = c // 4
        off = 2048 * (c % 4)
        xb_ = np.ascontiguousarray(np.concatenate([x[b, off:], x[b, :off]], axis=0))
        cost, sint = _host_tables(off)
        in_maps.append({
            "xb": xb_, "wkvT": wkvT, "wqT": wqT, "gateT": gateT,
            "wg": wg_f, "wu": wu_f, "wd": wd_f, "woutT": woutT_,
            "costab": cost, "sintab": sint, "sel8_in": sel8,
            "ident_in": ident, "onesr": onesr,
        })
    res = run_bass_kernel_spmd(nc, in_maps, core_ids=list(range(8)), trace=_trace)
    outp = np.empty((B, L, D), np.float32)
    for c in range(8):
        b = c // 4
        off = 2048 * (c % 4)
        outp[b, off:off + 2048] = res.results[c]["out"]
    if _debug_taps:
        return outp, res
    return outp


# version 06 (final)
# speedup vs baseline: 1.2715x; 1.0251x over previous
import numpy as np
import ml_dtypes
import concourse.bass as bass
import concourse.mybir as mybir
from concourse.bass_utils import run_bass_kernel_spmd
from contextlib import ExitStack

F32 = mybir.dt.float32
BF16 = mybir.dt.bfloat16
AF = mybir.ActivationFunctionType
ALU = mybir.AluOpType
AX = mybir.AxisListType.X

B, L, D = 2, 8192, 1024
NPAIR = 8
TOK = 2048
NT = L // 128
OWN_T = TOK // 128
NG = 8
GTOK = 1024
NKT = L // 128
NSTRIP = TOK // 512
NHG = 4
NE, DFF = 8, 256

ENGS = ("tensor", "vector", "scalar", "gpsimd", "sync")


class Prog:
    def __init__(self):
        self.steps = {e: [] for e in ENGS}
        self.semnames = []
        self.counts = {}

    def sem(self, name):
        self.semnames.append(name)
        self.counts[name] = 0
        return name

    def n(self, s):
        return self.counts[s]

    def step(self, eng, fn, waits=(), incs=()):
        waits = [(s, v) for (s, v) in waits if v > 0]
        for s, k in incs:
            self.counts[s] += k
        self.steps[eng].append((fn, waits, list(incs)))

    def check_deadlock(self):
        idx = {e: 0 for e in ENGS}
        cnt = {s: 0 for s in self.semnames}
        progress = True
        while progress:
            progress = False
            for e in ENGS:
                while idx[e] < len(self.steps[e]):
                    fn, waits, incs = self.steps[e][idx[e]]
                    if all(cnt[s] >= v for s, v in waits):
                        for s, k in incs:
                            cnt[s] += k
                        idx[e] += 1
                        progress = True
                    else:
                        break
        stuck = {e: (idx[e], len(self.steps[e])) for e in ENGS if idx[e] < len(self.steps[e])}
        if stuck:
            detail = {}
            for e in stuck:
                fn, waits, incs = self.steps[e][idx[e]]
                detail[e] = [(s, v, cnt[s]) for s, v in waits if cnt[s] < v]
            raise RuntimeError(f"schedule deadlock: {stuck}; blocked: {detail}")

    def emit(self, eng_name, eng, semh):
        for fn, waits, incs in self.steps[eng_name]:
            for s, v in waits:
                eng.wait_ge(semh[s], v)
            ins = fn(eng)
            for s, k in incs:
                ins = ins.then_inc(semh[s], k)


def build_nc(debug_taps=()):
    nc = bass.Bass("TRN2", target_bir_lowering=False, debug=False)
    p = Prog()

    xbm = nc.dram_tensor("xb", [L, D], F32, kind="ExternalInput").ap()
    wkvT = nc.dram_tensor("wkvT", [D, 2048], F32, kind="ExternalInput").ap()
    wqT = nc.dram_tensor("wqT", [D, D], F32, kind="ExternalInput").ap()
    gateT = nc.dram_tensor("gateT", [D, 8], F32, kind="ExternalInput").ap()
    wg_in = nc.dram_tensor("wg", [NE * D, DFF], F32, kind="ExternalInput").ap()
    wu_in = nc.dram_tensor("wu", [NE * D, DFF], F32, kind="ExternalInput").ap()
    wd_in = nc.dram_tensor("wd", [NE * DFF, D], F32, kind="ExternalInput").ap()
    woutT = nc.dram_tensor("woutT", [D, D], F32, kind="ExternalInput").ap()
    costab = nc.dram_tensor("costab", [L, 64], F32, kind="ExternalInput").ap()
    sintab = nc.dram_tensor("sintab", [L, 64], F32, kind="ExternalInput").ap()
    sel8_in = nc.dram_tensor("sel8_in", [8, 1024], F32, kind="ExternalInput").ap()
    ident_in = nc.dram_tensor("ident_in", [128, 128], F32, kind="ExternalInput").ap()
    onesr_in = nc.dram_tensor("onesr", [1, 128], F32, kind="ExternalInput").ap()

    out = nc.dram_tensor("out", [TOK, D], F32, kind="ExternalOutput").ap()

    xn_dram = nc.dram_tensor("xn_dram", [L, D], BF16, kind="Internal").ap()
    xnlo_dram = nc.dram_tensor("xnlo_dram", [TOK, D], BF16, kind="Internal").ap()
    k_dram = nc.dram_tensor("k_dram", [L, D], BF16, kind="Internal").ap()
    q_dram = nc.dram_tensor("q_dram", [TOK, D], BF16, kind="Internal").ap()
    v_dram = nc.dram_tensor("v_dram", [L, D], BF16, kind="Internal").ap()
    o_dram = nc.dram_tensor("o_dram", [D, TOK], BF16, kind="Internal").ap()
    s_dram = nc.dram_tensor("s_dram", [TOK, D], BF16, kind="Internal").ap()

    taps = {}
    for name, shape, dt in [("q", [TOK, D], BF16), ("k", [L, D], BF16), ("v", [L, D], BF16),
                            ("o", [D, TOK], BF16), ("l", [1, NHG * NSTRIP * 2048], F32),
                            ("mask", [8, 2048], BF16), ("logit", [8, 2048], F32),
                            ("xn", [L, D], BF16), ("bias", [128, 1], F32),
                            ("s", [TOK, D], BF16), ("g", [128, 16384], BF16)]:
        if name in debug_taps:
            taps[name] = nc.dram_tensor("tap_" + name, shape, dt, kind="ExternalOutput").ap()

    es = ExitStack()
    with es:
        PS = es.enter_context(nc.psum_tensor([128, 4096], F32))

        def bank(i, w=1):
            return PS[:, i * 512:(i + w) * 512]

        A1 = es.enter_context(nc.sbuf_tensor("A1", [128, 16384], BF16, is_transpose_dest=True))
        A2 = es.enter_context(nc.sbuf_tensor("A2", [128, 16384], BF16, is_transpose_dest=True))
        A3 = es.enter_context(nc.sbuf_tensor("A3", [128, 16896], BF16, is_transpose_dest=True))
        A5 = es.enter_context(nc.sbuf_tensor("A5", [128, 18432], BF16, is_transpose_dest=True))
        A6 = es.enter_context(nc.sbuf_tensor("A6", [128, 16384], BF16))
        XNB = es.enter_context(nc.sbuf_tensor("XNB", [128, 4096], BF16))
        B1 = es.enter_context(nc.sbuf_tensor("B1", [128, 6144], F32))

        bn6 = es.enter_context(nc.sbuf_tensor("bn6", [128, 12], F32))
        mv = es.enter_context(nc.sbuf_tensor("mv", [128, 16], F32))
        rstd = es.enter_context(nc.sbuf_tensor("rstd", [128, 8], F32))
        nmur = es.enter_context(nc.sbuf_tensor("nmur", [128, 8], F32))
        stat = es.enter_context(nc.sbuf_tensor("stat", [128, 64], F32))
        bias_col = es.enter_context(nc.sbuf_tensor("bias_col", [128, 1], F32))
        eps_col = es.enter_context(nc.sbuf_tensor("eps_col", [128, 1], F32))
        ident = es.enter_context(nc.sbuf_tensor("ident", [128, 128], F32))
        onesr32 = es.enter_context(nc.sbuf_tensor("onesr32", [1, 128], F32))
        onesr16 = es.enter_context(nc.sbuf_tensor("onesr16", [1, 128], BF16))
        sel8 = es.enter_context(nc.sbuf_tensor("sel8", [8, 1024], BF16))
        gate32 = es.enter_context(nc.sbuf_tensor("gate32", [128, 64], F32))
        gatehi = es.enter_context(nc.sbuf_tensor("gatehi", [128, 64], BF16))
        gatelo = es.enter_context(nc.sbuf_tensor("gatelo", [128, 64], BF16))
        ltm = es.enter_context(nc.sbuf_tensor("ltm", [128, 128], F32))
        rmaxtm = es.enter_context(nc.sbuf_tensor("rmaxtm", [128, 16], F32))
        xnlo_b = es.enter_context(nc.sbuf_tensor("xnlo_b", [128, 1024], BF16))

        def xnf(slot, dc):
            return A1[:, slot * 8192 + dc * 1024:slot * 8192 + (dc + 1) * 1024]
        wkv = A2
        wq_sb = A3[:, 8192:16384]
        cos_sb = A3[:, 0:4096]
        sin_sb = A3[:, 4096:8192]
        k_ev = A6[:, 0:2048]
        v_ev = A6[:, 2048:4096]
        q_ev = A6[:, 4096:6144]
        ksq = A6[:, 6144:7168]
        kt_sb = A2
        vt_sb = A3
        q_sb = A5[:, 0:16384]
        p_sb = A6[:, 0:8192]
        xin = B1[:, 0:4096]
        xn32 = B1[:, 5120:6144]
        kr32q = B1[:, 4096:5120]
        orawB = B1[:, 4096:6144]
        lrow = B1[0:1, 0:2048]
        lsb = B1[0:8, 0:2048]
        rmaxrow = B1[0:1, 2048:4096]
        lrow16 = A1[0:1, 0:2048]
        lb4 = A1[0:64, 2048:4096]
        o_tmp = A1[0:64, 4096:6144]
        wout = A3[:, 0:8192]
        o_sb = A6
        inter = A1
        xnof = A3[:, 0:16384]
        gsil = A6[:, 0:16384]
        wgu = A5[:, 0:8192]
        ebm = A5[:, 8192:12288]
        gr_sb = A5[:, 12288:14336]
        gr2_sb = A5[:, 14336:16384]
        um_sb = A5[:, 16384:18432]
        wd_sb = A2
        maskfm = XNB[0:8, 0:2048]
        sload = A5[:, 0:2048]
        x6 = B1[:, 0:2048]
        o6 = B1[:, 2048:4096]

        s1x = p.sem("s1x"); s1v1 = p.sem("s1v1"); s1a = p.sem("s1a")
        s1v2 = p.sem("s1v2"); s1st = p.sem("s1st"); s1f = p.sem("s1f")
        s2w = p.sem("s2w"); s2t = p.sem("s2t"); s2mm = p.sem("s2mm")
        s2r = p.sem("s2r"); s2st = p.sem("s2st"); s2stat = p.sem("s2stat")
        s2a = p.sem("s2a"); s5v = p.sem("s5v"); s5m = p.sem("s5m")
        s3k = p.sem("s3k"); s3v = p.sem("s3v"); s3sc = p.sem("s3sc")
        s3ex = p.sem("s3ex"); s3pv = p.sem("s3pv"); s3ev = p.sem("s3ev")
        s3mx = p.sem("s3mx"); s3st = p.sem("s3st")
        s4 = p.sem("s4"); s4e = p.sem("s4e"); s4st = p.sem("s4st")
        s5 = p.sem("s5"); s5w = p.sem("s5w"); s5a = p.sem("s5a")
        s6x = p.sem("s6x"); s6s = p.sem("s6s"); s6v = p.sem("s6v"); s6st = p.sem("s6st")

        def dma(dst, src):
            return lambda e: e.dma_start(dst, src)

        for d_, s_ in [(ident[:], ident_in), (onesr32[:], onesr_in), (onesr16[:], onesr_in),
                       (sel8[:], sel8_in)]:
            p.step("gpsimd", dma(d_, s_), incs=[(s2w, 16)])
        p.step("gpsimd", dma(cos_sb[:], costab.rearrange("(t p) d -> p t d", p=128)), incs=[(s2w, 16)])
        p.step("gpsimd", dma(sin_sb[:], sintab.rearrange("(t p) d -> p t d", p=128)), incs=[(s2w, 16)])
        for dc in range(8):
            p.step("gpsimd", dma(wkv[:, dc * 2048:(dc + 1) * 2048],
                                 wkvT[dc * 128:(dc + 1) * 128, :]), incs=[(s2w, 16)])
        for dc in range(8):
            p.step("gpsimd", dma(wq_sb[:, dc * 1024:(dc + 1) * 1024],
                                 wqT[dc * 128:(dc + 1) * 128, :]), incs=[(s2w, 16)])
        for dc in range(8):
            p.step("gpsimd", dma(gate32[:, dc * 8:(dc + 1) * 8],
                                 gateT[dc * 128:(dc + 1) * 128, :]), incs=[(s2w, 16)])
        W_LOADS = p.n(s2w)
        p.step("vector", lambda e: e.tensor_copy(gatehi[:], gate32[:]),
               waits=[(s2w, W_LOADS)], incs=[(s2stat, 1)])
        p.step("vector", lambda e: e.tensor_tensor(gatelo[:], gate32[:], gatehi[:], ALU.subtract),
               waits=[(s2stat, 1)], incs=[(s2stat, 1)])
        p.step("vector", lambda e: e.memset(stat[:], 0.0), incs=[(s2stat, 1)])
        p.step("vector", lambda e: e.memset(eps_col[:], 1e-5), incs=[(s1a, 1)])

        xnf_loaded = {}
        xnf_free_w = {}

        def emit_xnf_load(g):
            slot = g % 2
            w = [(s1st, store_done[8 * g + 7])]
            if g >= 2:
                w.append(xnf_free_w[g - 2])
            for dc in range(8):
                p.step("sync", (lambda sl, gg, d: lambda e: e.dma_start(
                    xnf(sl, d), xn_dram[gg * GTOK:(gg + 1) * GTOK, d * 128:(d + 1) * 128],
                    transpose=True))(slot, g, dc),
                    waits=w if dc == 0 else [], incs=[(s2t, 16)])
            xnf_loaded[g] = p.n(s2t)

        store_done = {}
        xn_done = {}
        act_done = {}
        sqrt_done = {}
        lo_done = {}
        NSKEW = 2

        def p1_A(t):
            xs = (t % 4) * 1024
            w = [(s1a, act_done[t - 4])] if t >= 4 else []
            p.step("sync", (lambda xs_, t_: lambda e: e.dma_start(
                xin[:, xs_:xs_ + 1024], xbm[t_ * 128:(t_ + 1) * 128, :]))(xs, t),
                waits=w, incs=[(s1x, 16)])

        def p1_B(t):
            xs = (t % 4) * 1024
            ms = t % 8
            wb = [(s1x, 16 * (t + 1))]
            if t:
                wb.append((s1v1, 3 * t))
            if t >= 8:
                wb.append((s1a, act_done[t - 8]))
            p.step("vector", (lambda xs_: lambda e: e.bn_stats(bn6[:, 0:6], xin[:, xs_:xs_ + 512]))(xs),
                   waits=wb, incs=[(s1v1, 1)])
            p.step("vector", (lambda xs_: lambda e: e.bn_stats(bn6[:, 6:12], xin[:, xs_ + 512:xs_ + 1024]))(xs),
                   incs=[(s1v1, 1)])
            p.step("vector", (lambda ms_: lambda e: e.bn_aggr(mv[:, ms_ * 2:ms_ * 2 + 2], bn6[:]))(ms),
                   waits=[(s1v1, 3 * t + 2)], incs=[(s1v1, 1)])

        def p1_C(t):
            ms = t % 8
            w = [(s1v1, 3 * (t + 1)), (s1a, 1)]
            if t >= 8:
                w.append((s1v2, xn_done[t - 8]))
            p.step("scalar", (lambda ms_: lambda e: e.activation(
                rstd[:, ms_:ms_ + 1], mv[:, ms_ * 2 + 1:ms_ * 2 + 2], AF.Sqrt, bias=eps_col[:]))(ms),
                waits=w, incs=[(s1a, 1)])
            sqrt_done[t] = p.n(s1a)

        def p1_D(t):
            ms = t % 8
            p.step("vector", (lambda ms_: lambda e: e.reciprocal(rstd[:, ms_:ms_ + 1], rstd[:, ms_:ms_ + 1]))(ms),
                   waits=[(s1a, sqrt_done[t])], incs=[(s1v2, 1)])
            p.step("vector", (lambda ms_: lambda e: e.scalar_tensor_tensor(
                nmur[:, ms_:ms_ + 1], mv[:, ms_ * 2:ms_ * 2 + 1], -1.0, rstd[:, ms_:ms_ + 1],
                ALU.mult, ALU.mult))(ms),
                waits=[(s1v2, p.n(s1v2))], incs=[(s1v2, 1)])
            xn_done[t] = p.n(s1v2)

        def p1_E(t):
            own = t < OWN_T
            xs = (t % 4) * 1024
            ms = t % 8
            ns = (t % 4) * 1024
            wact = [(s1v2, xn_done[t])]
            if t >= 4:
                wact.append((s1st, store_done[t - 4]))
            p.step("scalar", (lambda xs_, ns_, ms_: lambda e: e.activation(
                XNB[:, ns_:ns_ + 1024], xin[:, xs_:xs_ + 1024], AF.Identity,
                bias=nmur[:, ms_:ms_ + 1], scale=rstd[:, ms_:ms_ + 1]))(xs, ns, ms),
                waits=wact, incs=[(s1a, 1)])
            if own:
                p.step("scalar", (lambda xs_, ms_: lambda e: e.activation(
                    xn32[:, 0:1024], xin[:, xs_:xs_ + 1024], AF.Identity,
                    bias=nmur[:, ms_:ms_ + 1], scale=rstd[:, ms_:ms_ + 1]))(xs, ms),
                    waits=([(s1f, lo_done[t - 1])] if t >= 1 else []),
                    incs=[(s1a, 1)])
            act_done[t] = p.n(s1a)

        def p1_F(t):
            if t < OWN_T:
                ns = (t % 4) * 1024
                p.step("gpsimd", (lambda ns_: lambda e: e.tensor_tensor(
                    xnlo_b[:, 0:1024], xn32[:, 0:1024], XNB[:, ns_:ns_ + 1024], ALU.subtract))(ns),
                    waits=[(s1a, act_done[t]), (s1st, store_done.get(t - 1, 0))],
                    incs=[(s1f, 1)])
            lo_done[t] = p.n(s1f)

        def p1_G(t):
            ns = (t % 4) * 1024
            own = t < OWN_T
            wst = [(s1a, act_done[t])]
            if own:
                wst.append((s1f, lo_done[t]))
            p.step("gpsimd", (lambda ns_, t_: lambda e: e.dma_start(
                xn_dram[t_ * 128:(t_ + 1) * 128, :], XNB[:, ns_:ns_ + 1024]))(ns, t),
                waits=wst, incs=[(s1st, 16)])
            if own:
                p.step("gpsimd", (lambda t_: lambda e: e.dma_start(
                    xnlo_dram[t_ * 128:(t_ + 1) * 128, :], xnlo_b[:, 0:1024]))(t),
                    incs=[(s1st, 16)])
            store_done[t] = p.n(s1st)

        for t in range(NT + NSKEW):
            if t < NT:
                p1_A(t)
                p1_B(t)
                p1_C(t)
                p1_D(t)
            tp_ = t - NSKEW
            if 0 <= tp_ < NT:
                p1_E(tp_)
                p1_F(tp_)
                p1_G(tp_)
                if tp_ == 7:
                    emit_xnf_load(0)
                if tp_ == 15:
                    emit_xnf_load(1)


        def emit_rope_tm(src_ps, ti, dst, sem, extra_w=()):
            c3 = cos_sb[:, ti * 64:(ti + 1) * 64].rearrange(
                "p (o x) -> p o x", o=1).broadcast_to([128, 16, 64])
            s3 = sin_sb[:, ti * 64:(ti + 1) * 64].rearrange(
                "p (o x) -> p o x", o=1).broadcast_to([128, 16, 64])
            ps3 = src_ps.rearrange("p (h x) -> p h x", x=64)
            d3 = dst.rearrange("p (h x) -> p h x", x=64)
            sc3 = kr32q.rearrange("p (h x) -> p h x", x=64)
            p.step("vector", (lambda a_, b_: lambda e: e.tensor_tensor(sc3, a_, b_, ALU.mult))(ps3, c3),
                   waits=list(extra_w), incs=[(sem, 1)])
            p.step("vector", (lambda a_, b_, d_: lambda e: e.tensor_tensor(
                d_[:, :, 0:32], a_[:, :, 32:64], b_[:, :, 0:32], ALU.mult))(ps3, s3, d3),
                incs=[(sem, 1)])
            p.step("vector", (lambda a_, b_, d_: lambda e: e.tensor_tensor(
                d_[:, :, 32:64], a_[:, :, 0:32], b_[:, :, 32:64], ALU.mult))(ps3, s3, d3),
                incs=[(sem, 1)])
            p.step("vector", (lambda d_: lambda e: e.tensor_tensor(d_, d_, kr32q, ALU.add))(dst),
                   waits=[(sem, p.n(sem))], incs=[(sem, 1)])

        def emit_stats(dst, acc_col, sem):
            p.step("gpsimd", (lambda d_: lambda e: e.tensor_tensor(ksq, d_, d_, ALU.mult))(dst),
                   waits=[(sem, p.n(sem)), (s2stat, p.n(s2stat))], incs=[(s2stat, 1)])
            p.step("vector", lambda e: e.reduce_sum(
                stat[:, 48:64], ksq.rearrange("p (h x) -> p h x", x=64), AX),
                waits=[(s2stat, p.n(s2stat))], incs=[(s2stat, 1)])
            p.step("vector", (lambda c: lambda e: e.tensor_tensor(
                stat[:, c:c + 16], stat[:, c:c + 16], stat[:, 48:64], ALU.max))(acc_col),
                waits=[(s2stat, p.n(s2stat))], incs=[(s2stat, 1)])

        kst_done = {}
        vst_done = {}
        qst_done = {}
        proj_wait_extra = {}

        def emit_proj_tile(ti, qpass):
            g, tt = divmod(ti, 8)
            slot = ti % 2
            pb = bank(slot * 4, 4) if not qpass else bank(slot * 2, 2)
            w0 = [(s2t, xnf_loaded[g]), (s2w, W_LOADS)]
            key = ("q" if qpass else "kv", ti)
            w0 += proj_wait_extra.get(key, [])
            nmm = 2 if qpass else 4
            wmat = wq_sb if qpass else wkv
            wwid = 1024 if qpass else 2048
            for dc in range(8):
                p.step("tensor", (lambda sl, d, t_: lambda e: e.ldweights(
                    xnf(sl, d)[:, t_ * 128:t_ * 128 + 128]))(g % 2, dc, tt),
                    waits=w0 if dc == 0 else [])
                for sub in range(nmm):
                    p.step("tensor", (lambda sl, d, t_, sb_, b_, wm, wd_: lambda e: e.matmul(
                        b_[:, sb_ * 512:(sb_ + 1) * 512],
                        xnf(sl, d)[:, t_ * 128:t_ * 128 + 128],
                        wm[:, d * wd_ + sb_ * 512:d * wd_ + (sb_ + 1) * 512],
                        start=(d == 0), stop=(d == 7)))(g % 2, dc, tt, sub, pb, wmat, wwid),
                        incs=[(s2mm, 1)] if (dc == 7 and sub == nmm - 1) else [])
            mm_done = p.n(s2mm)
            if qpass:
                dst = q_ev[:, slot * 1024:slot * 1024 + 1024]
                ew = [(s2mm, mm_done)]
                if ti >= 2:
                    ew.append((s2st, qst_done.get(ti - 2, 0)))
                emit_rope_tm(pb[:, 0:1024], ti, dst, s2r, ew)
                emit_stats(dst, 16, s2r)
                p.step("gpsimd", (lambda d_, t_: lambda e: e.dma_start(
                    q_dram[t_ * 128:(t_ + 1) * 128, :], d_))(dst, ti),
                    waits=[(s2r, p.n(s2r)), (s2stat, p.n(s2stat))], incs=[(s2st, 16)])
                qst_done[ti] = p.n(s2st)
            else:
                kdst = k_ev[:, slot * 1024:slot * 1024 + 1024]
                ew = [(s2mm, mm_done)]
                if ti >= 2:
                    ew.append((s2st, kst_done.get(ti - 2, 0)))
                emit_rope_tm(pb[:, 0:1024], ti, kdst, s2r, ew)
                emit_stats(kdst, 0, s2r)
                p.step("gpsimd", (lambda d_, t_: lambda e: e.dma_start(
                    k_dram[t_ * 128:(t_ + 1) * 128, :], d_))(kdst, ti),
                    waits=[(s2r, p.n(s2r)), (s2stat, p.n(s2stat))], incs=[(s2st, 16)])
                kst_done[ti] = p.n(s2st)
                vdst = v_ev[:, slot * 1024:slot * 1024 + 1024]
                wv_ = [(s2mm, mm_done), (s2a, p.n(s2a))]
                if ti >= 2:
                    wv_.append((s2st, vst_done.get(ti - 2, 0)))
                p.step("scalar", (lambda b_, d_: lambda e: e.copy(d_, b_[:, 1024:2048]))(pb, vdst),
                       waits=wv_, incs=[(s2a, 1)])
                p.step("gpsimd", (lambda d_, t_: lambda e: e.dma_start(
                    v_dram[t_ * 128:(t_ + 1) * 128, :], d_))(vdst, ti),
                    waits=[(s2a, p.n(s2a))], incs=[(s2st, 16)])
                vst_done[ti] = p.n(s2st)

        rope_after = {}
        vact_after = {}
        q_rope_end = 0
        for ti in range(NT):
            if ti >= 2:
                rwait = rope_after[("kv", ti - 2)]
                if ti in (16, 17):
                    rwait = max(rwait, q_rope_end)
                proj_wait_extra[("kv", ti)] = [(s2r, rwait), (s2a, vact_after[ti - 2])]
            emit_proj_tile(ti, False)
            rope_after[("kv", ti)] = p.n(s2r)
            vact_after[ti] = p.n(s2a)
            if ti == 15:
                for qi in range(OWN_T):
                    if qi >= 2:
                        proj_wait_extra[("q", qi)] = [(s2r, rope_after[("q", qi - 2)])]
                    else:
                        proj_wait_extra[("q", qi)] = [(s2r, rope_after[("kv", 15)]),
                                                      (s2a, vact_after[15])]
                    emit_proj_tile(qi, True)
                    rope_after[("q", qi)] = p.n(s2r)
                q_rope_end = p.n(s2r)
                xnf_free_w[0] = (s2mm, p.n(s2mm))
                xnf_free_w[1] = (s2mm, p.n(s2mm))
                emit_xnf_load(2)
                emit_xnf_load(3)
            if ti >= 16 and ti % 8 == 7:
                g = ti // 8
                xnf_free_w[g] = (s2mm, p.n(s2mm))
                if g + 2 < NG:
                    emit_xnf_load(g + 2)
        LAST_ROPE = p.n(s2r)
        LAST_STAT = p.n(s2stat)
        LAST_MM = p.n(s2mm)
        LAST_VA = p.n(s2a)
        KV_ST = p.n(s2st)

        if "k" in taps:
            p.step("gpsimd", lambda e: e.dma_start(taps["k"], k_dram), waits=[(s2st, KV_ST)], incs=[(s2st, 16)])
        if "v" in taps:
            p.step("gpsimd", lambda e: e.dma_start(taps["v"], v_dram), waits=[(s2st, KV_ST)], incs=[(s2st, 16)])
        if "q" in taps:
            p.step("gpsimd", lambda e: e.dma_start(taps["q"], q_dram), waits=[(s2st, KV_ST)], incs=[(s2st, 16)])
        if "xn" in taps:
            p.step("gpsimd", lambda e: e.dma_start(taps["xn"], xn_dram), waits=[(s1st, store_done[NT - 1])], incs=[(s2st, 16)])

        XHI = [A2[:, dc * 2048:(dc + 1) * 2048] for dc in range(8)]
        XLO = [A1[:, dc * 2048:(dc + 1) * 2048] for dc in range(8)]
        for dc in range(8):
            p.step("sync", (lambda d: lambda e: e.dma_start(
                XHI[d], xn_dram[0:TOK, d * 128:(d + 1) * 128], transpose=True))(dc),
                waits=[(s2mm, LAST_MM), (s1st, store_done[NT - 1]), (s2a, LAST_VA)] if dc == 0 else [],
                incs=[(s2t, 16)])
        for dc in range(8):
            p.step("sync", (lambda d: lambda e: e.dma_start(
                XLO[d], xnlo_dram[:, d * 128:(d + 1) * 128], transpose=True))(dc),
                waits=[(s2mm, LAST_MM)] if dc == 0 else [],
                incs=[(s2t, 16)])
        GXL = p.n(s2t)
        LPS = PS[0:8, 0:2048]
        combos = [(XHI, gatehi), (XHI, gatelo), (XLO, gatehi)]
        first_mm = True
        for ci, (xsrc, gw) in enumerate(combos):
            for dc in range(8):
                w = []
                if first_mm:
                    w = [(s2t, GXL), (s2stat, 2), (s2w, W_LOADS), (s2r, LAST_ROPE),
                         (s2a, LAST_VA)]
                    first_mm = False
                p.step("tensor", (lambda gw_, d: lambda e: e.ldweights(gw_[:, d * 8:(d + 1) * 8]))(gw, dc),
                       waits=w)
                for sub in range(4):
                    st_ = (ci == 0 and dc == 0)
                    sp_ = (ci == 2 and dc == 7)
                    p.step("tensor", (lambda gw_, d, sb_, sr, st__, sp__: lambda e: e.matmul(
                        LPS[:, sb_ * 512:(sb_ + 1) * 512], gw_[:, d * 8:(d + 1) * 8],
                        sr[d][:, sb_ * 512:(sb_ + 1) * 512],
                        start=st__, stop=sp__))(gw, dc, sub, xsrc, st_, sp_),
                        incs=[(s2mm, 1)] if (ci == 2 and dc == 7 and sub == 3) else [])
        GATE_MM = p.n(s2mm)
        p.step("vector", lambda e: e.tensor_copy(lsb[:], LPS),
               waits=[(s2mm, GATE_MM), (s1v2, xn_done[NT - 1])], incs=[(s5v, 1)])
        LSB_C = p.n(s5v)
        if "logit" in taps:
            p.step("gpsimd", lambda e: e.dma_start(taps["logit"], lsb), waits=[(s5v, LSB_C)], incs=[(s2st, 16)])
        for tt in range(16):
            p.step("tensor", (lambda t_: lambda e: e.transpose(
                bank(0)[0:128, 0:8], lsb[0:8, t_ * 128:(t_ + 1) * 128], ident[0:8, 0:8]))(tt),
                waits=[(s5v, LSB_C + tt)], incs=[(s2mm, 1)])
            p.step("vector", (lambda t_: lambda e: e.tensor_copy(
                ltm[:, t_ * 8:(t_ + 1) * 8], bank(0)[0:128, 0:8]))(tt),
                waits=[(s2mm, p.n(s2mm))], incs=[(s5v, 1)])
        p.step("vector", lambda e: e.reduce_max(
            rmaxtm[:, 0:16], ltm.rearrange("p (t e) -> p t e", e=8), AX),
            waits=[(s5v, p.n(s5v))], incs=[(s5v, 1)])
        RMT = p.n(s5v)
        for tt in range(16):
            p.step("tensor", (lambda t_: lambda e: e.transpose(
                bank(1)[0:1, 0:128], rmaxtm[:, t_:t_ + 1], ident[:]))(tt),
                waits=[(s5v, RMT + tt)], incs=[(s2mm, 1)])
            p.step("vector", (lambda t_: lambda e: e.tensor_copy(
                rmaxrow[0:1, t_ * 128:(t_ + 1) * 128], bank(1)[0:1, 0:128]))(tt),
                waits=[(s2mm, p.n(s2mm))], incs=[(s5v, 1)])
        RMROW = p.n(s5v)
        for sub in range(4):
            p.step("tensor", (lambda sb_: lambda e: e.matmul(
                PS[0:8, sb_ * 512:(sb_ + 1) * 512], onesr32[0:1, 0:8],
                rmaxrow[0:1, sb_ * 512:(sb_ + 1) * 512], start=True, stop=True))(sub),
                waits=[(s5v, RMROW)] if sub == 0 else [], incs=[(s2mm, 1)])
            p.step("vector", (lambda sb_: lambda e: e.tensor_tensor(
                maskfm[0:8, sb_ * 512:(sb_ + 1) * 512], lsb[0:8, sb_ * 512:(sb_ + 1) * 512],
                PS[0:8, sb_ * 512:(sb_ + 1) * 512], ALU.is_ge))(sub),
                waits=[(s2mm, p.n(s2mm))], incs=[(s5m, 1)])
        MASK_DONE = p.n(s5m)
        if "mask" in taps:
            p.step("gpsimd", lambda e: e.dma_start(taps["mask"], maskfm[:]),
                   waits=[(s5m, MASK_DONE)], incs=[(s2st, 16)])

        p.step("tensor", lambda e: e.transpose(bank(2)[0:16, 0:128], stat[:, 0:16], ident[:]),
               waits=[(s2stat, LAST_STAT), (s5m, MASK_DONE)], incs=[(s2mm, 1)])
        p.step("tensor", lambda e: e.transpose(bank(2)[0:16, 128:256], stat[:, 16:32], ident[:]),
               incs=[(s2mm, 1)])
        TRC = p.n(s2mm)
        p.step("vector", lambda e: e.reduce_max(stat[0:16, 32:33], bank(2)[0:16, 0:128], AX),
               waits=[(s2mm, TRC), (s2stat, p.n(s2stat))], incs=[(s2stat, 1)])
        p.step("vector", lambda e: e.reduce_max(stat[0:16, 33:34], bank(2)[0:16, 128:256], AX),
               incs=[(s2stat, 1)])
        p.step("vector", lambda e: e.tensor_tensor(
            stat[0:16, 34:35], stat[0:16, 32:33], stat[0:16, 33:34], ALU.mult),
            waits=[(s2stat, p.n(s2stat))], incs=[(s2stat, 1)])
        p.step("scalar", lambda e: e.activation(stat[0:16, 35:36], stat[0:16, 34:35], AF.Sqrt),
               waits=[(s2stat, p.n(s2stat))], incs=[(s2stat, 1)])
        p.step("tensor", lambda e: e.transpose(bank(2)[0:1, 256:272], stat[0:16, 35:36], ident[0:16, 0:16]),
               waits=[(s2stat, p.n(s2stat))], incs=[(s2mm, 1)])
        p.step("vector", lambda e: e.reduce_max(stat[0:1, 36:37], bank(2)[0:1, 256:272], AX),
               waits=[(s2mm, p.n(s2mm)), (s2stat, p.n(s2stat))], incs=[(s2stat, 1)])
        p.step("vector", lambda e: e.tensor_scalar(
            stat[0:1, 37:38], stat[0:1, 36:37], -0.1275, None, ALU.mult),
            waits=[(s2stat, p.n(s2stat))], incs=[(s2stat, 1)])
        p.step("tensor", lambda e: e.matmul(bank(0)[:, 0:1], onesr32[:], stat[0:1, 37:38],
                                            start=True, stop=True),
               waits=[(s2stat, p.n(s2stat))], incs=[(s2mm, 1)])
        p.step("vector", lambda e: e.tensor_copy(bias_col[:], bank(0)[:, 0:1]),
               waits=[(s2mm, p.n(s2mm))], incs=[(s2stat, 1)])
        BIAS_DONE = p.n(s2stat)
        if "bias" in taps:
            p.step("gpsimd", lambda e: e.dma_start(taps["bias"], bias_col[:]),
                   waits=[(s2stat, BIAS_DONE)], incs=[(s2st, 16)])

        for pr in range(8):
            p.step("sync", (lambda pp: lambda e: e.dma_start(
                q_sb[:, pp * 2048:(pp + 1) * 2048],
                q_dram[0:TOK, pp * 128:(pp + 1) * 128], transpose=True))(pr),
                waits=[(s2st, KV_ST), (s2t, GXL)] if pr == 0 else [], incs=[(s2t, 16)])
        QSB_LD = p.n(s2t)
        p.step("gpsimd", lambda e: e.memset(vt_sb[:], 1.0),
               waits=[(s2r, LAST_ROPE), (s2stat, BIAS_DONE)], incs=[(s3v, 1)])
        VONES = p.n(s3v)

        blk_list = [(s, hg) for s in range(NSTRIP) for hg in range(NHG)]
        kch_ready = {}
        vch_ready = {}
        kch_free_w = {}
        load_order = [(s, hg, cc) for (s, hg) in blk_list for cc in range(4)]
        li = [0]

        def emit_kv_load(idx):
            s_, hg, cc = load_order[idx]
            cid = idx
            slot = cid % 4
            w = [(s2st, KV_ST), (s3v, VONES), (s2mm, GATE_MM), (s2t, QSB_LD)]
            if cid >= 4:
                w.append(kch_free_w[cid - 4])
            for pr in range(2):
                gpr = hg * 2 + pr
                p.step("sync", (lambda sl, gp, c_, pp: lambda e: e.dma_start(
                    kt_sb[:, sl * 4096 + pp * 2048:sl * 4096 + pp * 2048 + 2048],
                    k_dram[c_ * 2048:(c_ + 1) * 2048, gp * 128:(gp + 1) * 128],
                    transpose=True))(slot, gpr, cc, pr),
                    waits=w if pr == 0 else [], incs=[(s3k, 16)])
            kch_ready[(s_, hg, cc)] = p.n(s3k)
            for hx in range(4):
                wv_ = []
                if hx == 0:
                    wv_ = [(s3v, VONES)]
                    if cid >= 4:
                        wv_.append(kch_free_w[cid - 4])
                p.step("sync", (lambda sl, h_, c_, hx_: lambda e: e.dma_start(
                    vt_sb[:, 0:16896].rearrange("p (sl2 x) -> p sl2 x", sl2=4)[:, sl, 0:4160].rearrange("p (k h e) -> p k h e", k=16, h=4)[:, :, hx_, 0:64],
                    v_dram.rearrange("(ck p) (h e) -> p ck h e", p=128, h=16)[:, c_ * 16:(c_ + 1) * 16, h_ * 4 + hx_, :]))(slot, hg, cc, hx),
                    waits=wv_, incs=[(s3v, 16)])
            vch_ready[(s_, hg, cc)] = p.n(s3v)

        for _ in range(4):
            emit_kv_load(li[0]); li[0] += 1

        sc_done = {}
        ex_done = {}
        pv_done = {}
        ev_done = {}
        lb_done = {}
        ost_done = {}
        pending_bcast = [None]
        pv_next = [0]
        j = 0

        def emit_pv(pj):
            bi, rem = divmod(pj, NKT * 2)
            s_, hg = blk_list[bi]
            pkti, ppr = divmod(rem, 2)
            cc, kk = divmod(pkti, 16)
            cid = bi * 4 + cc
            slot = cid % 4
            pb = (pj % 8) * 1024
            first = pkti == 0
            last = pkti == NKT - 1
            w = [(s3ex, ex_done[pj])]
            if pkti == 0 and bi > 0:
                w.append((s3ev, lb_done[bi - 1][ppr]))
            if first:
                w.append((s3v, vch_ready[(s_, hg, cc)]))
            for h in range(2):
                hh = ppr * 2 + h
                vo = slot * 4224 + kk * 260 + hh * 65
                p.step("tensor", (lambda v_: lambda e: e.ldweights(vt_sb[:, v_:v_ + 128]))(vo),
                       waits=w if h == 0 else [])
                p.step("tensor", (lambda v_, hh_, pb_, f_, l_: lambda e: e.matmul(
                    PS[0:128, (4 + hh_) * 512:(4 + hh_) * 512 + 512],
                    vt_sb[:, v_:v_ + 128], p_sb[:, pb_ + (hh_ % 2) * 512:pb_ + (hh_ % 2) * 512 + 512],
                    start=f_, stop=l_))(vo, hh, pb, first, last),
                    incs=[(s3pv, 1)] if h == 1 else [])
            pv_done[pj] = p.n(s3pv)
            if kk == 15 and ppr == 1:
                kch_free_w[cid] = (s3pv, p.n(s3pv))
                if li[0] < len(load_order):
                    emit_kv_load(li[0]); li[0] += 1

        for bi, (s_, hg) in enumerate(blk_list):
            for kti in range(NKT):
                cc, kk = divmod(kti, 16)
                for pr in range(2):
                    so = (j % 2) * 1024
                    w = []
                    if kk == 0 and pr == 0:
                        w.append((s3k, kch_ready[(s_, hg, cc)]))
                    if j == 0:
                        w += [(s5m, MASK_DONE), (s2stat, BIAS_DONE)]
                    if j >= 2:
                        w.append((s3ex, ex_done[j - 2]))
                    slot = (bi * 4 + cc) % 4
                    ko = slot * 4096 + pr * 2048 + kk * 128
                    qo = (hg * 2 + pr) * 2048 + s_ * 512
                    p.step("tensor", (lambda k_: lambda e: e.ldweights(
                        kt_sb[0:64, k_:k_ + 128], tile_position=(0, 0)))(ko), waits=w)
                    p.step("tensor", (lambda k_: lambda e: e.ldweights(
                        kt_sb[64:128, k_:k_ + 128], tile_position=(64, 0)))(ko))
                    p.step("tensor", (lambda k_, q_, so_: lambda e: e.matmul(
                        PS[:, so_:so_ + 512], kt_sb[0:64, k_:k_ + 128], q_sb[0:64, q_:q_ + 512],
                        start=True, stop=True, tile_position=(0, 0)))(ko, qo, so))
                    p.step("tensor", (lambda k_, q_, so_: lambda e: e.matmul(
                        PS[:, so_ + 512:so_ + 1024], kt_sb[64:128, k_:k_ + 128], q_sb[64:128, q_:q_ + 512],
                        start=True, stop=True, tile_position=(64, 0)))(ko, qo, so),
                        incs=[(s3sc, 1)])
                    sc_done[j] = p.n(s3sc)
                    we = [(s3sc, sc_done[j])]
                    if j == 0:
                        we.append((s2stat, BIAS_DONE))
                    if j >= 8:
                        we.append((s3pv, pv_done[j - 8]))
                    p.step("scalar", (lambda so_, pb_: lambda e: e.activation(
                        p_sb[:, pb_:pb_ + 1024], PS[:, so_:so_ + 1024], AF.Exp,
                        bias=bias_col[:], scale=0.125))(so, (j % 8) * 1024),
                        waits=we, incs=[(s3ex, 1)])
                    ex_done[j] = p.n(s3ex)
                    local = j % (NKT * 2)
                    if local == 7 and pending_bcast[0] is not None:
                        pending_bcast[0]()
                        pending_bcast[0] = None
                    if local >= 7 or j < NKT * 2:
                        while pv_next[0] < j:
                            emit_pv(pv_next[0])
                            pv_next[0] += 1
                    j += 1
            while pv_next[0] < j:
                emit_pv(pv_next[0])
                pv_next[0] += 1
            w = [(s3pv, pv_done[j - 1])]
            if bi > 0:
                w.append((s3st, ost_done[bi - 1]))
            p.step("vector", lambda e: e.tensor_copy(orawB[0:65, :], PS[0:65, 2048:4096]),
                   waits=w, incs=[(s3ev, 1)])
            LR = p.n(s3ev)
            def mk_bcast(CP_, bi_, hg_, ss_):
                def emit_bc():
                    lb_pair = []
                    for hh in range(4):
                        p.step("scalar", (lambda h_: lambda e: e.activation(
                            lrow[0:1, h_ * 512:h_ * 512 + 512],
                            orawB[64:65, h_ * 512:h_ * 512 + 512], AF.Ln))(hh),
                            waits=[(s3ev, CP_)] if hh == 0 else [], incs=[(s3mx, 1)])
                        p.step("scalar", (lambda h_: lambda e: e.activation(
                            lrow16[0:1, h_ * 512:h_ * 512 + 512],
                            lrow[0:1, h_ * 512:h_ * 512 + 512], AF.Exp, scale=-1.0))(hh),
                            waits=[(s3mx, p.n(s3mx))], incs=[(s3mx, 1)])
                        p.step("tensor", (lambda h_: lambda e: e.matmul(
                            PS[0:64, (4 + h_) * 512:(4 + h_) * 512 + 512],
                            onesr16[0:1, 0:64], lrow16[0:1, h_ * 512:h_ * 512 + 512],
                            start=True, stop=True))(hh),
                            waits=[(s3mx, p.n(s3mx))], incs=[(s3mx, 1)])
                        p.step("vector", (lambda h_: lambda e: e.tensor_copy(
                            lb4[0:64, h_ * 512:h_ * 512 + 512],
                            PS[0:64, (4 + h_) * 512:(4 + h_) * 512 + 512]))(hh),
                            waits=[(s3mx, p.n(s3mx))], incs=[(s3ev, 1)])
                        if hh in (1, 3):
                            lb_pair.append(p.n(s3ev))
                    lb_done[bi_] = tuple(lb_pair)
                    p.step("vector", lambda e: e.tensor_tensor(
                        o_tmp[0:64, :], orawB[0:64, :], lb4[0:64, :], ALU.mult),
                        waits=[(s3ev, lb_pair[1])], incs=[(s3ev, 1)])
                    ev_done[bi_] = p.n(s3ev)
                    p.step("gpsimd", (lambda: lambda e: e.dma_start(
                        o_dram.rearrange("(hh q) t -> q hh t", q=64)[:, hg_ * 4:(hg_ + 1) * 4, ss_ * 512:(ss_ + 1) * 512],
                        o_tmp[0:64, :].rearrange("q (hh t) -> q hh t", t=512)))(),
                        waits=[(s3ev, ev_done[bi_])], incs=[(s3st, 16)])
                    ost_done[bi_] = p.n(s3st)
                return emit_bc
            pending_bcast[0] = mk_bcast(LR, bi, hg, s_)
            if bi == len(blk_list) - 1:
                pending_bcast[0]()
                pending_bcast[0] = None

        ATT_DONE = p.n(s3ev)
        EXP_ALL = p.n(s3ex)
        PV_ALL = p.n(s3pv)
        O_ST = p.n(s3st)
        if "o" in taps:
            p.step("gpsimd", lambda e: e.dma_start(taps["o"], o_dram), waits=[(s3st, O_ST)], incs=[(s2st, 16)])

        for dc in range(8):
            p.step("gpsimd", (lambda d: lambda e: e.dma_start(
                wout[:, d * 1024:(d + 1) * 1024], woutT[d * 128:(d + 1) * 128, :]))(dc),
                waits=[(s3ev, ATT_DONE), (s3pv, PV_ALL)] if dc == 0 else [], incs=[(s2st, 16)])
        for dc in range(8):
            p.step("sync", (lambda d: lambda e: e.dma_start(
                o_sb[:, d * 2048:(d + 1) * 2048], o_dram[d * 128:(d + 1) * 128, :]))(dc),
                waits=[(s3st, O_ST)] if dc == 0 else [], incs=[(s2st, 16)])
        WOUT_LD = p.n(s2st)

        s4_ev = {}
        for tt in range(16):
            slot = tt % 2
            sb4 = bank(slot * 2, 2)
            w0 = [(s2st, WOUT_LD), (s3ex, EXP_ALL)]
            if tt >= 2:
                w0.append((s4e, s4_ev[tt - 2]))
            for dc in range(8):
                p.step("tensor", (lambda d, t_: lambda e: e.ldweights(
                    o_sb[:, d * 2048 + t_ * 128:d * 2048 + t_ * 128 + 128]))(dc, tt),
                    waits=w0 if dc == 0 else [])
                for half in range(2):
                    p.step("tensor", (lambda d, h, b_, t_: lambda e: e.matmul(
                        b_[:, h * 512:(h + 1) * 512],
                        o_sb[:, d * 2048 + t_ * 128:d * 2048 + t_ * 128 + 128],
                        wout[:, d * 1024 + h * 512:d * 1024 + (h + 1) * 512],
                        start=(d == 0), stop=(d == 7)))(dc, half, sb4, tt),
                        incs=[(s4, 1)] if dc == 7 and half == 1 else [])
            wev = [(s4, p.n(s4))]
            if tt >= 2:
                wev.append((s4st, 16 * (tt - 1)))
            p.step("vector", (lambda sl, b_: lambda e: e.tensor_copy(
                A5[:, 16384 + sl * 1024:16384 + sl * 1024 + 1024], b_))(slot, sb4),
                waits=wev, incs=[(s4e, 1)])
            s4_ev[tt] = p.n(s4e)
            p.step("gpsimd", (lambda sl, t_: lambda e: e.dma_start(
                s_dram[t_ * 128:(t_ + 1) * 128, :], A5[:, 16384 + sl * 1024:16384 + sl * 1024 + 1024]))(slot, tt),
                waits=[(s4e, s4_ev[tt])], incs=[(s4st, 16), (s2st, 16)])
        S_ST = p.n(s4st)
        P4_EV = p.n(s4e)
        if "s" in taps:
            p.step("gpsimd", lambda e: e.dma_start(taps["s"], s_dram), waits=[(s4st, S_ST)], incs=[(s2st, 16)])

        for dc in range(8):
            p.step("sync", (lambda d: lambda e: e.dma_start(
                xnof[:, d * 2048:(d + 1) * 2048],
                xn_dram[0:TOK, d * 128:(d + 1) * 128], transpose=True))(dc),
                waits=[(s4e, s4_ev[15])] if dc == 0 else [], incs=[(s2t, 16)])
        XNOF_LD = p.n(s2t)

        for rc in range(16):
            p.step("gpsimd", (lambda r_: lambda e: e.dma_start(
                wd_sb[:, r_ * 1024:(r_ + 1) * 1024],
                wd_in[r_ * 128:(r_ + 1) * 128, :]))(rc),
                waits=[(s3pv, PV_ALL), (s3ev, ATT_DONE)] if rc == 0 else [], incs=[(s5w, 16)])
        WD_LD = p.n(s5w)

        wgu_ld = {}
        wgu_free = {}

        def emit_wgu_load(key):
            half, e_ = divmod(key, NE)
            slot = key % 2
            w = []
            if key == 0:
                w = [(s5w, WD_LD)]
            if key >= 2:
                w.append(wgu_free[key - 2])
            for op, win in ((0, wg_in), (1, wu_in)):
                for ch in range(8):
                    p.step("gpsimd", (lambda sl, o_, c_, wi, ee: lambda e: e.dma_start(
                        wgu[:, sl * 4096 + o_ * 2048 + c_ * 256:sl * 4096 + o_ * 2048 + (c_ + 1) * 256],
                        wi[ee * 1024 + c_ * 128:ee * 1024 + (c_ + 1) * 128, :]))(slot, op, ch, win, e_),
                        waits=(w if (op == 0 and ch == 0) else []), incs=[(s5w, 16)])
            wgu_ld[key] = p.n(s5w)

        emit_wgu_load(0)
        emit_wgu_load(1)

        gu_psum_free = {}
        ebm_done = {}
        inter_done = {}
        down_ev = {}
        oui = 0
        for half in range(2):
            hcol = half * 1024
            for e_ in range(NE):
                key = half * NE + e_
                eslot = key % 2
                gslot = oui % 2
                gb = bank(gslot * 4, 4)
                w = [(s5m, MASK_DONE), (s5w, wgu_ld[key]), (s4st, S_ST), (s2t, XNOF_LD)]
                if key == 0:
                    w += [(s3ev, ATT_DONE), (s4e, P4_EV)]
                if half == 1 and e_ == 0:
                    w.append((s5a, down_ev[(0, 7)]))
                if oui >= 2:
                    w.append(gu_psum_free[oui - 2])
                for sub in range(2):
                    p.step("tensor", (lambda sb_, b_, ee, hc: lambda e: e.matmul(
                        b_[:, sb_ * 512:(sb_ + 1) * 512], sel8[:, ee * 128:ee * 128 + 128],
                        maskfm[0:8, hc + sb_ * 512:hc + (sb_ + 1) * 512],
                        start=True, stop=True))(sub, gb, e_, hcol),
                        waits=w if sub == 0 else [], incs=[(s2mm, 1)] if sub == 1 else [])
                mb_mm = p.n(s2mm)
                p.step("vector", (lambda sl, b_: lambda e: e.tensor_copy(
                    ebm[:, sl * 2048:sl * 2048 + 1024], b_[:, 0:1024]))(eslot, gb),
                    waits=[(s2mm, mb_mm)], incs=[(s5v, 1)])
                ebm_done[key] = p.n(s5v)
                gu_mm = {}
                for op in range(2):
                    slot = oui % 2
                    ob = bank(slot * 4, 4)
                    w0 = []
                    if op == 0:
                        w0.append((s5v, ebm_done[key]))
                    else:
                        if oui >= 2:
                            w0.append(gu_psum_free[oui - 2])
                        if half == 1 and e_ == 0:
                            w0.append((s5a, down_ev[(0, 7)]))
                    for Mt in range(2):
                        for dc in range(8):
                            p.step("tensor", (lambda sl_, o_, c_, kk_: lambda e: e.ldweights(
                                wgu[:, (kk_ % 2) * 4096 + o_ * 2048 + c_ * 256 + sl_ * 128:
                                       (kk_ % 2) * 4096 + o_ * 2048 + c_ * 256 + sl_ * 128 + 128]))(Mt, op, dc, key),
                                waits=w0 if (Mt == 0 and dc == 0) else [])
                            for sub in range(2):
                                p.step("tensor", (lambda o_, c_, m_, sb_, b_, kk_, hc: lambda e: e.matmul(
                                    b_[:, m_ * 1024 + sb_ * 512:m_ * 1024 + (sb_ + 1) * 512],
                                    wgu[:, (kk_ % 2) * 4096 + o_ * 2048 + c_ * 256 + m_ * 128:
                                           (kk_ % 2) * 4096 + o_ * 2048 + c_ * 256 + m_ * 128 + 128],
                                    xnof[:, c_ * 2048 + hc + sb_ * 512:c_ * 2048 + hc + (sb_ + 1) * 512],
                                    start=(c_ == 0), stop=(c_ == 7)))(op, dc, Mt, sub, ob, key, hcol),
                                    incs=[(s5, 1)] if (Mt == 1 and dc == 7 and sub == 1) else [])
                    gu_mm[op] = p.n(s5)
                    if op == 0:
                        wr = [(s5, gu_mm[0]), (s5a, p.n(s5a))]
                        if key >= 1:
                            wr.append((s5v, inter_done[key - 1]))
                        p.step("scalar", (lambda b_: lambda e: e.activation(
                            gr_sb[:, 0:2048], b_[:, 0:2048], AF.Relu))(ob),
                            waits=wr, incs=[(s5a, 1)])
                        gu_psum_free[oui] = (s5a, p.n(s5a))
                    else:
                        for Mt in range(2):
                            p.step("vector", (lambda b_, sl, m_: lambda e: e.tensor_tensor(
                                um_sb[:, m_ * 1024:m_ * 1024 + 1024],
                                b_[:, m_ * 1024:m_ * 1024 + 1024],
                                ebm[:, sl * 2048:sl * 2048 + 1024], ALU.mult))(ob, eslot, Mt),
                                waits=([(s5, gu_mm[1]), (s5v, p.n(s5v))] if Mt == 0 else []),
                                incs=[(s5v, 1)])
                        gu_psum_free[oui] = (s5v, p.n(s5v))
                    oui += 1
                p.step("vector", lambda e: e.tensor_tensor(
                    gr2_sb[:, 0:2048], gr_sb[:, 0:2048], gr_sb[:, 0:2048], ALU.mult),
                    waits=[(s5a, p.n(s5a)), (s5v, p.n(s5v))], incs=[(s5v, 1)])
                p.step("vector", (lambda ee: lambda e: e.tensor_tensor(
                    inter[:, ee * 2048:ee * 2048 + 2048], gr2_sb[:, 0:2048], um_sb[:, 0:2048], ALU.mult))(e_),
                    waits=[(s5v, p.n(s5v))], incs=[(s5v, 1)])
                inter_done[key] = p.n(s5v)
                wgu_free[key] = (s5, gu_mm[1])
                nk = key + 2
                if nk < 2 * NE:
                    emit_wgu_load(nk)
            for tt in range(8):
                slot = tt % 2
                db = bank(slot * 2, 2)
                w0 = [(s5v, inter_done[half * NE + NE - 1]), (s5w, WD_LD)]
                if oui >= 2:
                    w0.append(gu_psum_free[oui - 2])
                    w0.append(gu_psum_free[oui - 1])
                if tt >= 2:
                    w0.append((s5a, down_ev[(half, tt - 2)]))
                for rc in range(16):
                    p.step("tensor", (lambda r_, t_: lambda e: e.ldweights(
                        inter[:, r_ * 1024 + t_ * 128:r_ * 1024 + t_ * 128 + 128]))(rc, tt),
                        waits=w0 if rc == 0 else [])
                    for hf in range(2):
                        p.step("tensor", (lambda r_, t_, h_, b_: lambda e: e.matmul(
                            b_[:, h_ * 512:(h_ + 1) * 512],
                            inter[:, r_ * 1024 + t_ * 128:r_ * 1024 + t_ * 128 + 128],
                            wd_sb[:, r_ * 1024 + h_ * 512:r_ * 1024 + (h_ + 1) * 512],
                            start=(r_ == 0), stop=(r_ == 15)))(rc, tt, hf, db),
                            incs=[(s5, 1)] if rc == 15 and hf == 1 else [])
                p.step("scalar", (lambda b_, t_, hf_: lambda e: e.activation(
                    gsil[:, (hf_ * 8 + t_) * 1024:(hf_ * 8 + t_) * 1024 + 1024],
                    b_[:, 0:1024], AF.Silu))(db, tt, half),
                    waits=[(s5, p.n(s5)), (s5a, p.n(s5a))] +
                          ([(s4e, P4_EV), (s2st, WOUT_LD)] if (half == 0 and tt == 0) else []),
                    incs=[(s5a, 1)])
                down_ev[(half, tt)] = p.n(s5a)
        FFN_DONE = p.n(s5a)
        if "g" in taps:
            p.step("gpsimd", lambda e: e.dma_start(taps["g"], gsil), waits=[(s5a, FFN_DONE)], incs=[(s2st, 16)])

        out_done = {}
        for tt in range(16):
            slot = tt % 2
            wx = [(s6v, out_done[tt - 2])] if tt >= 2 else [(s4st, S_ST)]
            p.step("sync", (lambda sl, t_: lambda e: e.dma_start(
                x6[:, sl * 1024:sl * 1024 + 1024], xbm[t_ * 128:(t_ + 1) * 128, :]))(slot, tt),
                waits=wx, incs=[(s6x, 16)])
            ws = [(s4st, 16 * (tt + 1)), (s5a, FFN_DONE)]
            if tt >= 2:
                ws.append((s6v, out_done[tt - 2]))
            p.step("sync", (lambda sl, t_: lambda e: e.dma_start(
                sload[:, sl * 1024:sl * 1024 + 1024], s_dram[t_ * 128:(t_ + 1) * 128, :]))(slot, tt),
                waits=ws, incs=[(s6s, 16)])
            p.step("vector", (lambda sl, gc: lambda e: e.tensor_tensor(
                o6[:, sl * 1024:sl * 1024 + 1024], sload[:, sl * 1024:sl * 1024 + 1024],
                gsil[:, gc:gc + 1024], ALU.mult))(slot, tt * 1024),
                waits=[(s6s, 16 * (tt + 1)), (s5a, FFN_DONE)] +
                      ([(s6st, 16 * (tt - 1))] if tt >= 2 else []),
                incs=[(s6v, 1)])
            p.step("vector", (lambda sl: lambda e: e.tensor_tensor(
                o6[:, sl * 1024:sl * 1024 + 1024], o6[:, sl * 1024:sl * 1024 + 1024],
                x6[:, sl * 1024:sl * 1024 + 1024], ALU.add))(slot),
                waits=[(s6x, 16 * (tt + 1)), (s6v, p.n(s6v))], incs=[(s6v, 1)])
            out_done[tt] = p.n(s6v)
            p.step("gpsimd", (lambda sl, t_: lambda e: e.dma_start(
                out[t_ * 128:(t_ + 1) * 128, :], o6[:, sl * 1024:sl * 1024 + 1024]))(slot, tt),
                waits=[(s6v, out_done[tt])], incs=[(s6st, 16)])
        p.step("gpsimd", lambda e: e.nop(), waits=[(s6st, 16 * 16)])

        p.check_deadlock()

        semh = {}
        for name in p.semnames:
            semh[name] = es.enter_context(nc.semaphore(name))
        block = es.enter_context(nc.Block())

        @block.tensor
        def _(t):
            p.emit("tensor", t, semh)

        @block.vector
        def _(v):
            p.emit("vector", v, semh)

        @block.scalar
        def _(s):
            p.emit("scalar", s, semh)

        @block.gpsimd
        def _(g):
            p.emit("gpsimd", g, semh)

        @block.sync
        def _(sy):
            p.emit("sync", sy, semh)

    return nc


_ROPE_BASE = 10000.0


def _host_tables(off):
    half = 32
    inv = 1.0 / _ROPE_BASE ** (np.arange(half, dtype=np.float64) / half)
    pos = np.arange(L, dtype=np.float64)
    pos = np.concatenate([pos[off:], pos[:off]])
    ang = pos[:, None] * inv[None, :]
    c = np.cos(ang).astype(np.float32)
    s = np.sin(ang).astype(np.float32)
    cost = np.concatenate([c, c], axis=1)
    sint = np.concatenate([-s, s], axis=1)
    return np.ascontiguousarray(cost), np.ascontiguousarray(sint)


def _host_consts():
    sel8 = np.zeros((8, 1024), np.float32)
    for e in range(8):
        sel8[e, e * 128:(e + 1) * 128] = 1
    ident = np.eye(128, dtype=np.float32)
    onesr = np.ones((1, 128), np.float32)
    return sel8, ident, onesr


_NC_CACHE = {}


def kernel(L0_attn_x, L0_attn_w_qkv, L0_attn_w_out, L0_ffn_gate_w,
           L0_ffn_wg, L0_ffn_wu, L0_ffn_wd, _debug_taps=(), _trace=False):
    x = np.asarray(L0_attn_x, dtype=np.float32)
    w_qkv = np.asarray(L0_attn_w_qkv, dtype=np.float32)
    w_out = np.asarray(L0_attn_w_out, dtype=np.float32)
    gate_w = np.asarray(L0_ffn_gate_w, dtype=np.float32)
    wg = np.asarray(L0_ffn_wg, dtype=np.float32)
    wu = np.asarray(L0_ffn_wu, dtype=np.float32)
    wd = np.asarray(L0_ffn_wd, dtype=np.float32)

    wkvT = np.ascontiguousarray(w_qkv[D:3 * D].T)
    wqT = np.ascontiguousarray(w_qkv[0:D].T)
    gateT = np.ascontiguousarray(gate_w.T)
    wg_f = np.ascontiguousarray(wg.reshape(NE * D, DFF))
    wu_f = np.ascontiguousarray(wu.reshape(NE * D, DFF))
    wd_f = np.ascontiguousarray(wd.reshape(NE * DFF, D))
    woutT_ = np.ascontiguousarray(w_out.T)
    sel8, ident, onesr = _host_consts()

    key = tuple(sorted(_debug_taps))
    if key not in _NC_CACHE:
        _NC_CACHE[key] = build_nc(_debug_taps)
    nc = _NC_CACHE[key]

    in_maps = []
    for c in range(8):
        b = c // 4
        off = 2048 * (c % 4)
        xb_ = np.ascontiguousarray(np.concatenate([x[b, off:], x[b, :off]], axis=0))
        cost, sint = _host_tables(off)
        in_maps.append({
            "xb": xb_, "wkvT": wkvT, "wqT": wqT, "gateT": gateT,
            "wg": wg_f, "wu": wu_f, "wd": wd_f, "woutT": woutT_,
            "costab": cost, "sintab": sint, "sel8_in": sel8,
            "ident_in": ident, "onesr": onesr,
        })
    res = run_bass_kernel_spmd(nc, in_maps, core_ids=list(range(8)), trace=_trace)
    outp = np.empty((B, L, D), np.float32)
    for c in range(8):
        b = c // 4
        off = 2048 * (c % 4)
        outp[b, off:off + 2048] = res.results[c]["out"]
    if _debug_taps:
        return outp, res
    return outp
